# version 07
# baseline (speedup 1.0000x reference)
import numpy as np
import concourse.bass as bass
import concourse.mybir as mybir
from concourse.bass_utils import run_bass_kernel_spmd

F32 = mybir.dt.float32
BF16 = mybir.dt.bfloat16
A = mybir.AluOpType
AF = mybir.ActivationFunctionType
AX = mybir.AxisListType.X

D = 1024
C2 = 2048
S64 = 64
I3 = 3072
TT = 512
NSUB = 4
C32 = 12582912.0
C52 = 6755399441055744.0
EPS = 1e-5


class Sem:
    def __init__(self, handle, name):
        self.h = handle
        self.name = name
        self.n = 0

    def inc(self, amt=1):
        self.n += amt
        return (self, amt)


class Planner:
    ENGINES = ("sync", "vector", "scalar", "tensor", "gpsimd")

    def __init__(self, nc):
        self.nc = nc
        self.guards = []
        self.sems = {}
        self.events = {e: [] for e in self.ENGINES}
        self.waits_seen = []

    def sem(self, name):
        if name not in self.sems:
            g = self.nc.semaphore(name)
            h = g.__enter__()
            self.guards.append(g)
            self.sems[name] = Sem(h, name)
        return self.sems[name]

    def ev(self, eng, fn, wait=(), inc=None):
        wait = [w for w in wait if w[1] > 0]
        for s, v in wait:
            self.waits_seen.append((s, v))
        self.events[eng].append((wait, fn, inc))

    def drain(self, sems):
        targets = [(s, s.n) for s in sems if s.n > 0]
        for e in self.ENGINES:
            self.ev(e, lambda eng: eng.nop(), wait=targets)

    def emit_block(self):
        evs = {e: self.events[e] for e in self.ENGINES if self.events[e]}
        if not evs:
            return
        with self.nc.Block() as blk:
            for ename, elist in evs.items():
                def body(eng, elist=elist):
                    for wait, fn, inc in elist:
                        for s, v in wait:
                            eng.wait_ge(s.h, v)
                        inst = fn(eng)
                        if inc is not None:
                            s, amt = inc
                            inst.then_inc(s.h, amt)
                getattr(blk, ename)(body)
        self.events = {e: [] for e in self.ENGINES}

    def check(self):
        for s, v in self.waits_seen:
            assert v <= s.n, f"wait on {s.name} for {v} but final count {s.n}"


def _enter(stack, guard):
    t = guard.__enter__()
    stack.append(guard)
    return t


def _exit_all(stack):
    for g in reversed(stack):
        g.__exit__(None, None, None)
    stack.clear()


def build(shard=True, dbg=False, cc=None, stop_after=None, nt_limit=None, main_stop=None):
    nc = bass.Bass("TRN2", target_bir_lowering=False,
                   detect_race_conditions=False)
    if cc is None:
        cc = shard
    if shard:
        NT = 9
        resets = {0}
        NTOK = 4096
        XROWS = 4608
    else:
        NT = 64
        resets = {0, 32}
        NTOK = 32768
        XROWS = 32768
    NFT = NTOK // TT

    xp = nc.dram_tensor("xp", [XROWS, D], F32, kind="ExternalInput").ap()
    winT = nc.dram_tensor("winT", [D, 4096], F32, kind="ExternalInput").ap()
    wBT = nc.dram_tensor("wBT", [C2, S64], F32, kind="ExternalInput").ap()
    wCT = nc.dram_tensor("wCT", [S64, C2], F32, kind="ExternalInput").ap()
    woutT = nc.dram_tensor("woutT", [C2, D], F32, kind="ExternalInput").ap()
    convd = nc.dram_tensor("convd", [64 * 128, 128], F32, kind="ExternalInput").ap()
    convb = nc.dram_tensor("convb", [128, 16], F32, kind="ExternalInput").ap()
    dparam = nc.dram_tensor("dparam", [128, 16], F32, kind="ExternalInput").ap()
    gbin = nc.dram_tensor("gbin", [128, D], F32, kind="ExternalInput").ap()
    bbin = nc.dram_tensor("bbin", [128, D], F32, kind="ExternalInput").ap()
    wupT = nc.dram_tensor("wupT", [D, I3], F32, kind="ExternalInput").ap()
    wgT = nc.dram_tensor("wgT", [D, I3], F32, kind="ExternalInput").ap()
    wdT = nc.dram_tensor("wdT", [I3, D], F32, kind="ExternalInput").ap()
    wup_n = nc.dram_tensor("wup_n", [I3, D], F32, kind="ExternalInput").ap()
    wg_n = nc.dram_tensor("wg_n", [I3, D], F32, kind="ExternalInput").ap()
    wd_n = nc.dram_tensor("wd_n", [D, I3], F32, kind="ExternalInput").ap()
    maskin = nc.dram_tensor("maskin", [1, 4], F32, kind="ExternalInput").ap()
    identin = nc.dram_tensor("identin", [128, 128], F32, kind="ExternalInput").ap()
    out = nc.dram_tensor("out", [NTOK, D], F32, kind="ExternalOutput").ap()

    kd = "ExternalOutput" if dbg else "Internal"
    x1scr = nc.dram_tensor("x1scr", [NTOK, D], F32, kind=kd).ap()
    bTscr = nc.dram_tensor("bTscr", [8, 128, NTOK], BF16, kind=kd).ap()
    woutscr = nc.dram_tensor("woutscr", [16, 128, D], BF16).ap()
    convdscr = nc.dram_tensor("convdscr", [64 * 128, 128], BF16).ap()
    wupdq = nc.dram_tensor("wupdq", [8, 128, I3], BF16, kind=kd).ap()
    wgdq = nc.dram_tensor("wgdq", [8, 128, I3], BF16).ap()
    wddq = nc.dram_tensor("wddq", [24, 128, D], BF16).ap()
    upscr = nc.dram_tensor("upscr", [24, 128, NTOK], BF16).ap()
    gscr = nc.dram_tensor("gscr", [24, 128, NTOK], BF16).ap()
    pscr = nc.dram_tensor("pscr", [24, 128, NTOK], BF16).ap()
    opscr = nc.dram_tensor("opscr", [24, 128, NTOK], BF16).ap()
    sdU = nc.dram_tensor("sdU", [32, I3], F32).ap()
    sdG = nc.dram_tensor("sdG", [32, I3], F32).ap()
    sdD = nc.dram_tensor("sdD", [96, D], F32).ap()
    rdU = nc.dram_tensor("rdU", [32, I3], F32).ap()
    rdG = nc.dram_tensor("rdG", [32, I3], F32).ap()
    rdD = nc.dram_tensor("rdD", [96, D], F32).ap()
    foldd = nc.dram_tensor("foldd", [4, 128], F32).ap()
    dbgscl = nc.dram_tensor("dbgscl", [32, 4], F32, kind=kd).ap()
    dbg_aT = nc.dram_tensor("dbg_aT", [128, 8 * TT], BF16, kind=kd).ap()
    dbg_u = nc.dram_tensor("dbg_u", [128, 16 * TT], BF16, kind=kd).ap()
    dbg_sxg = nc.dram_tensor("dbg_sxg", [128, 16 * TT], BF16, kind=kd).ap()
    dbg_hs = nc.dram_tensor("dbg_hs", [S64, TT], F32, kind=kd).ap()
    dbg_yo = nc.dram_tensor("dbg_yo", [128, 8 * TT], F32, kind=kd).ap()
    dbg_z = nc.dram_tensor("dbg_z", [128, D], F32, kind=kd).ap()
    dbg_x = nc.dram_tensor("dbg_x", [128, NSUB * D], F32, kind=kd).ap()
    dbg_a = nc.dram_tensor("dbg_a", [128, D], F32, kind=kd).ap()
    dbg_w = nc.dram_tensor("dbg_w", [128, 4096], BF16, kind=kd).ap()
    cc_in = nc.dram_tensor("cc_in", [1, 16], F32).ap()
    cc_out = nc.dram_tensor("cc_out", [1, 16], F32).ap()

    P = Planner(nc)
    sb = []

    ident = _enter(sb, nc.sbuf_tensor("ident", [128, 128], F32))
    ks = _enter(sb, nc.sbuf_tensor("ks", [128, 4], F32))
    ckv = _enter(sb, nc.sbuf_tensor("ckv", [128, 4], F32))
    r7 = _enter(sb, nc.sbuf_tensor("r7", [128, 4], F32))
    r327 = _enter(sb, nc.sbuf_tensor("r327", [128, 4], F32))
    mask_sb = _enter(sb, nc.sbuf_tensor("mask_sb", [128, 4], F32))
    dpar = _enter(sb, nc.sbuf_tensor("dpar", [128, 16], F32))
    cvb = _enter(sb, nc.sbuf_tensor("cvb", [128, 16], F32))
    gbs = _enter(sb, nc.sbuf_tensor("gbs", [128, D], F32))
    bbs = _enter(sb, nc.sbuf_tensor("bbs", [128, D], F32))
    dec95 = _enter(sb, nc.sbuf_tensor("dec95", [S64, TT], F32))
    zk64 = _enter(sb, nc.sbuf_tensor("zk64", [S64, 4], F32))
    scl = {}
    for nm in ("m_b", "s0", "r0", "rK0", "m_up", "s_up", "r_up", "cU",
               "m_g", "s_g", "r_g", "cG", "sgmix", "c65", "m_p", "s_g2",
               "r_g2", "c2m", "m_op", "s_o", "r_o", "c3m", "cout",
               "tmpa", "tmpb", "tmpc"):
        scl[nm] = _enter(sb, nc.sbuf_tensor("scl_" + nm, [128, 4], F32))
    bmax = _enter(sb, nc.sbuf_tensor("bmax", [128, 4], F32))
    dumb = _enter(sb, nc.sbuf_tensor("dumb", [128, 4], F32))
    fmax = {nm: _enter(sb, nc.sbuf_tensor("fmax_" + nm, [128, 4], F32))
            for nm in ("up", "g", "p", "op")}
    rowb = _enter(sb, nc.sbuf_tensor("rowb", [1, 128], F32))
    locb = _enter(sb, nc.sbuf_tensor("locb", [1, 16], F32))
    loc2b = _enter(sb, nc.sbuf_tensor("loc2b", [1, 16], F32))
    st = {nm: _enter(sb, nc.sbuf_tensor("st_" + nm, [128, 4], F32))
          for nm in ("sx", "sq", "mu1", "r1", "mu2", "r2", "mu3", "r3", "va", "vb")}
    pz0 = _enter(sb, nc.psum_tensor("pz0", [128, 1024], F32))
    pz1 = _enter(sb, nc.psum_tensor("pz1", [128, 1024], F32))
    pm0 = _enter(sb, nc.psum_tensor("pm0", [128, 512], F32))
    pm1 = _enter(sb, nc.psum_tensor("pm1", [128, 512], F32))
    pc0 = _enter(sb, nc.psum_tensor("pc0", [128, 512], F32))
    pbu = _enter(sb, nc.psum_tensor("pbu", [128, 512], F32))

    svs = P.sem("svs")

    def V(fn, wait=(), inc=None):
        P.ev("vector", fn, wait, inc)

    def VS(fn, wait=(), inc=None):
        P.ev("vector", fn, list(wait) + [(svs, svs.n)], svs.inc())
        if inc is not None:
            P.ev("vector",
                 lambda e: e.tensor_copy(dumb[:, 0:4], ks[:, 0:4]),
                 [(svs, svs.n)], inc)

    def SC(fn, wait=(), inc=None):
        P.ev("scalar", fn, wait, inc)

    def G(fn, wait=(), inc=None):
        P.ev("gpsimd", fn, wait, inc)

    def T(fn, wait=(), inc=None):
        P.ev("tensor", fn, wait, inc)

    def SY(fn, wait=(), inc=None):
        P.ev("sync", fn, wait, inc)

    sload = P.sem("sload")
    sprep = P.sem("sprep")
    scast = P.sem("scast")
    sst = P.sem("sst")

    for dst, src in ((ident[:, :], identin),
                     (mask_sb[:, :], bass.AP(maskin.tensor, 0, [[0, 128], [1, 4]])),
                     (dpar[:, :], dparam),
                     (cvb[:, :], convb),
                     (gbs[:, :], gbin),
                     (bbs[:, :], bbin)):
        SY(lambda e, d=dst, s=src: e.dma_start(d, s), inc=sload.inc(16))

    VS(lambda e: e.memset(ks[:, :], 1.0))
    VS(lambda e: e.scalar_tensor_tensor(scl["tmpb"][:, :], ks[:, :],
                                        1.0 + 2.0 ** -40, ks[:, :],
                                        op0=A.mult, op1=A.subtract))
    VS(lambda e: e.tensor_scalar(ks[:, :], scl["tmpb"][:, :], 2.0 ** 40, None,
                                 op0=A.mult))
    VS(lambda e: e.tensor_scalar(ckv[:, :], ks[:, :], C52, None, op0=A.mult))
    VS(lambda e: e.memset(scl["tmpa"][:, :], 7.0))
    VS(lambda e: e.reciprocal(r7[:, :], scl["tmpa"][:, :]))
    VS(lambda e: e.memset(scl["tmpa"][:, :], 32767.0))
    VS(lambda e: e.reciprocal(r327[:, :], scl["tmpa"][:, :]))
    V(lambda e: e.memset(dec95[:, :], 0.95))
    for _nm in ("sx", "sq", "mu1", "r1", "mu2", "r2", "mu3", "r3", "va", "vb"):
        VS(lambda e, b=st[_nm]: e.memset(b[:, :], 1.0))
    VS(lambda e: e.memset(zk64[:, :], 0.0), inc=sprep.inc())

    st0 = []
    wstg = _enter(st0, nc.sbuf_tensor("wstg", [128, 4096], F32))
    wbf = _enter(st0, nc.sbuf_tensor("wbf", [128, 4096], BF16))
    for ck in range(16):
        SY(lambda e, c=ck: e.dma_start(wstg[:, 0:D], woutT[c * 128:(c + 1) * 128, :]),
           wait=[(sst, sst.n)], inc=scast.inc(16))
        V(lambda e: e.tensor_copy(wbf[:, 0:D], wstg[:, 0:D]),
          wait=[(scast, scast.n)], inc=sprep.inc())
        SY(lambda e, c=ck: e.dma_start(
            bass.AP(woutscr.tensor, c * 128 * D, [[D, 128], [1, D]]), wbf[:, 0:D]),
           wait=[(sprep, sprep.n)], inc=sst.inc(16))
    for blk in range(4):
        SY(lambda e, c=blk: e.dma_start(
            wstg[:, 0:2048],
            bass.AP(convd.tensor, c * 16 * 128 * 128,
                    [[128, 128], [128 * 128, 16], [1, 128]])),
           wait=[(sst, sst.n)], inc=scast.inc(16))
        V(lambda e: e.tensor_copy(wbf[:, 0:2048], wstg[:, 0:2048]),
          wait=[(scast, scast.n)], inc=sprep.inc())
        SY(lambda e, c=blk: e.dma_start(
            bass.AP(convdscr.tensor, c * 16 * 128 * 128,
                    [[128, 128], [128 * 128, 16], [1, 128]]),
            wbf[:, 0:2048]),
           wait=[(sprep, sprep.n)], inc=sst.inc(16))
    P.drain([sload, scast, sst, sprep, svs])
    P.emit_block()
    _exit_all(st0)

    std = []
    wnat = _enter(std, nc.sbuf_tensor("wnat", [128, I3], F32))
    snat = _enter(std, nc.sbuf_tensor("snat", [128, 768], F32))
    rnat = _enter(std, nc.sbuf_tensor("rnat", [128, 768], F32))
    sTs = _enter(std, nc.sbuf_tensor("sTs", [96, I3], F32))
    rTs = _enter(std, nc.sbuf_tensor("rTs", [96, I3], F32))
    sbc = _enter(std, nc.sbuf_tensor("sbc", [128, I3], F32))
    rbc = _enter(std, nc.sbuf_tensor("rbc", [128, I3], F32))
    wTb = _enter(std, nc.sbuf_tensor("wTb", [128, I3], F32))
    wqb = _enter(std, nc.sbuf_tensor("wqb", [128, I3], BF16))
    sdqd = P.sem("sdqd")
    sdqc = P.sem("sdqc")

    def dequant_weight(nat_ap, T_ap, nrows, iblk, sdram, rdram, dqscr, nTchunks, tfree):
        nch = nrows // 128
        nfree = iblk * 32
        for ck in range(nch):
            SY(lambda e, c=ck: e.dma_start(wnat[:, 0:nfree],
                                           nat_ap[c * 128:(c + 1) * 128, :]),
               wait=[(sdqd, sdqd.n), (sdqc, sdqc.n)], inc=sdqd.inc(16))
            V(lambda e, c=ck: e.tensor_reduce(
                bass.AP(snat, c * iblk, [[768, 128], [1, iblk], [1, 1]]),
                wnat[:, 0:nfree].rearrange("p (b k) -> p b k", k=32),
                axis=AX, op=A.max, apply_absolute_value=True),
              wait=[(sdqd, sdqd.n), (sdqc, sdqc.n)], inc=sdqc.inc())
        V(lambda e: e.tensor_scalar(snat[:, 0:nch * iblk], snat[:, 0:nch * iblk],
                                    r7[:, 0:1], 1e-12, op0=A.mult, op1=A.max),
          wait=[(sdqd, sdqd.n), (sdqc, sdqc.n)])
        V(lambda e: e.reciprocal(rnat[:, 0:nch * iblk], snat[:, 0:nch * iblk]),
          inc=sdqc.inc())
        for ck in range(nch):
            pzt = pz0 if ck % 2 == 0 else pz1
            T(lambda e, c=ck, p=pzt: e.transpose(
                p[0:iblk, 0:128],
                bass.AP(snat, c * iblk, [[768, 128], [1, iblk]]), ident[:, :]),
              wait=[(sdqd, sdqd.n), (sdqc, sdqc.n)])
            T(lambda e, c=ck, p=pzt: e.transpose(
                p[0:iblk, 128:256],
                bass.AP(rnat, c * iblk, [[768, 128], [1, iblk]]), ident[:, :]),
              inc=sdqc.inc())
            V(lambda e, c=ck, p=pzt: e.tensor_copy(
                sTs[0:iblk, c * 128:(c + 1) * 128], p[0:iblk, 0:128]),
              wait=[(sdqd, sdqd.n), (sdqc, sdqc.n)])
            V(lambda e, c=ck, p=pzt: e.tensor_copy(
                rTs[0:iblk, c * 128:(c + 1) * 128], p[0:iblk, 128:256]),
              inc=sdqc.inc())
        SY(lambda e: e.dma_start(sdram[0:iblk, 0:nrows], sTs[0:iblk, 0:nrows]),
           wait=[(sdqd, sdqd.n), (sdqc, sdqc.n)], inc=sdqd.inc(16))
        SY(lambda e: e.dma_start(rdram[0:iblk, 0:nrows], rTs[0:iblk, 0:nrows]),
           inc=sdqd.inc(16))
        for hk in range(nTchunks):
            SY(lambda e, c=hk: e.dma_start(wTb[:, 0:tfree],
                                           T_ap[c * 128:(c + 1) * 128, :]),
               wait=[(sdqd, sdqd.n), (sdqc, sdqc.n)], inc=sdqd.inc(16))
            SY(lambda e, c=hk: e.dma_start(
                sbc[:, 0:tfree],
                bass.AP(sdram.tensor, c * 4 * tfree, [[tfree, 4], [0, 32], [1, tfree]])),
               inc=sdqd.inc(16))
            SY(lambda e, c=hk: e.dma_start(
                rbc[:, 0:tfree],
                bass.AP(rdram.tensor, c * 4 * tfree, [[tfree, 4], [0, 32], [1, tfree]])),
               inc=sdqd.inc(16))
            V(lambda e: e.tensor_mul(wTb[:, 0:tfree], wTb[:, 0:tfree], rbc[:, 0:tfree]),
              wait=[(sdqd, sdqd.n), (sdqc, sdqc.n)])
            V(lambda e: e.tensor_scalar(wTb[:, 0:tfree], wTb[:, 0:tfree],
                                        C32, C32, op0=A.add, op1=A.subtract))
            V(lambda e: e.tensor_scalar(wTb[:, 0:tfree], wTb[:, 0:tfree],
                                        ckv[:, 0:1], ckv[:, 0:1],
                                        op0=A.add, op1=A.subtract))
            V(lambda e: e.tensor_mul(wqb[:, 0:tfree], wTb[:, 0:tfree], sbc[:, 0:tfree]),
              inc=sdqc.inc())
            SY(lambda e, c=hk, d=dqscr: e.dma_start(d[c, :, :], wqb[:, 0:tfree]),
               wait=[(sdqd, sdqd.n), (sdqc, sdqc.n)], inc=sdqd.inc(16))

    dequant_weight(wup_n, wupT, I3, 32, sdU, rdU, wupdq, 8, I3)
    dequant_weight(wg_n, wgT, I3, 32, sdG, rdG, wgdq, 8, I3)
    dequant_weight(wd_n, wdT, D, 96, sdD, rdD, wddq, 24, D)
    P.drain([sdqd, sdqc, svs])
    P.emit_block()
    _exit_all(std)
    if stop_after == "dq":
        _exit_all(sb)
        for g in reversed(P.guards):
            g.__exit__(None, None, None)
        P.check()
        return nc

    sa = []
    winb = _enter(sa, nc.sbuf_tensor("winb", [128, 8 * 4096], BF16))
    wBb = _enter(sa, nc.sbuf_tensor("wBb", [128, 16 * 64], BF16))
    wCb = _enter(sa, nc.sbuf_tensor("wCb", [S64, C2], BF16))
    wob = _enter(sa, nc.sbuf_tensor("wob", [128, 2 * 1024], BF16))
    cvd = _enter(sa, nc.sbuf_tensor("cvd", [128, 2 * 4 * 128], BF16))
    x_t = _enter(sa, nc.sbuf_tensor("x_t", [128, NSUB * D], F32))
    a_sub = _enter(sa, nc.sbuf_tensor("a_sub", [128, D], F32))
    aT = _enter(sa, nc.sbuf_tensor("aT", [128, 8 * TT], BF16))
    xin = _enter(sa, nc.sbuf_tensor("xin", [128, 2 * 520], BF16))
    xtail = _enter(sa, nc.sbuf_tensor("xtail", [128, 16 * 4], BF16))
    u_bf = _enter(sa, nc.sbuf_tensor("u_bf", [128, 16 * TT], BF16))
    sxg = _enter(sa, nc.sbuf_tensor("sxg", [128, 16 * TT], BF16))
    y2t = _enter(sa, nc.sbuf_tensor("y2t", [128, 2 * TT], F32))
    hs0 = _enter(sa, nc.sbuf_tensor("hs0", [S64, TT], F32))
    hs1 = _enter(sa, nc.sbuf_tensor("hs1", [S64, TT], F32))
    hsb = _enter(sa, nc.sbuf_tensor("hsb", [S64, TT], BF16))
    bu_m = _enter(sa, nc.sbuf_tensor("bu_m", [S64, TT], F32))
    yo = _enter(sa, nc.sbuf_tensor("yo", [128, 8 * TT], F32))
    zb = _enter(sa, nc.sbuf_tensor("zb", [128, D], F32))
    nb = _enter(sa, nc.sbuf_tensor("nb", [128, D], F32))
    x1b = _enter(sa, nc.sbuf_tensor("x1b", [128, D], F32))
    wbuf = zb
    bsub = nb
    sdump = _enter(sa, nc.sbuf_tensor("sdump", [128, D], F32))
    bT_b = _enter(sa, nc.sbuf_tensor("bT_b", [128, 8 * TT], BF16))

    swl = P.sem("swl")
    sx = P.sem("sx")
    sfr = P.sem("sfr")
    sfs = P.sem("sfs")
    saT = P.sem("saT")
    saTe = P.sem("saTe")
    sxp = P.sem("sxp")
    sxe = P.sem("sxe")
    scv = P.sem("scv")
    sce = P.sem("sce")
    sbu = P.sem("sbu")
    ssc = P.sem("ssc")
    shb = P.sem("shb")
    swc = P.sem("swc")
    sy2 = P.sem("sy2")
    swo = P.sem("swo")
    swoe = P.sem("swoe")
    szt = P.sem("szt")
    sepi = P.sem("sepi")
    ssq = P.sem("ssq")
    sbt = P.sem("sbt")
    sbte = P.sem("sbte")
    sst1 = P.sem("sst1")
    sstb = P.sem("sstb")
    swos = P.sem("swos")
    scvs = P.sem("scvs")

    stw = []
    wstg3 = yo
    for dk in range(8):
        SY(lambda e, d=dk: e.dma_start(wstg3[:, :], winT[d * 128:(d + 1) * 128, :]),
           wait=[(sprep, sprep.n)], inc=swl.inc(16))
        V(lambda e, d=dk: e.tensor_copy(winb[:, d * 4096:(d + 1) * 4096], wstg3[:, :]),
          wait=[(swl, swl.n)], inc=sprep.inc())
    for ck in range(16):
        SY(lambda e, c=ck: e.dma_start(wstg3[:, 0:64], wBT[c * 128:(c + 1) * 128, :]),
           wait=[(sprep, sprep.n)], inc=swl.inc(16))
        V(lambda e, c=ck: e.tensor_scalar(wBb[:, c * 64:(c + 1) * 64], wstg3[:, 0:64],
                                          0.05, None, op0=A.mult),
          wait=[(swl, swl.n)], inc=sprep.inc())
    SY(lambda e: e.dma_start(wstg3[0:64, 0:C2], wCT), wait=[(sprep, sprep.n)],
       inc=swl.inc(16))
    V(lambda e: e.tensor_copy(wCb[:, :], wstg3[0:64, 0:C2]),
      wait=[(swl, swl.n)], inc=sprep.inc())
    V(lambda e: e.memset(xtail[:, :], 0.0), inc=sprep.inc())
    P.drain([swl, sprep])
    P.emit_block()

    for t in range(NT if nt_limit is None else min(NT, nt_limit)):
        warm = shard and (t == 0)
        ft = t - 1 if shard else t
        for s in range(NSUB):
            SY(lambda e, tt=t, ss=s: e.dma_start(
                x_t[:, ss * D:(ss + 1) * D],
                xp[tt * TT + ss * 128: tt * TT + (ss + 1) * 128, :]),
               wait=[(sepi, sepi.n), (sfr, sfr.n)] if t > 0 else [],
               inc=sx.inc(16))
        if dbg and t == (1 if shard else 0):
            G(lambda e: e.dma_start(dbg_x[:, :], x_t[:, :]),
              wait=[(sx, sx.n)], inc=P.sem("sdbg2").inc(16))
            G(lambda e: e.dma_start(dbg_w[:, :], winb[:, 0:4096]),
              inc=P.sem("sdbg2").inc(16))
        for s in range(NSUB):
            V(lambda e, ss=s: e.tensor_scalar(
                a_sub[:, :], x_t[:, ss * D:(ss + 1) * D], 1.0, 0.0,
                op0=A.mult, op1=A.add, accum_out=st["sx"][:, ss:ss + 1]),
              wait=[(sx, sx.n)], inc=sfs.inc())
            SC(lambda e, ss=s: e.activation(
                sdump[:, :], x_t[:, ss * D:(ss + 1) * D], AF.Square,
                accum_out=st["sq"][:, ss:ss + 1]),
               wait=[(sx, sx.n)], inc=ssq.inc())
        VS(lambda e: e.tensor_scalar(st["mu1"][:, :], st["sx"][:, :], 1.0 / D, None,
                                     op0=A.mult),
           wait=[(ssq, ssq.n), (sfs, sfs.n)])
        VS(lambda e: e.tensor_scalar(st["va"][:, :], st["sq"][:, :], 1.0 / D, EPS,
                                     op0=A.mult, op1=A.add))
        VS(lambda e: e.tensor_mul(st["vb"][:, :], st["mu1"][:, :], st["mu1"][:, :]))
        VS(lambda e: e.tensor_sub(st["va"][:, :], st["va"][:, :], st["vb"][:, :]),
           inc=sfs.inc())
        SC(lambda e: e.activation(st["vb"][:, :], st["va"][:, :], AF.Sqrt,
                                  bias=0.0, scale=1.0),
           wait=[(sfs, sfs.n)], inc=ssq.inc())
        VS(lambda e: e.reciprocal(st["r1"][:, :], st["vb"][:, :]),
           wait=[(ssq, ssq.n)], inc=sfs.inc())
        for s in range(NSUB):
            V(lambda e, ss=s: e.tensor_scalar(
                a_sub[:, :], x_t[:, ss * D:(ss + 1) * D],
                st["mu1"][:, ss:ss + 1], st["r1"][:, ss:ss + 1],
                op0=A.subtract, op1=A.mult),
              wait=[(ssq, ssq.n), (saTe, saTe.n), (svs, svs.n)],
              inc=sfr.inc())
            if dbg and t == (1 if shard else 0) and s == 0:
                G(lambda e: e.dma_start(dbg_a[:, :], a_sub[:, :]),
                  wait=[(sfr, sfr.n)], inc=P.sem("sdbg2").inc(16))
            pzt = pz0 if s % 2 == 0 else pz1
            for dk in range(8):
                T(lambda e, d=dk, p=pzt: e.transpose(
                    p[:, d * 128:(d + 1) * 128], a_sub[:, d * 128:(d + 1) * 128],
                    ident[:, :]),
                  wait=([(sfr, sfr.n), (saTe, max(0, saTe.n - 1)),
                         (sbte, sbte.n), (sepi, sepi.n)] if dk == 0 else []),
                  inc=(saT.inc() if dk == 7 else None))
            SC(lambda e, ss=s, p=pzt: e.activation(
                bass.AP(aT, ss * 128, [[8 * TT, 128], [TT, 8], [1, 128]]),
                p[:, :], AF.Identity),
               wait=[(saT, saT.n)], inc=saTe.inc())
        if (not warm and main_stop not in ("scan", "y2", "woutnost", "woutseq", "woutub", "woutmm", "woutev", "woutev2")) or main_stop == "stream":
            wout_base = swo.n
            for ck in range(16):
                G(lambda e, c=ck, buf=(ck % 2): e.dma_start(
                    wob[:, buf * 1024:(buf + 1) * 1024],
                    bass.AP(woutscr.tensor, c * 128 * D, [[D, 128], [1, D]])),
                  wait=([] if main_stop == "stream" else
                        [(swo, max(0, wout_base + ck - 1))]),
                  inc=swos.inc(16))
        conv_base = scv.n
        scvs_base = scvs.n
        for fk in range(16):
            SY(lambda e, c=fk, buf=(fk % 2): e.dma_start(
                cvd[:, buf * 512:(buf + 1) * 512],
                bass.AP(convdscr.tensor, c * 128 * 128,
                        [[128, 128], [16 * 128 * 128, 4], [1, 128]])),
               wait=[(scv, max(0, conv_base + fk - 1))], inc=scvs.inc(16))
        nchunks = 16 if warm else 32
        for fk in range(nchunks):
            pmt = pm0 if fk % 2 == 0 else pm1
            for dk in range(8):
                T(lambda e, d=dk, f=fk, p=pmt: e.matmul(
                    p[:, :], winb[:, d * 4096 + f * 128: d * 4096 + (f + 1) * 128],
                    aT[:, d * TT:(d + 1) * TT],
                    start=(d == 0), stop=(d == 7)),
                  wait=([(saTe, saTe.n), (sxe, max(0, sxe.n - 1))] if dk == 0 else []),
                  inc=(sxp.inc() if dk == 7 else None))
            if fk < 16:
                xb = fk % 2
                SC(lambda e, b=xb, p=pmt: e.activation(
                    xin[:, b * 520 + 4: b * 520 + 4 + TT], p[:, :], AF.Identity),
                   wait=[(sxp, sxp.n), (scv, max(0, scv.n - 1)), (sfr, sfr.n)],
                   inc=sxe.inc())
                if t in resets:
                    V(lambda e, b=xb: e.memset(xin[:, b * 520 + 1: b * 520 + 4], 0.0),
                      wait=[(sxe, sxe.n)], inc=sfr.inc())
                else:
                    V(lambda e, f=fk, b=xb: e.tensor_copy(
                        xin[:, b * 520 + 1: b * 520 + 4], xtail[:, f * 4: f * 4 + 3]),
                      wait=[(sxe, sxe.n)], inc=sfr.inc())
                V(lambda e, f=fk, b=xb: e.tensor_copy(
                    xtail[:, f * 4: f * 4 + 3],
                    xin[:, b * 520 + 4 + TT - 3: b * 520 + 4 + TT]),
                  inc=sfr.inc())
                for tap in range(4):
                    T(lambda e, b=xb, tp=tap: e.matmul(
                        pc0[:, :],
                        cvd[:, b * 512 + tp * 128: b * 512 + (tp + 1) * 128],
                        xin[:, b * 520 + 1 + tp: b * 520 + 1 + tp + TT],
                        start=(tp == 0), stop=(tp == 3)),
                      wait=([(sfr, sfr.n), (scvs, scvs_base + 16 * (fk + 1)),
                             (sce, sce.n), (sy2, sy2.n)] if tap == 0 else []),
                      inc=(scv.inc() if tap == 3 else None))
                SC(lambda e, f=fk: e.activation(
                    u_bf[:, f * TT:(f + 1) * TT], pc0[:, :], AF.Silu,
                    bias=cvb[:, f:f + 1], scale=1.0),
                   wait=[(scv, scv.n)], inc=sce.inc())
                T(lambda e, f=fk: e.matmul(
                    pbu[0:S64, :], wBb[:, f * 64:(f + 1) * 64],
                    u_bf[:, f * TT:(f + 1) * TT],
                    start=(f == 0), stop=(f == 15)),
                  wait=[(sce, sce.n)] + ([(ssc, ssc.n)] if fk == 0 else []),
                  inc=(sbu.inc() if fk == 15 else None))
            else:
                SC(lambda e, f=fk, p=pmt: e.activation(
                    sxg[:, (f - 16) * TT:(f - 15) * TT], p[:, :], AF.Silu),
                   wait=[(sxp, sxp.n)], inc=sxe.inc())
        hs_cur = hs0 if t % 2 == 0 else hs1
        hs_prev = hs1 if t % 2 == 0 else hs0
        if warm:
            V(lambda e: e.tensor_scalar(bu_m[:, :], pbu[0:S64, :], mask_sb[0:S64, 0:1],
                                        None, op0=A.mult),
              wait=[(sbu, sbu.n)])
            V(lambda e, h=hs_cur: e.tensor_tensor_scan(
                h[:, :], dec95[:, :], bu_m[:, :], zk64[0:S64, 0:1],
                op0=A.mult, op1=A.add),
              inc=ssc.inc())
            continue
        init = zk64[0:S64, 0:1] if (t in resets) else hs_prev[:, TT - 1:TT]
        V(lambda e, h=hs_cur, i=init: e.tensor_tensor_scan(
            h[:, :], dec95[:, :], pbu[0:S64, :], i, op0=A.mult, op1=A.add),
          wait=[(sbu, sbu.n)], inc=ssc.inc())
        V(lambda e, h=hs_cur: e.tensor_copy(hsb[:, :], h[:, :]), inc=shb.inc())
        if main_stop in ("scan", "stream"):
            continue
        for ck in range(16):
            pmt = pm0 if ck % 2 == 0 else pm1
            T(lambda e, c=ck, p=pmt: e.matmul(
                p[:, :], wCb[:, c * 128:(c + 1) * 128], hsb[:, :],
                start=True, stop=True),
              wait=([(shb, shb.n), (sxe, sxe.n)] if ck == 0 else
                    [(sy2, max(0, sy2.n - 1))]),
              inc=swc.inc())
            V(lambda e, c=ck, p=pmt, yb=(ck % 2): e.scalar_tensor_tensor(
                y2t[:, yb * TT:(yb + 1) * TT], u_bf[:, c * TT:(c + 1) * TT],
                dpar[:, c:c + 1], p[:, :], op0=A.mult, op1=A.add),
              wait=[(swc, swc.n)])
            V(lambda e, c=ck, yb=(ck % 2): e.tensor_mul(
                sxg[:, c * TT:(c + 1) * TT], y2t[:, yb * TT:(yb + 1) * TT],
                sxg[:, c * TT:(c + 1) * TT]),
              inc=sy2.inc())
        if main_stop == "y2":
            continue
        wout_ps = [pm0[:, :], pm1[:, :], pc0[:, :], pbu[:, :],
                   pz0[:, 0:512], pz0[:, 512:1024], pz1[:, 0:512], pz1[:, 512:1024]]
        NDK = 4 if main_stop in ("wout4", "woutnost") else 8
        if main_stop in ("woutub", "woutmm"):
            NDK = 2
        if main_stop in ("woutev", "woutev2"):
            NDK = 0
        _seq = (main_stop in ("woutseq", "woutub", "woutmm"))
        _iter = ([(c, d) for d in range(2) for c in range(16)] if _seq else
                 [(c, d) for c in range(16) for d in range(NDK)])
        for ck, dk in _iter:
            if True:
                _lhs = (winb if main_stop in ("woutnost", "woutseq", "woutub", "woutmm", "woutev") else wob)
                _lo = (0 if main_stop in ("woutnost", "woutseq", "woutub") else (ck % 2) * 1024)
                _rhs = (u_bf if main_stop in ("woutub", "woutmm") else sxg)
                T(lambda e, d=dk, c=ck, p=wout_ps[dk], L=_lhs, o=_lo, R=_rhs: e.matmul(
                    p, L[:, o + d * 128: o + (d + 1) * 128],
                    R[:, c * TT:(c + 1) * TT],
                    start=(c == 0), stop=(c == 15)),
                  wait=([(sy2, sy2.n)] +
                        ([] if main_stop in ("woutnost", "woutseq", "woutub", "woutmm", "woutev") else
                         [(swos, swos.n - 16 * (15 - ck))]) +
                        [(swoe, max(0, swoe.n - 8))] if dk == 0 else []),
                  inc=(swo.inc() if (dk == NDK - 1 and not _seq) or
                       (_seq and ck == 15) else None))
        NEV = 0 if main_stop == "woutmm" else (2 if (_seq or main_stop in ("woutev", "woutev2")) else NDK)
        _ydst = (sdump if main_stop == "woutev2" else yo)
        for dk in range(NEV):
            SC(lambda e, d=dk, p=wout_ps[dk], Y=_ydst: e.activation(
                (Y[:, 0:TT] if main_stop == "woutev2" else Y[:, d * TT:(d + 1) * TT]),
                p, AF.Identity),
               wait=[(swo, swo.n), (sy2, sy2.n)], inc=swoe.inc())
        if main_stop in ("wout", "wout4", "woutnost", "woutseq", "woutub", "woutmm", "woutev", "woutev2"):
            continue
        if dbg and t == (1 if shard else 0):
            G(lambda e: e.dma_start(dbg_u[:, :], u_bf[:, :]),
              wait=[(sce, sce.n)], inc=P.sem("sdbg2").inc(16))
            G(lambda e: e.dma_start(dbg_sxg[:, :], sxg[:, :]),
              wait=[(sy2, sy2.n)], inc=P.sem("sdbg2").inc(16))
            G(lambda e, h=hs_cur: e.dma_start(dbg_hs[:, :], h[:, :]),
              wait=[(ssc, ssc.n)], inc=P.sem("sdbg2").inc(16))
            G(lambda e: e.dma_start(dbg_yo[:, :], yo[:, :]),
              wait=[(swoe, swoe.n)], inc=P.sem("sdbg2").inc(16))
            G(lambda e: e.dma_start(dbg_aT[:, :], aT[:, :]),
              wait=[(saTe, saTe.n)], inc=P.sem("sdbg2").inc(16))
        for s in range(NSUB):
            pzt = pz0 if s % 2 == 0 else pz1
            for dk in range(8):
                T(lambda e, d=dk, ss=s, p=pzt: e.transpose(
                    p[:, d * 128:(d + 1) * 128],
                    yo[:, d * TT + ss * 128: d * TT + (ss + 1) * 128], ident[:, :]),
                  wait=([(swoe, swoe.n), (sepi, max(0, sepi.n - 6)),
                         (sbte, max(0, sbte.n - 1))] if dk == 0 else []),
                  inc=(szt.inc() if dk == 7 else None))
            V(lambda e, ss=s: e.tensor_scalar(
                a_sub[:, :], x_t[:, ss * D:(ss + 1) * D],
                st["mu1"][:, ss:ss + 1], st["r1"][:, ss:ss + 1],
                op0=A.subtract, op1=A.mult),
              wait=[(szt, szt.n)])
            V(lambda e, p=pzt, ss=s: e.scalar_tensor_tensor(
                zb[:, :], p[:, :], 1.0, a_sub[:, :], op0=A.mult, op1=A.add,
                accum_out=st["sx"][:, ss:ss + 1]),
              wait=[(ssq, ssq.n)], inc=sepi.inc())
            if dbg and t == (1 if shard else 0) and s == 0:
                G(lambda e: e.dma_start(dbg_z[:, :], zb[:, :]),
                  wait=[(sepi, sepi.n)], inc=P.sem("sdbg2").inc(16))
            SC(lambda e, ss=s: e.activation(sdump[:, :], zb[:, :], AF.Square,
                                            accum_out=st["sq"][:, ss:ss + 1]),
               wait=[(sepi, sepi.n)], inc=ssq.inc())
            VS(lambda e, ss=s: e.tensor_scalar(st["mu2"][:, ss:ss + 1],
                                               st["sx"][:, ss:ss + 1],
                                               1.0 / D, None, op0=A.mult),
               wait=[(ssq, ssq.n)])
            VS(lambda e, ss=s: e.tensor_scalar(st["va"][:, ss:ss + 1],
                                               st["sq"][:, ss:ss + 1],
                                               1.0 / D, EPS, op0=A.mult, op1=A.add))
            VS(lambda e, ss=s: e.tensor_mul(st["vb"][:, ss:ss + 1],
                                            st["mu2"][:, ss:ss + 1],
                                            st["mu2"][:, ss:ss + 1]))
            VS(lambda e, ss=s: e.tensor_sub(st["va"][:, ss:ss + 1],
                                            st["va"][:, ss:ss + 1],
                                            st["vb"][:, ss:ss + 1]), inc=sepi.inc())
            SC(lambda e: e.activation(st["vb"][:, :], st["va"][:, :],
                                      AF.Sqrt, bias=0.0, scale=1.0),
               wait=[(sepi, sepi.n)], inc=ssq.inc())
            VS(lambda e: e.reciprocal(st["r2"][:, :], st["vb"][:, :]),
               wait=[(ssq, ssq.n)])
            if main_stop == "epi1":
                continue
            V(lambda e, ss=s: e.tensor_scalar(nb[:, :], zb[:, :],
                                              st["mu2"][:, ss:ss + 1],
                                              st["r2"][:, ss:ss + 1],
                                              op0=A.subtract, op1=A.mult),
              wait=[(ssq, ssq.n), (svs, svs.n)])
            V(lambda e: e.tensor_mul(nb[:, :], nb[:, :], gbs[:, :]))
            V(lambda e, ss=s: e.scalar_tensor_tensor(
                wbuf[:, :], nb[:, :], 1.0, bbs[:, :], op0=A.mult, op1=A.add,
                accum_out=st["sx"][:, ss:ss + 1]),
              inc=sepi.inc())
            SC(lambda e, ss=s: e.activation(sdump[:, :], wbuf[:, :], AF.Square,
                                            accum_out=st["sq"][:, ss:ss + 1]),
               wait=[(sepi, sepi.n)], inc=ssq.inc())
            VS(lambda e, ss=s: e.tensor_scalar(st["mu3"][:, ss:ss + 1],
                                               st["sx"][:, ss:ss + 1],
                                               1.0 / D, None, op0=A.mult),
               wait=[(ssq, ssq.n)])
            VS(lambda e, ss=s: e.tensor_scalar(st["va"][:, ss:ss + 1],
                                               st["sq"][:, ss:ss + 1],
                                               1.0 / D, EPS, op0=A.mult, op1=A.add))
            VS(lambda e, ss=s: e.tensor_mul(st["vb"][:, ss:ss + 1],
                                            st["mu3"][:, ss:ss + 1],
                                            st["mu3"][:, ss:ss + 1]))
            VS(lambda e, ss=s: e.tensor_sub(st["va"][:, ss:ss + 1],
                                            st["va"][:, ss:ss + 1],
                                            st["vb"][:, ss:ss + 1]), inc=sepi.inc())
            SC(lambda e: e.activation(st["vb"][:, :], st["va"][:, :],
                                      AF.Sqrt, bias=0.0, scale=1.0),
               wait=[(sepi, sepi.n)], inc=ssq.inc())
            VS(lambda e: e.reciprocal(st["r3"][:, :], st["vb"][:, :]),
               wait=[(ssq, ssq.n)])
            V(lambda e, ss=s: e.tensor_scalar(nb[:, :], wbuf[:, :],
                                              st["mu3"][:, ss:ss + 1],
                                              st["r3"][:, ss:ss + 1],
                                              op0=A.subtract, op1=A.mult),
              wait=[(ssq, ssq.n), (svs, svs.n)])
            V(lambda e, ss=s: e.scalar_tensor_tensor(
                x1b[:, :], nb[:, :], 1.0, x_t[:, ss * D:(ss + 1) * D],
                op0=A.mult, op1=A.add, accum_out=st["sx"][:, ss:ss + 1]),
              wait=[(sst1, sst1.n)], inc=sepi.inc())
            SC(lambda e, ss=s: e.activation(sdump[:, :], x1b[:, :], AF.Square,
                                            accum_out=st["sq"][:, ss:ss + 1]),
               wait=[(sepi, sepi.n)], inc=ssq.inc())
            G(lambda e, tt=ft, ss=s: e.dma_start(
                x1scr[tt * TT + ss * 128: tt * TT + (ss + 1) * 128, :], x1b[:, :]),
              wait=[(ssq, ssq.n)], inc=sst1.inc(16))
            VS(lambda e, ss=s: e.tensor_scalar(st["mu2"][:, ss:ss + 1],
                                               st["sx"][:, ss:ss + 1],
                                               1.0 / D, None, op0=A.mult),
               wait=[(ssq, ssq.n)])
            VS(lambda e, ss=s: e.tensor_scalar(st["va"][:, ss:ss + 1],
                                               st["sq"][:, ss:ss + 1],
                                               1.0 / D, EPS, op0=A.mult, op1=A.add))
            VS(lambda e, ss=s: e.tensor_mul(st["vb"][:, ss:ss + 1],
                                            st["mu2"][:, ss:ss + 1],
                                            st["mu2"][:, ss:ss + 1]))
            VS(lambda e, ss=s: e.tensor_sub(st["va"][:, ss:ss + 1],
                                            st["va"][:, ss:ss + 1],
                                            st["vb"][:, ss:ss + 1]), inc=sepi.inc())
            SC(lambda e: e.activation(st["vb"][:, :], st["va"][:, :],
                                      AF.Sqrt, bias=0.0, scale=1.0),
               wait=[(sepi, sepi.n)], inc=ssq.inc())
            VS(lambda e: e.reciprocal(st["r2"][:, :], st["vb"][:, :]),
               wait=[(ssq, ssq.n)])
            if main_stop == "epi2":
                continue
            V(lambda e, ss=s: e.tensor_scalar(bsub[:, :], x1b[:, :],
                                              st["mu2"][:, ss:ss + 1],
                                              st["r2"][:, ss:ss + 1],
                                              op0=A.subtract, op1=A.mult),
              wait=[(ssq, ssq.n), (sbte, sbte.n), (svs, svs.n)], inc=sepi.inc())
            for dk in range(8):
                T(lambda e, d=dk, p=pzt: e.transpose(
                    p[:, d * 128:(d + 1) * 128], bsub[:, d * 128:(d + 1) * 128],
                    ident[:, :]),
                  wait=([(sepi, sepi.n)] if dk == 0 else []),
                  inc=(sbt.inc() if dk == 7 else None))
            SC(lambda e, ss=s, p=pzt: e.activation(
                bass.AP(bT_b, ss * 128, [[8 * TT, 128], [TT, 8], [1, 128]]),
                p[:, :], AF.Identity),
               wait=[(sbt, sbt.n), (sstb, sstb.n)], inc=sbte.inc())
            first_b = (t == (1 if shard else 0) and s == 0)
            VS(lambda e: e.tensor_reduce(
                st["vb"][:, 0:1], bsub[:, :], axis=AX, op=A.max,
                apply_absolute_value=True),
               wait=[(sepi, sepi.n)])
            if first_b:
                VS(lambda e: e.tensor_copy(bmax[:, 0:1], st["vb"][:, 0:1]),
                   inc=sepi.inc())
            else:
                VS(lambda e: e.tensor_max(bmax[:, 0:1], bmax[:, 0:1], st["vb"][:, 0:1]),
                   inc=sepi.inc())
        if main_stop in ("epi1", "epi2"):
            continue
        for dk in range(8):
            G(lambda e, tt=ft, d=dk: e.dma_start(
                bTscr[d, :, tt * TT:(tt + 1) * TT], bT_b[:, d * TT:(d + 1) * TT]),
              wait=[(sbte, sbte.n)], inc=sstb.inc(16))
    P.drain([sst1, sstb, sx, swos, scvs, svs])
    P.emit_block()
    _exit_all(sa)
    if stop_after == "attn":
        _exit_all(sb)
        for g in reversed(P.guards):
            g.__exit__(None, None, None)
        P.check()
        return nc

    scc = P.sem("scc")
    sccd = P.sem("sccd")

    def global_max(pairs):
        n = len(pairs)
        for i, (src_, dst) in enumerate(pairs):
            G(lambda e, s=src_, i=i: e.dma_start(
                bass.AP(foldd.tensor, i * 128, [[1, 128], [1, 1]]), s[:, 0:1]),
              wait=[(scc, scc.n), (sccd, sccd.n)], inc=sccd.inc(16))
        VS(lambda e: e.memset(locb[0:1, 0:16], 0.0), wait=[(sccd, sccd.n)],
           inc=scc.inc())
        for i in range(n):
            G(lambda e, i=i: e.dma_start(
                rowb[0:1, :], bass.AP(foldd.tensor, i * 128, [[128, 1], [1, 128]])),
              wait=[(sccd, sccd.n)], inc=sccd.inc(16))
            VS(lambda e, i=i: e.tensor_reduce(locb[0:1, i:i + 1], rowb[0:1, :],
                                              axis=AX, op=A.max),
               wait=[(sccd, sccd.n), (scc, scc.n)], inc=scc.inc())
        G(lambda e: e.dma_start(cc_in[0:1, 0:16], locb[0:1, 0:16]),
          wait=[(scc, scc.n)], inc=sccd.inc(16))
        if cc:
            G(lambda e: e.collective_compute(
                "AllReduce", A.max, replica_groups=[list(range(8))],
                ins=[cc_in.tensor.ap().opt()], outs=[cc_out.tensor.ap().opt()]),
              wait=[(sccd, sccd.n)], inc=scc.inc())
        else:
            G(lambda e: e.dma_start(cc_out[0:1, 0:16], cc_in[0:1, 0:16]),
              wait=[(sccd, sccd.n)], inc=sccd.inc(16))
        for i, (src_, dst) in enumerate(pairs):
            G(lambda e, i=i, d=dst: e.dma_start(
                d[:, 0:1], bass.AP(cc_out.tensor, i, [[0, 128], [1, 1]])),
              wait=[(scc, scc.n), (sccd, sccd.n)], inc=sccd.inc(16))
        for i, (src_, dst) in enumerate(pairs):
            VS(lambda e, d=dst: e.tensor_copy(
                d[:, 0:4], bass.AP(d, 0, [[4, 128], [0, 4]])),
               wait=[(sccd, sccd.n)], inc=scc.inc())

    def mix1(dst, src):
        VS(lambda e, d=dst, s=src: e.tensor_scalar(d[:, :], s[:, :], 1.0, None,
                                                   op0=A.subtract))
        VS(lambda e, d=dst: e.tensor_mul(d[:, :], d[:, :], ks[:, :]))
        VS(lambda e, d=dst: e.tensor_scalar(d[:, :], d[:, :], 1.0, None, op0=A.add))

    global_max([(bmax, scl["m_b"])])
    VS(lambda e: e.tensor_scalar(scl["s0"][:, :], scl["m_b"][:, :], r327[:, 0:1],
                                 1e-12, op0=A.mult, op1=A.max),
       wait=[(scc, scc.n), (sccd, sccd.n)])
    VS(lambda e: e.reciprocal(scl["r0"][:, :], scl["s0"][:, :]))
    mix1(scl["rK0"], scl["r0"])
    V(lambda e: e.nop() if False else e.tensor_copy(scl["tmpa"][:, :], scl["rK0"][:, :]),
      inc=scc.inc())
    P.emit_block()
    if stop_after == "b1":
        _exit_all(sb)
        for g in reversed(P.guards):
            g.__exit__(None, None, None)
        P.check()
        return nc

    sfa = []
    wupb = _enter(sfa, nc.sbuf_tensor("wupb", [128, 8 * I3], BF16))
    wgb = _enter(sfa, nc.sbuf_tensor("wgb", [128, 8 * I3], BF16))
    bTl = _enter(sfa, nc.sbuf_tensor("bTl", [128, 8 * TT], BF16))
    bTl2 = _enter(sfa, nc.sbuf_tensor("bTl2", [128, 8 * TT], BF16))
    xq = _enter(sfa, nc.sbuf_tensor("xq", [128, 8 * TT], BF16))
    xqf = _enter(sfa, nc.sbuf_tensor("xqf", [128, 8 * TT], F32))
    upe = _enter(sfa, nc.sbuf_tensor("upe", [128, 24 * TT], BF16))
    gte = _enter(sfa, nc.sbuf_tensor("gte", [128, 24 * TT], BF16))

    sfw = P.sem("sfw")
    sbl = P.sem("sbl")
    sqd = P.sem("sqd")
    sum_ = P.sem("sum_")
    sue = P.sem("sue")
    sgm = P.sem("sgm")
    sge = P.sem("sge")
    sus = P.sem("sus")
    sgs = P.sem("sgs")
    smx = P.sem("smx")

    for hk in range(8):
        SY(lambda e, h=hk: e.dma_start(wupb[:, h * I3:(h + 1) * I3], wupdq[h, :, :]),
           inc=sfw.inc(16))
        SY(lambda e, h=hk: e.dma_start(wgb[:, h * I3:(h + 1) * I3], wgdq[h, :, :]),
           inc=sfw.inc(16))
    for t in range(NFT):
        buf = bTl if t % 2 == 0 else bTl2
        for dk in range(8):
            SY(lambda e, tt=t, d=dk, b=buf: e.dma_start(
                b[:, d * TT:(d + 1) * TT], bTscr[d, :, tt * TT:(tt + 1) * TT]),
               wait=([(sstb, sstb.n)] if (t == 0 and dk == 0) else []) +
                    ([(sqd, max(0, sqd.n - 1))] if (t >= 2 and dk == 0) else []),
               inc=sbl.inc(16))
        V(lambda e, b=buf: e.tensor_scalar(xqf[:, :], b[:, :], scl["rK0"][:, 0:1],
                                           None, op0=A.mult),
          wait=[(sbl, sbl.n), (scc, scc.n), (smx, max(0, smx.n - 24))])
        V(lambda e: e.tensor_scalar(xq[:, :], xqf[:, :], ckv[:, 0:1], ckv[:, 0:1],
                                    op0=A.add, op1=A.subtract),
          inc=sqd.inc())
        for ik in range(24):
            pmt = pm0 if ik % 2 == 0 else pm1
            pgt = pc0 if ik % 2 == 0 else pz0
            for hk in range(8):
                T(lambda e, i=ik, h=hk, p=pmt: e.matmul(
                    p[:, :], wupb[:, h * I3 + i * 128: h * I3 + (i + 1) * 128],
                    xq[:, h * TT:(h + 1) * TT], start=(h == 0), stop=(h == 7)),
                  wait=([(sqd, sqd.n), (sfw, sfw.n),
                         (sue, max(0, sue.n - 1))] if hk == 0 else []),
                  inc=(sum_.inc() if hk == 7 else None))
            SC(lambda e, i=ik, p=pmt: e.activation(
                upe[:, i * TT:(i + 1) * TT], p[:, :], AF.Identity),
               wait=[(sum_, sum_.n), (sus, max(0, sus.n - 16 * 23))],
               inc=sue.inc())
            for hk in range(8):
                T(lambda e, i=ik, h=hk, p=pgt: e.matmul(
                    p[:, 0:512], wgb[:, h * I3 + i * 128: h * I3 + (i + 1) * 128],
                    xq[:, h * TT:(h + 1) * TT], start=(h == 0), stop=(h == 7)),
                  wait=([(sge, max(0, sge.n - 1))] if hk == 0 else []),
                  inc=(sgm.inc() if hk == 7 else None))
            SC(lambda e, i=ik, p=pgt: e.activation(
                gte[:, i * TT:(i + 1) * TT], p[:, 0:512], AF.Identity),
               wait=[(sgm, sgm.n), (sgs, max(0, sgs.n - 16 * 23))],
               inc=sge.inc())
            VS(lambda e, i=ik: e.tensor_reduce(scl["tmpb"][:, 0:1],
                                               upe[:, i * TT:(i + 1) * TT],
                                               axis=AX, op=A.max,
                                               apply_absolute_value=True),
               wait=[(sue, sue.n)])
            if t == 0 and ik == 0:
                VS(lambda e: e.tensor_copy(fmax["up"][:, 0:1], scl["tmpb"][:, 0:1]))
            else:
                VS(lambda e: e.tensor_max(fmax["up"][:, 0:1], fmax["up"][:, 0:1],
                                          scl["tmpb"][:, 0:1]))
            VS(lambda e, i=ik: e.tensor_reduce(scl["tmpc"][:, 0:1],
                                               gte[:, i * TT:(i + 1) * TT],
                                               axis=AX, op=A.max,
                                               apply_absolute_value=True),
               wait=[(sge, sge.n)])
            if t == 0 and ik == 0:
                VS(lambda e: e.tensor_copy(fmax["g"][:, 0:1], scl["tmpc"][:, 0:1]),
                   inc=smx.inc())
            else:
                VS(lambda e: e.tensor_max(fmax["g"][:, 0:1], fmax["g"][:, 0:1],
                                          scl["tmpc"][:, 0:1]),
                   inc=smx.inc())
            G(lambda e, tt=t, i=ik: e.dma_start(
                upscr[i, :, tt * TT:(tt + 1) * TT], upe[:, i * TT:(i + 1) * TT]),
              wait=[(smx, smx.n)], inc=sus.inc(16))
            G(lambda e, tt=t, i=ik: e.dma_start(
                gscr[i, :, tt * TT:(tt + 1) * TT], gte[:, i * TT:(i + 1) * TT]),
              inc=sgs.inc(16))
    P.drain([sus, sgs, svs])
    P.emit_block()
    _exit_all(sfa)
    if stop_after == "fa":
        _exit_all(sb)
        for g in reversed(P.guards):
            g.__exit__(None, None, None)
        P.check()
        return nc

    global_max([(fmax["up"], scl["m_up"]), (fmax["g"], scl["m_g"])])
    VS(lambda e: e.tensor_mul(scl["tmpa"][:, :], scl["m_up"][:, :], scl["s0"][:, :]),
       wait=[(scc, scc.n), (sccd, sccd.n)])
    VS(lambda e: e.tensor_scalar(scl["s_up"][:, :], scl["tmpa"][:, :], r327[:, 0:1],
                                 1e-12, op0=A.mult, op1=A.max))
    VS(lambda e: e.reciprocal(scl["r_up"][:, :], scl["s_up"][:, :]))
    VS(lambda e: e.tensor_mul(scl["tmpa"][:, :], scl["s0"][:, :], scl["r_up"][:, :]))
    mix1(scl["cU"], scl["tmpa"])
    VS(lambda e: e.tensor_mul(scl["tmpa"][:, :], scl["m_g"][:, :], scl["s0"][:, :]))
    VS(lambda e: e.tensor_scalar(scl["s_g"][:, :], scl["tmpa"][:, :], r327[:, 0:1],
                                 1e-12, op0=A.mult, op1=A.max))
    VS(lambda e: e.reciprocal(scl["r_g"][:, :], scl["s_g"][:, :]))
    VS(lambda e: e.tensor_mul(scl["tmpa"][:, :], scl["s0"][:, :], scl["r_g"][:, :]))
    mix1(scl["cG"], scl["tmpa"])
    VS(lambda e: e.tensor_mul(scl["tmpa"][:, :], scl["s_g"][:, :], ks[:, :]))
    VS(lambda e: e.tensor_scalar(scl["tmpb"][:, :], ks[:, :], -1.0, 1.0,
                                 op0=A.mult, op1=A.add))
    VS(lambda e: e.tensor_add(scl["sgmix"][:, :], scl["tmpa"][:, :], scl["tmpb"][:, :]))
    VS(lambda e: e.tensor_scalar(scl["tmpa"][:, :], ks[:, :], 65536.0, None, op0=A.mult))
    VS(lambda e: e.tensor_add(scl["c65"][:, :], scl["tmpa"][:, :], scl["tmpb"][:, :]),
       inc=scc.inc())
    P.emit_block()

    sb1 = []
    gl0 = _enter(sb1, nc.sbuf_tensor("gl0", [128, 24 * TT], BF16))
    qgb = _enter(sb1, nc.sbuf_tensor("qgb", [128, 24 * TT], BF16))
    sgf = _enter(sb1, nc.sbuf_tensor("sgf", [128, 12 * TT], F32))
    qsb = _enter(sb1, nc.sbuf_tensor("qsb", [128, 24 * TT], BF16))
    pb = _enter(sb1, nc.sbuf_tensor("pb", [128, 24 * TT], BF16))

    sgl = P.sem("sgl")
    sq1 = P.sem("sq1")
    ssg = P.sem("ssg")
    sq2 = P.sem("sq2")
    spp = P.sem("spp")
    sps = P.sem("sps")

    for t in range(NFT):
        for ik in range(24):
            SY(lambda e, tt=t, i=ik: e.dma_start(
                gl0[:, i * TT:(i + 1) * TT], gscr[i, :, tt * TT:(tt + 1) * TT]),
               wait=([(sgs, sgs.n)] if (t == 0 and ik == 0) else []) +
                    ([(sps, max(0, sps.n - 16 * 12))] if (t >= 1 and ik == 0) else []),
               inc=sgl.inc(16))
        for half in range(2):
            off = half * 12 * TT
            z = 12 * TT
            V(lambda e, o=off, zz=z: e.tensor_scalar(
                sgf[:, 0:zz], gl0[:, o:o + zz], scl["cG"][:, 0:1], None, op0=A.mult),
              wait=[(sgl, sgl.n), (scc, scc.n)] +
                   ([(spp, max(0, spp.n - 1))] if spp.n >= 1 else []))
            V(lambda e, o=off, zz=z: e.tensor_scalar(
                qgb[:, o:o + zz], sgf[:, 0:zz], ckv[:, 0:1], ckv[:, 0:1],
                op0=A.add, op1=A.subtract),
              inc=sq1.inc())
            SC(lambda e, o=off, zz=z: e.activation(
                sgf[:, 0:zz], qgb[:, o:o + zz], AF.Sigmoid, bias=0.0,
                scale=scl["sgmix"][:, 0:1]),
               wait=[(sq1, sq1.n)], inc=ssg.inc())
            V(lambda e, o=off, zz=z: e.tensor_scalar(
                sgf[:, 0:zz], sgf[:, 0:zz], scl["c65"][:, 0:1], None, op0=A.mult),
              wait=[(ssg, ssg.n)])
            V(lambda e, o=off, zz=z: e.tensor_scalar(
                qsb[:, o:o + zz], sgf[:, 0:zz], ckv[:, 0:1], ckv[:, 0:1],
                op0=A.add, op1=A.subtract))
            V(lambda e, o=off, zz=z: e.tensor_mul(
                pb[:, o:o + zz], qgb[:, o:o + zz], qsb[:, o:o + zz]),
              inc=sq2.inc())
            VS(lambda e, o=off, zz=z: e.tensor_reduce(
                scl["tmpb"][:, 0:1], pb[:, o:o + zz], axis=AX, op=A.max,
                apply_absolute_value=True),
               wait=[(sq2, sq2.n)])
            if t == 0 and half == 0:
                VS(lambda e: e.tensor_copy(fmax["p"][:, 0:1], scl["tmpb"][:, 0:1]),
                   inc=spp.inc())
            else:
                VS(lambda e: e.tensor_max(fmax["p"][:, 0:1], fmax["p"][:, 0:1],
                                          scl["tmpb"][:, 0:1]),
                   inc=spp.inc())
        for ik in range(24):
            G(lambda e, tt=t, i=ik: e.dma_start(
                pscr[i, :, tt * TT:(tt + 1) * TT], pb[:, i * TT:(i + 1) * TT]),
              wait=[(spp, spp.n)], inc=sps.inc(16))
    P.drain([sps, svs])
    P.emit_block()
    _exit_all(sb1)
    if stop_after == "fb1":
        _exit_all(sb)
        for g in reversed(P.guards):
            g.__exit__(None, None, None)
        P.check()
        return nc

    global_max([(fmax["p"], scl["m_p"])])
    VS(lambda e: e.tensor_mul(scl["tmpa"][:, :], scl["m_p"][:, :], scl["s_g"][:, :]),
       wait=[(scc, scc.n), (sccd, sccd.n)])
    VS(lambda e: e.tensor_scalar(scl["tmpa"][:, :], scl["tmpa"][:, :], 1.0 / 65536.0,
                                 None, op0=A.mult))
    VS(lambda e: e.tensor_scalar(scl["s_g2"][:, :], scl["tmpa"][:, :], r327[:, 0:1],
                                 1e-12, op0=A.mult, op1=A.max))
    VS(lambda e: e.reciprocal(scl["r_g2"][:, :], scl["s_g2"][:, :]))
    VS(lambda e: e.tensor_scalar(scl["tmpa"][:, :], scl["s_g"][:, :], 1.0 / 65536.0,
                                 None, op0=A.mult))
    VS(lambda e: e.tensor_mul(scl["tmpa"][:, :], scl["tmpa"][:, :], scl["r_g2"][:, :]))
    mix1(scl["c2m"], scl["tmpa"])
    VS(lambda e: e.tensor_copy(scl["tmpb"][:, :], scl["c2m"][:, :]), inc=scc.inc())
    P.emit_block()

    sb2 = []
    pl = _enter(sb2, nc.sbuf_tensor("pl", [128, 24 * TT], BF16))
    ul = _enter(sb2, nc.sbuf_tensor("ul", [128, 24 * TT], BF16))
    g2b = _enter(sb2, nc.sbuf_tensor("g2b", [128, 24 * TT], BF16))
    uqb = _enter(sb2, nc.sbuf_tensor("uqb", [128, 24 * TT], BF16))
    opb = _enter(sb2, nc.sbuf_tensor("opb", [128, 24 * TT], BF16))
    tf2 = _enter(sb2, nc.sbuf_tensor("tf2", [128, 12 * TT], F32))

    spl = P.sem("spl")
    sq3 = P.sem("sq3")
    sop = P.sem("sop")
    sos = P.sem("sos")

    for t in range(NFT):
        for ik in range(24):
            SY(lambda e, tt=t, i=ik: e.dma_start(
                pl[:, i * TT:(i + 1) * TT], pscr[i, :, tt * TT:(tt + 1) * TT]),
               wait=([(sps, sps.n), (sus, sus.n)] if (t == 0 and ik == 0) else []) +
                    ([(sos, max(0, sos.n - 16 * 12))] if (t >= 1 and ik == 0) else []),
               inc=spl.inc(16))
            SY(lambda e, tt=t, i=ik: e.dma_start(
                ul[:, i * TT:(i + 1) * TT], upscr[i, :, tt * TT:(tt + 1) * TT]),
               inc=spl.inc(16))
        for half in range(2):
            off = half * 12 * TT
            z = 12 * TT
            V(lambda e, o=off, zz=z: e.tensor_scalar(
                tf2[:, 0:zz], pl[:, o:o + zz], scl["c2m"][:, 0:1], None, op0=A.mult),
              wait=[(spl, spl.n), (scc, scc.n)] +
                   ([(sop, max(0, sop.n - 1))] if sop.n >= 1 else []))
            V(lambda e, o=off, zz=z: e.tensor_scalar(
                g2b[:, o:o + zz], tf2[:, 0:zz], ckv[:, 0:1], ckv[:, 0:1],
                op0=A.add, op1=A.subtract))
            V(lambda e, o=off, zz=z: e.tensor_scalar(
                tf2[:, 0:zz], ul[:, o:o + zz], scl["cU"][:, 0:1], None, op0=A.mult))
            V(lambda e, o=off, zz=z: e.tensor_scalar(
                uqb[:, o:o + zz], tf2[:, 0:zz], ckv[:, 0:1], ckv[:, 0:1],
                op0=A.add, op1=A.subtract))
            V(lambda e, o=off, zz=z: e.tensor_mul(
                opb[:, o:o + zz], g2b[:, o:o + zz], uqb[:, o:o + zz]),
              inc=sq3.inc())
            VS(lambda e, o=off, zz=z: e.tensor_reduce(
                scl["tmpb"][:, 0:1], opb[:, o:o + zz], axis=AX, op=A.max,
                apply_absolute_value=True),
               wait=[(sq3, sq3.n)])
            if t == 0 and half == 0:
                VS(lambda e: e.tensor_copy(fmax["op"][:, 0:1], scl["tmpb"][:, 0:1]),
                   inc=sop.inc())
            else:
                VS(lambda e: e.tensor_max(fmax["op"][:, 0:1], fmax["op"][:, 0:1],
                                          scl["tmpb"][:, 0:1]),
                   inc=sop.inc())
        for ik in range(24):
            G(lambda e, tt=t, i=ik: e.dma_start(
                opscr[i, :, tt * TT:(tt + 1) * TT], opb[:, i * TT:(i + 1) * TT]),
              wait=[(sop, sop.n)], inc=sos.inc(16))
    P.drain([sos, svs])
    P.emit_block()
    _exit_all(sb2)
    if stop_after == "fb2":
        _exit_all(sb)
        for g in reversed(P.guards):
            g.__exit__(None, None, None)
        P.check()
        return nc

    global_max([(fmax["op"], scl["m_op"])])
    VS(lambda e: e.tensor_mul(scl["tmpc"][:, :], scl["s_g2"][:, :], scl["s_up"][:, :]),
       wait=[(scc, scc.n), (sccd, sccd.n)])
    VS(lambda e: e.tensor_mul(scl["tmpb"][:, :], scl["tmpc"][:, :], scl["m_op"][:, :]))
    VS(lambda e: e.tensor_scalar(scl["s_o"][:, :], scl["tmpb"][:, :], r327[:, 0:1],
                                 1e-12, op0=A.mult, op1=A.max))
    VS(lambda e: e.reciprocal(scl["r_o"][:, :], scl["s_o"][:, :]))
    VS(lambda e: e.tensor_mul(scl["tmpa"][:, :], scl["tmpc"][:, :], scl["r_o"][:, :]))
    mix1(scl["c3m"], scl["tmpa"])
    VS(lambda e: e.tensor_mul(scl["tmpa"][:, :], scl["s_o"][:, :], ks[:, :]))
    VS(lambda e: e.tensor_scalar(scl["tmpb"][:, :], ks[:, :], -1.0, 1.0,
                                 op0=A.mult, op1=A.add))
    VS(lambda e: e.tensor_add(scl["cout"][:, :], scl["tmpa"][:, :], scl["tmpb"][:, :]),
       inc=scc.inc())
    P.emit_block()

    sb3 = []
    wdb = _enter(sb3, nc.sbuf_tensor("wdb", [128, 24 * D], BF16))
    opl = _enter(sb3, nc.sbuf_tensor("opl", [128, 24 * TT], BF16))
    oqb = _enter(sb3, nc.sbuf_tensor("oqb", [128, 24 * TT], BF16))
    tf3 = _enter(sb3, nc.sbuf_tensor("tf3", [128, 12 * TT], F32))
    yf = _enter(sb3, nc.sbuf_tensor("yf", [128, 8 * TT], F32))
    zf = _enter(sb3, nc.sbuf_tensor("zf", [128, D], F32))
    n3 = _enter(sb3, nc.sbuf_tensor("n3", [128, D], F32))
    x1l = _enter(sb3, nc.sbuf_tensor("x1l", [128, 2 * D], F32))
    x2b = _enter(sb3, nc.sbuf_tensor("x2b", [128, D], F32))
    sdump2 = _enter(sb3, nc.sbuf_tensor("sdump2", [128, D], F32))

    sdw = P.sem("sdw")
    sol = P.sem("sol")
    sq4 = P.sem("sq4")
    sdm = P.sem("sdm")
    sde = P.sem("sde")
    szt2 = P.sem("szt2")
    sep2 = P.sem("sep2")
    ssq2 = P.sem("ssq2")
    sxl = P.sem("sxl")
    sout = P.sem("sout")

    for ik in range(24):
        SY(lambda e, i=ik: e.dma_start(wdb[:, i * D:(i + 1) * D], wddq[i, :, :]),
           inc=sdw.inc(16))
    for t in range(NFT):
        for ik in range(24):
            SY(lambda e, tt=t, i=ik: e.dma_start(
                opl[:, i * TT:(i + 1) * TT], opscr[i, :, tt * TT:(tt + 1) * TT]),
               wait=([(sos, sos.n)] if (t == 0 and ik == 0) else []) +
                    ([(sq4, sq4.n)] if (t >= 1 and ik == 0) else []),
               inc=sol.inc(16))
        for half in range(2):
            off = half * 12 * TT
            z = 12 * TT
            V(lambda e, o=off, zz=z: e.tensor_scalar(
                tf3[:, 0:zz], opl[:, o:o + zz], scl["c3m"][:, 0:1], None, op0=A.mult),
              wait=[(sol, sol.n), (scc, scc.n)] +
                   ([(sdm, max(0, sdm.n - 1))] if sdm.n >= 1 else []))
            V(lambda e, o=off, zz=z: e.tensor_scalar(
                oqb[:, o:o + zz], tf3[:, 0:zz], ckv[:, 0:1], ckv[:, 0:1],
                op0=A.add, op1=A.subtract),
              inc=sq4.inc())
        for dk in range(8):
            pmt = pm0 if dk % 2 == 0 else pm1
            for ik in range(24):
                T(lambda e, d=dk, i=ik, p=pmt: e.matmul(
                    p[:, :], wdb[:, i * D + d * 128: i * D + (d + 1) * 128],
                    oqb[:, i * TT:(i + 1) * TT], start=(i == 0), stop=(i == 23)),
                  wait=([(sq4, sq4.n), (sdw, sdw.n),
                         (sde, max(0, sde.n - 1))] if ik == 0 else []),
                  inc=(sdm.inc() if ik == 23 else None))
            SC(lambda e, d=dk, p=pmt: e.activation(
                yf[:, d * TT:(d + 1) * TT], p[:, :], AF.Identity, bias=0.0,
                scale=scl["cout"][:, 0:1]),
               wait=[(sdm, sdm.n)], inc=sde.inc())
        for s in range(NSUB):
            pzt = pz0 if s % 2 == 0 else pz1
            xb = s % 2
            SY(lambda e, tt=t, ss=s, b=xb: e.dma_start(
                x1l[:, b * D:(b + 1) * D],
                x1scr[tt * TT + ss * 128: tt * TT + (ss + 1) * 128, :]),
               wait=([(sst1, sst1.n)] if (t == 0 and s == 0) else []) +
                    ([(sep2, max(0, sep2.n - 3))] if sep2.n >= 3 else []),
               inc=sxl.inc(16))
            for dk in range(8):
                T(lambda e, d=dk, ss=s, p=pzt: e.transpose(
                    p[:, d * 128:(d + 1) * 128],
                    yf[:, d * TT + ss * 128: d * TT + (ss + 1) * 128], ident[:, :]),
                  wait=([(sde, sde.n), (sep2, max(0, sep2.n - 5))] if dk == 0 else []),
                  inc=(szt2.inc() if dk == 7 else None))
            V(lambda e, ss=s, p=pzt: e.tensor_scalar(
                zf[:, :], p[:, :], 1.0, 0.0, op0=A.mult, op1=A.add,
                accum_out=st["sx"][:, ss:ss + 1]),
              wait=[(szt2, szt2.n), (ssq2, ssq2.n)], inc=sep2.inc())
            SC(lambda e, ss=s: e.activation(sdump2[:, :], zf[:, :], AF.Square,
                                            accum_out=st["sq"][:, ss:ss + 1]),
               wait=[(sep2, sep2.n)], inc=ssq2.inc())
            VS(lambda e, ss=s: e.tensor_scalar(st["mu1"][:, ss:ss + 1],
                                              st["sx"][:, ss:ss + 1],
                                              1.0 / D, None, op0=A.mult),
               wait=[(ssq2, ssq2.n)])
            VS(lambda e, ss=s: e.tensor_scalar(st["va"][:, ss:ss + 1],
                                              st["sq"][:, ss:ss + 1],
                                              1.0 / D, EPS, op0=A.mult, op1=A.add))
            VS(lambda e, ss=s: e.tensor_mul(st["vb"][:, ss:ss + 1],
                                           st["mu1"][:, ss:ss + 1],
                                           st["mu1"][:, ss:ss + 1]))
            VS(lambda e, ss=s: e.tensor_sub(st["va"][:, ss:ss + 1],
                                           st["va"][:, ss:ss + 1],
                                           st["vb"][:, ss:ss + 1]), inc=sep2.inc())
            SC(lambda e: e.activation(st["vb"][:, :], st["va"][:, :],
                                      AF.Sqrt, bias=0.0, scale=1.0),
                wait=[(sep2, sep2.n)], inc=ssq2.inc())
            VS(lambda e: e.reciprocal(st["r1"][:, :], st["vb"][:, :]),
               wait=[(ssq2, ssq2.n)])
            V(lambda e, ss=s: e.tensor_scalar(n3[:, :], zf[:, :],
                                              st["mu1"][:, ss:ss + 1],
                                              st["r1"][:, ss:ss + 1],
                                              op0=A.subtract, op1=A.mult),
              wait=[(ssq2, ssq2.n), (svs, svs.n)])
            V(lambda e, b=xb: e.tensor_add(x2b[:, :], n3[:, :],
                                           x1l[:, b * D:(b + 1) * D]),
              wait=[(sxl, sxl.n), (sout, sout.n)], inc=sep2.inc())
            SY(lambda e, tt=t, ss=s: e.dma_start(
                out[tt * TT + ss * 128: tt * TT + (ss + 1) * 128, :], x2b[:, :]),
               wait=[(sep2, sep2.n)], inc=sout.inc(16))
    if dbg:
        _names = ["m_b", "s0", "r0", "rK0", "m_up", "s_up", "cU", "m_g", "s_g",
                  "cG", "sgmix", "c65", "m_p", "s_g2", "c2m", "m_op", "s_o",
                  "c3m", "cout", "r_o", "r_g2", "r_up", "r_g"]
        sdbg = P.sem("sdbg")
        for _i, _nm in enumerate(_names):
            G(lambda e, i=_i, b=scl[_nm]: e.dma_start(dbgscl[i:i + 1, :], b[0:1, :]),
              wait=[(scc, scc.n), (sout, sout.n)], inc=sdbg.inc(16))
        G(lambda e, b=ks: e.dma_start(dbgscl[30:31, :], b[0:1, :]), inc=sdbg.inc(16))
        G(lambda e, b=ckv: e.dma_start(dbgscl[31:32, :], b[0:1, :]), inc=sdbg.inc(16))
        G(lambda e: e.nop(), wait=[(sdbg, sdbg.n)])
    SY(lambda e: e.nop(), wait=[(sout, sout.n)])
    P.emit_block()
    _exit_all(sb3)

    _exit_all(sb)
    for g in reversed(P.guards):
        g.__exit__(None, None, None)
    P.check()
    return nc


def _prep_inputs(inputs, shard=True):
    x = np.asarray(inputs["L0_attn_x"])
    W_in = np.asarray(inputs["L0_attn_W_in"])
    conv_w = np.asarray(inputs["L0_attn_conv_w"])
    conv_b = np.asarray(inputs["L0_attn_conv_b"])
    W_B = np.asarray(inputs["L0_attn_W_B"])
    W_C = np.asarray(inputs["L0_attn_W_C"])
    D_param = np.asarray(inputs["L0_attn_D_param"])
    W_out = np.asarray(inputs["L0_attn_W_out"])
    ln_g = np.asarray(inputs["L0_attn_ln_gamma"])
    ln_b = np.asarray(inputs["L0_attn_ln_beta"])
    w_gate = np.asarray(inputs["L0_ffn_w_gate"])
    w_up = np.asarray(inputs["L0_ffn_w_up"])
    w_down = np.asarray(inputs["L0_ffn_w_down"])

    f = np.float32
    com = {
        "winT": np.ascontiguousarray(W_in.T, dtype=f),
        "wBT": np.ascontiguousarray(W_B.T, dtype=f),
        "wCT": np.ascontiguousarray(W_C.T, dtype=f),
        "woutT": np.ascontiguousarray(W_out.T, dtype=f),
        "convb": np.ascontiguousarray(np.asarray(conv_b, f).reshape(16, 128).T),
        "dparam": np.ascontiguousarray(np.asarray(D_param, f).reshape(16, 128).T),
        "gb": None, "bb": None,
        "gbin": np.ascontiguousarray(np.broadcast_to(ln_g, (128, D)), dtype=f),
        "bbin": np.ascontiguousarray(np.broadcast_to(ln_b, (128, D)), dtype=f),
        "wupT": np.ascontiguousarray(w_up.T, dtype=f),
        "wgT": np.ascontiguousarray(w_gate.T, dtype=f),
        "wdT": np.ascontiguousarray(w_down.T, dtype=f),
        "wup_n": np.ascontiguousarray(w_up, dtype=f),
        "wg_n": np.ascontiguousarray(w_gate, dtype=f),
        "wd_n": np.ascontiguousarray(w_down, dtype=f),
        "identin": np.eye(128, dtype=f),
    }
    del com["gb"], com["bb"]
    cd = np.zeros((64 * 128, 128), f)
    idx = np.arange(128)
    for tap in range(4):
        for ck in range(16):
            base = (tap * 16 + ck) * 128
            cd[base + idx, idx] = conv_w[ck * 128 + idx, 0, tap]
    com["convd"] = cd

    maps = []
    if shard:
        for c in range(8):
            b, q = divmod(c, 4)
            lo = q * 4096
            if q == 0:
                xs = np.concatenate([np.zeros((512, D), f), x[b, 0:4096]], axis=0)
            else:
                xs = x[b, lo - 512: lo + 4096]
            m = dict(com)
            m["xp"] = np.ascontiguousarray(xs, dtype=f)
            m["maskin"] = np.full((1, 4), 0.0 if q == 0 else 1.0, f)
            maps.append(m)
    else:
        xall = np.ascontiguousarray(x.reshape(32768, D), dtype=f)
        for c in range(8):
            m = dict(com)
            m["xp"] = xall
            m["maskin"] = np.full((1, 4), 1.0, f)
            maps.append(m)
    return maps


def kernel(**inputs):
    shard = False
    maps = _prep_inputs(inputs, shard=shard)
    nc = build(shard=shard)
    res = run_bass_kernel_spmd(nc, maps, core_ids=list(range(8)))
    out = np.empty((2, 16384, 1024), np.float32)
    if shard:
        for c in range(8):
            b, q = divmod(c, 4)
            out[b, q * 4096:(q + 1) * 4096] = res.results[c]["out"]
    else:
        out[:] = res.results[0]["out"].reshape(2, 16384, 1024)
    return out


# version 11
# speedup vs baseline: 5.9216x; 5.9216x over previous
import numpy as np
import concourse.bass as bass
import concourse.mybir as mybir
from concourse.bass_utils import run_bass_kernel_spmd

F32 = mybir.dt.float32
BF16 = mybir.dt.bfloat16
A = mybir.AluOpType
AF = mybir.ActivationFunctionType
AX = mybir.AxisListType.X

D = 1024
C2 = 2048
S64 = 64
I3 = 3072
TT = 512
NSUB = 4
C32 = 12582912.0
C52 = 6755399441055744.0
EPS = 1e-5


class Sem:
    def __init__(self, handle, name):
        self.h = handle
        self.name = name
        self.n = 0

    def inc(self, amt=1):
        self.n += amt
        return (self, amt)


class Planner:
    ENGINES = ("sync", "vector", "scalar", "tensor", "gpsimd")

    def __init__(self, nc):
        self.nc = nc
        self.guards = []
        self.sems = {}
        self.events = {e: [] for e in self.ENGINES}
        self.waits_seen = []

    def sem(self, name):
        if name not in self.sems:
            g = self.nc.semaphore(name)
            h = g.__enter__()
            self.guards.append(g)
            self.sems[name] = Sem(h, name)
        return self.sems[name]

    def ev(self, eng, fn, wait=(), inc=None):
        wait = [w for w in wait if w[1] > 0]
        for s, v in wait:
            self.waits_seen.append((s, v))
        self.events[eng].append((wait, fn, inc))

    def drain(self, sems):
        targets = [(s, s.n) for s in sems if s.n > 0]
        for e in self.ENGINES:
            self.ev(e, lambda eng: eng.nop(), wait=targets)

    def emit_block(self):
        evs = {e: self.events[e] for e in self.ENGINES if self.events[e]}
        if not evs:
            return
        with self.nc.Block() as blk:
            for ename, elist in evs.items():
                def body(eng, elist=elist):
                    for wait, fn, inc in elist:
                        for s, v in wait:
                            eng.wait_ge(s.h, v)
                        inst = fn(eng)
                        if inc is not None:
                            s, amt = inc
                            inst.then_inc(s.h, amt)
                getattr(blk, ename)(body)
        self.events = {e: [] for e in self.ENGINES}

    def check(self):
        for s, v in self.waits_seen:
            assert v <= s.n, f"wait on {s.name} for {v} but final count {s.n}"


def _enter(stack, guard):
    t = guard.__enter__()
    stack.append(guard)
    return t


def _exit_all(stack):
    for g in reversed(stack):
        g.__exit__(None, None, None)
    stack.clear()


def build(shard=True, dbg=False, cc=None, stop_after=None, nt_limit=None, main_stop=None):
    nc = bass.Bass("TRN2", target_bir_lowering=False,
                   detect_race_conditions=False)
    if cc is None:
        cc = shard
    if shard:
        NT = 9
        resets = {0}
        NTOK = 4096
        XROWS = 4608
    else:
        NT = 64
        resets = {0, 32}
        NTOK = 32768
        XROWS = 32768
    NFT = NTOK // TT

    xp = nc.dram_tensor("xp", [XROWS, D], F32, kind="ExternalInput").ap()
    winT = nc.dram_tensor("winT", [D, 4096], F32, kind="ExternalInput").ap()
    wBT = nc.dram_tensor("wBT", [C2, S64], F32, kind="ExternalInput").ap()
    wCT = nc.dram_tensor("wCT", [S64, C2], F32, kind="ExternalInput").ap()
    woutT = nc.dram_tensor("woutT", [C2, D], F32, kind="ExternalInput").ap()
    convd = nc.dram_tensor("convd", [64 * 128, 128], F32, kind="ExternalInput").ap()
    convb = nc.dram_tensor("convb", [128, 16], F32, kind="ExternalInput").ap()
    dparam = nc.dram_tensor("dparam", [128, 16], F32, kind="ExternalInput").ap()
    gbin = nc.dram_tensor("gbin", [128, D], F32, kind="ExternalInput").ap()
    bbin = nc.dram_tensor("bbin", [128, D], F32, kind="ExternalInput").ap()
    wupT = nc.dram_tensor("wupT", [D, I3], F32, kind="ExternalInput").ap()
    wgT = nc.dram_tensor("wgT", [D, I3], F32, kind="ExternalInput").ap()
    wdT = nc.dram_tensor("wdT", [I3, D], F32, kind="ExternalInput").ap()
    wup_n = nc.dram_tensor("wup_n", [I3, D], F32, kind="ExternalInput").ap()
    wg_n = nc.dram_tensor("wg_n", [I3, D], F32, kind="ExternalInput").ap()
    wd_n = nc.dram_tensor("wd_n", [D, I3], F32, kind="ExternalInput").ap()
    maskin = nc.dram_tensor("maskin", [1, 4], F32, kind="ExternalInput").ap()
    identin = nc.dram_tensor("identin", [128, 128], F32, kind="ExternalInput").ap()
    out = nc.dram_tensor("out", [NTOK, D], F32, kind="ExternalOutput").ap()

    kd = "ExternalOutput" if dbg else "Internal"
    x1scr = nc.dram_tensor("x1scr", [NTOK, D], F32, kind=kd).ap()
    bTscr = nc.dram_tensor("bTscr", [NTOK // TT, 128, 8 * TT], BF16, kind=kd).ap()
    woutscr = nc.dram_tensor("woutscr", [16, 128, D], BF16).ap()
    convdscr = nc.dram_tensor("convdscr", [64 * 128, 128], BF16).ap()
    wupdq = nc.dram_tensor("wupdq", [8, 128, I3], BF16, kind=kd).ap()
    wgdq = nc.dram_tensor("wgdq", [8, 128, I3], BF16).ap()
    wddq = nc.dram_tensor("wddq", [24, 128, D], BF16).ap()
    upscr = nc.dram_tensor("upscr", [NTOK // TT, 128, 24 * TT], BF16).ap()
    gscr = nc.dram_tensor("gscr", [NTOK // TT, 128, 24 * TT], BF16).ap()
    pscr = nc.dram_tensor("pscr", [NTOK // TT, 128, 24 * TT], BF16).ap()
    opscr = nc.dram_tensor("opscr", [NTOK // TT, 128, 24 * TT], BF16).ap()
    sdU = nc.dram_tensor("sdU", [32, I3], F32).ap()
    sdG = nc.dram_tensor("sdG", [32, I3], F32).ap()
    sdD = nc.dram_tensor("sdD", [96, D], F32).ap()
    rdU = nc.dram_tensor("rdU", [32, I3], F32).ap()
    rdG = nc.dram_tensor("rdG", [32, I3], F32).ap()
    rdD = nc.dram_tensor("rdD", [96, D], F32).ap()
    foldd = nc.dram_tensor("foldd", [4, 128], F32).ap()
    dbgscl = nc.dram_tensor("dbgscl", [32, 4], F32, kind=kd).ap()
    dbg_aT = nc.dram_tensor("dbg_aT", [128, 8 * TT], BF16, kind=kd).ap()
    dbg_u = nc.dram_tensor("dbg_u", [128, 16 * TT], BF16, kind=kd).ap()
    dbg_sxg = nc.dram_tensor("dbg_sxg", [128, 16 * TT], BF16, kind=kd).ap()
    dbg_hs = nc.dram_tensor("dbg_hs", [S64, TT], F32, kind=kd).ap()
    dbg_yo = nc.dram_tensor("dbg_yo", [128, 8 * TT], F32, kind=kd).ap()
    dbg_z = nc.dram_tensor("dbg_z", [128, D], F32, kind=kd).ap()
    dbg_x = nc.dram_tensor("dbg_x", [128, NSUB * D], F32, kind=kd).ap()
    dbg_a = nc.dram_tensor("dbg_a", [128, D], F32, kind=kd).ap()
    dbg_w = nc.dram_tensor("dbg_w", [128, 4096], BF16, kind=kd).ap()
    cc_in = nc.dram_tensor("cc_in", [1, 16], F32).ap()
    cc_out = nc.dram_tensor("cc_out", [1, 16], F32).ap()

    P = Planner(nc)
    sb = []

    ident = _enter(sb, nc.sbuf_tensor("ident", [128, 128], F32))
    ks = _enter(sb, nc.sbuf_tensor("ks", [128, 4], F32))
    ckv = _enter(sb, nc.sbuf_tensor("ckv", [128, 4], F32))
    r7 = _enter(sb, nc.sbuf_tensor("r7", [128, 4], F32))
    r327 = _enter(sb, nc.sbuf_tensor("r327", [128, 4], F32))
    mask_sb = _enter(sb, nc.sbuf_tensor("mask_sb", [128, 4], F32))
    dpar = _enter(sb, nc.sbuf_tensor("dpar", [128, 16], F32))
    cvb = _enter(sb, nc.sbuf_tensor("cvb", [128, 16], F32))
    gbs = _enter(sb, nc.sbuf_tensor("gbs", [128, D], F32))
    bbs = _enter(sb, nc.sbuf_tensor("bbs", [128, D], F32))
    dec95 = _enter(sb, nc.sbuf_tensor("dec95", [S64, TT], F32))
    zk64 = _enter(sb, nc.sbuf_tensor("zk64", [S64, 4], F32))
    scl = {}
    for nm in ("m_b", "s0", "r0", "rK0", "m_up", "s_up", "r_up", "cU",
               "m_g", "s_g", "r_g", "cG", "sgmix", "c65", "m_p", "s_g2",
               "r_g2", "c2m", "m_op", "s_o", "r_o", "c3m", "cout",
               "tmpa", "tmpb", "tmpc"):
        scl[nm] = _enter(sb, nc.sbuf_tensor("scl_" + nm, [128, 4], F32))
    bmax = _enter(sb, nc.sbuf_tensor("bmax", [128, 4], F32))
    dumb = _enter(sb, nc.sbuf_tensor("dumb", [128, 4], F32))
    fmax = {nm: _enter(sb, nc.sbuf_tensor("fmax_" + nm, [128, 4], F32))
            for nm in ("up", "g", "p", "op")}
    rowb = _enter(sb, nc.sbuf_tensor("rowb", [1, 128], F32))
    locb = _enter(sb, nc.sbuf_tensor("locb", [1, 16], F32))
    loc2b = _enter(sb, nc.sbuf_tensor("loc2b", [1, 16], F32))
    st = {nm: _enter(sb, nc.sbuf_tensor("st_" + nm, [128, 4], F32))
          for nm in ("sx", "sq", "mu1", "r1", "mu2", "r2", "mu3", "r3", "va", "vb")}
    pz0 = _enter(sb, nc.psum_tensor("pz0", [128, 1024], F32))
    pz1 = _enter(sb, nc.psum_tensor("pz1", [128, 1024], F32))
    pm0 = _enter(sb, nc.psum_tensor("pm0", [128, 512], F32))
    pm1 = _enter(sb, nc.psum_tensor("pm1", [128, 512], F32))
    pc0 = _enter(sb, nc.psum_tensor("pc0", [128, 512], F32))
    pbu = _enter(sb, nc.psum_tensor("pbu", [128, 512], F32))

    svs = P.sem("svs")

    def V(fn, wait=(), inc=None):
        P.ev("vector", fn, wait, inc)

    def VS(fn, wait=(), inc=None):
        P.ev("vector", fn, list(wait) + [(svs, svs.n)], svs.inc())
        if inc is not None:
            P.ev("vector",
                 lambda e: e.tensor_copy(dumb[:, 0:4], ks[:, 0:4]),
                 [(svs, svs.n)], inc)

    def SC(fn, wait=(), inc=None):
        P.ev("scalar", fn, wait, inc)

    def G(fn, wait=(), inc=None):
        P.ev("gpsimd", fn, wait, inc)

    def T(fn, wait=(), inc=None):
        P.ev("tensor", fn, wait, inc)

    def SY(fn, wait=(), inc=None):
        P.ev("sync", fn, wait, inc)

    sload = P.sem("sload")
    sprep = P.sem("sprep")
    scast = P.sem("scast")
    sst = P.sem("sst")

    for dst, src in ((ident[:, :], identin),
                     (mask_sb[:, :], bass.AP(maskin.tensor, 0, [[0, 128], [1, 4]])),
                     (dpar[:, :], dparam),
                     (cvb[:, :], convb),
                     (gbs[:, :], gbin),
                     (bbs[:, :], bbin)):
        SY(lambda e, d=dst, s=src: e.dma_start(d, s), inc=sload.inc(16))

    VS(lambda e: e.memset(ks[:, :], 1.0))
    VS(lambda e: e.scalar_tensor_tensor(scl["tmpb"][:, :], ks[:, :],
                                        1.0 + 2.0 ** -40, ks[:, :],
                                        op0=A.mult, op1=A.subtract))
    VS(lambda e: e.tensor_scalar(ks[:, :], scl["tmpb"][:, :], 2.0 ** 40, None,
                                 op0=A.mult))
    VS(lambda e: e.tensor_scalar(ckv[:, :], ks[:, :], C52, None, op0=A.mult))
    VS(lambda e: e.memset(scl["tmpa"][:, :], 7.0))
    VS(lambda e: e.reciprocal(r7[:, :], scl["tmpa"][:, :]))
    VS(lambda e: e.memset(scl["tmpa"][:, :], 32767.0))
    VS(lambda e: e.reciprocal(r327[:, :], scl["tmpa"][:, :]))
    V(lambda e: e.memset(dec95[:, :], 0.95))
    for _nm in ("sx", "sq", "mu1", "r1", "mu2", "r2", "mu3", "r3", "va", "vb"):
        VS(lambda e, b=st[_nm]: e.memset(b[:, :], 1.0))
    VS(lambda e: e.memset(zk64[:, :], 0.0), inc=sprep.inc())

    st0 = []
    wstg = _enter(st0, nc.sbuf_tensor("wstg", [128, 4096], F32))
    wstg2 = _enter(st0, nc.sbuf_tensor("wstg2", [128, 4096], F32))
    wbf = _enter(st0, nc.sbuf_tensor("wbf", [128, 4096], BF16))
    wbf2 = _enter(st0, nc.sbuf_tensor("wbf2", [128, 4096], BF16))
    jobs = [("wo", c) for c in range(16)] + [("cv", c) for c in range(4)]
    cast_v = []
    for j, (kind, c) in enumerate(jobs):
        sg = wstg if j % 2 == 0 else wstg2
        bg = wbf if j % 2 == 0 else wbf2
        w = [(sprep, cast_v[j - 2])] if j >= 2 else []
        if kind == "wo":
            SY(lambda e, c=c, b=sg: e.dma_start(b[:, 0:D],
                                                woutT[c * 128:(c + 1) * 128, :]),
               wait=w, inc=scast.inc(16))
            V(lambda e, b=sg, o=bg: e.tensor_copy(o[:, 0:D], b[:, 0:D]),
              wait=[(scast, scast.n), (sst, max(0, sst.n - 16))], inc=sprep.inc())
            cast_v.append(sprep.n)
            SY(lambda e, c=c, o=bg: e.dma_start(
                bass.AP(woutscr.tensor, c * 128 * D, [[D, 128], [1, D]]), o[:, 0:D]),
               wait=[(sprep, sprep.n)], inc=sst.inc(16))
        else:
            SY(lambda e, c=c, b=sg: e.dma_start(
                b[:, 0:2048],
                bass.AP(convd.tensor, c * 16 * 128 * 128,
                        [[128, 128], [128 * 128, 16], [1, 128]])),
               wait=w, inc=scast.inc(16))
            V(lambda e, b=sg, o=bg: e.tensor_copy(o[:, 0:2048], b[:, 0:2048]),
              wait=[(scast, scast.n), (sst, max(0, sst.n - 16))], inc=sprep.inc())
            cast_v.append(sprep.n)
            SY(lambda e, c=c, o=bg: e.dma_start(
                bass.AP(convdscr.tensor, c * 16 * 128 * 128,
                        [[128, 128], [128 * 128, 16], [1, 128]]),
                o[:, 0:2048]),
               wait=[(sprep, sprep.n)], inc=sst.inc(16))
    std = st0
    wnat = _enter(std, nc.sbuf_tensor("wnat", [128, I3], F32))
    snat = _enter(std, nc.sbuf_tensor("snat", [128, 768], F32))
    rnat = _enter(std, nc.sbuf_tensor("rnat", [128, 768], F32))
    sTs = _enter(std, nc.sbuf_tensor("sTs", [96, I3], F32))
    rTs = _enter(std, nc.sbuf_tensor("rTs", [96, I3], F32))
    sbc = _enter(std, nc.sbuf_tensor("sbc", [128, I3], F32))
    rbc = _enter(std, nc.sbuf_tensor("rbc", [128, I3], F32))
    wTb = _enter(std, nc.sbuf_tensor("wTb", [128, I3], F32))
    wT2 = _enter(std, nc.sbuf_tensor("wT2", [128, I3], F32))
    wqb = _enter(std, nc.sbuf_tensor("wqb", [128, I3], BF16))
    wqb2 = _enter(std, nc.sbuf_tensor("wqb2", [128, I3], BF16))
    wnat2 = _enter(std, nc.sbuf_tensor("wnat2", [128, I3], F32))
    sbc2 = _enter(std, nc.sbuf_tensor("sbc2", [128, I3], F32))
    rbc2 = _enter(std, nc.sbuf_tensor("rbc2", [128, I3], F32))
    sdqd = P.sem("sdqd")
    sdqg = P.sem("sdqg")
    sdqs = P.sem("sdqs")
    sdqc = P.sem("sdqc")

    def dequant_weight(nat_ap, T_ap, nrows, iblk, sdram, rdram, dqscr, nTchunks, tfree):
        nch = nrows // 128
        nfree = iblk * 32
        red_v = []
        for ck in range(nch):
            b = wnat if ck % 2 == 0 else wnat2
            w = ([(sdqc, red_v[ck - 2])] if ck >= 2 else [(sdqc, sdqc.n)])
            SY(lambda e, c=ck, b=b: e.dma_start(b[:, 0:nfree],
                                                nat_ap[c * 128:(c + 1) * 128, :]),
               wait=w, inc=sdqg.inc(16))
            V(lambda e, c=ck, b=b: e.tensor_reduce(
                bass.AP(snat, c * iblk, [[768, 128], [1, iblk], [1, 1]]),
                b[:, 0:nfree].rearrange("p (b k) -> p b k", k=32),
                axis=AX, op=A.max, apply_absolute_value=True),
              wait=[(sdqg, sdqg.n)], inc=sdqc.inc())
            red_v.append(sdqc.n)
        V(lambda e: e.tensor_scalar(snat[:, 0:nch * iblk], snat[:, 0:nch * iblk],
                                    r7[:, 0:1], 1e-12, op0=A.mult, op1=A.max),
          wait=[(sdqc, sdqc.n), (svs, svs.n)])
        V(lambda e: e.reciprocal(rnat[:, 0:nch * iblk], snat[:, 0:nch * iblk]),
          inc=sdqc.inc())
        for ck in range(nch):
            pzt = pz0 if ck % 2 == 0 else pz1
            T(lambda e, c=ck, p=pzt: e.transpose(
                p[0:iblk, 0:128],
                bass.AP(snat, c * iblk, [[768, 128], [1, iblk]]), ident[:, :]),
              wait=[(sdqc, sdqc.n), (sdqd, max(0, sdqd.n - 32))] if ck < 2 else
                   [(sdqc, sdqc.n)])
            T(lambda e, c=ck, p=pzt: e.transpose(
                p[0:iblk, 128:256],
                bass.AP(rnat, c * iblk, [[768, 128], [1, iblk]]), ident[:, :]),
              inc=sdqc.inc())
            V(lambda e, c=ck, p=pzt: e.tensor_copy(
                sTs[0:iblk, c * 128:(c + 1) * 128], p[0:iblk, 0:128]),
              wait=[(sdqc, sdqc.n)])
            V(lambda e, c=ck, p=pzt: e.tensor_copy(
                rTs[0:iblk, c * 128:(c + 1) * 128], p[0:iblk, 128:256]),
              inc=sdqc.inc())
        SY(lambda e: e.dma_start(sdram[0:iblk, 0:nrows], sTs[0:iblk, 0:nrows]),
           wait=[(sdqc, sdqc.n)], inc=sdqd.inc(16))
        SY(lambda e: e.dma_start(rdram[0:iblk, 0:nrows], rTs[0:iblk, 0:nrows]),
           inc=sdqd.inc(16))
        base_d = sdqd.n
        qdq_v = []
        st_v = []
        for hk in range(nTchunks):
            bW = wTb if hk % 2 == 0 else wT2
            bS = sbc if hk % 2 == 0 else sbc2
            bR = rbc if hk % 2 == 0 else rbc2
            w = ([(sdqc, qdq_v[hk - 2])] if hk >= 2 else [(sdqc, sdqc.n)])
            SY(lambda e, c=hk, b=bW: e.dma_start(b[:, 0:tfree],
                                                 T_ap[c * 128:(c + 1) * 128, :]),
               wait=w, inc=sdqg.inc(16))
            SY(lambda e, c=hk, b=bS: e.dma_start(
                b[:, 0:tfree],
                bass.AP(sdram.tensor, c * 4 * tfree, [[tfree, 4], [0, 32], [1, tfree]])),
               wait=([(sdqd, base_d)] if hk < 2 else
                     [(sdqc, qdq_v[hk - 2])]),
               inc=sdqs.inc(16))
            SY(lambda e, c=hk, b=bR: e.dma_start(
                b[:, 0:tfree],
                bass.AP(rdram.tensor, c * 4 * tfree, [[tfree, 4], [0, 32], [1, tfree]])),
               inc=sdqs.inc(16))
            V(lambda e, b=bW, r=bR: e.tensor_mul(b[:, 0:tfree], b[:, 0:tfree],
                                                 r[:, 0:tfree]),
              wait=[(sdqs, sdqs.n), (sdqg, sdqg.n)] +
                   ([(sdqd, max(0, st_v[hk - 2]))] if hk >= 2 else []))
            V(lambda e, b=bW: e.tensor_scalar(b[:, 0:tfree], b[:, 0:tfree],
                                              C32, C32, op0=A.add, op1=A.subtract))
            V(lambda e, b=bW: e.tensor_scalar(b[:, 0:tfree], b[:, 0:tfree],
                                              ckv[:, 0:1], ckv[:, 0:1],
                                              op0=A.add, op1=A.subtract))
            V(lambda e, b=bW, s=bS, q=(wqb if hk % 2 == 0 else wqb2):
              e.tensor_mul(q[:, 0:tfree], b[:, 0:tfree], s[:, 0:tfree]),
              inc=sdqc.inc())
            qdq_v.append(sdqc.n)
            SY(lambda e, c=hk, d=dqscr, q=(wqb if hk % 2 == 0 else wqb2):
               e.dma_start(d[c, :, :], q[:, 0:tfree]),
               wait=[(sdqc, sdqc.n)], inc=sdqd.inc(16))
            st_v.append(sdqd.n)

    dequant_weight(wup_n, wupT, I3, 32, sdU, rdU, wupdq, 8, I3)
    dequant_weight(wg_n, wgT, I3, 32, sdG, rdG, wgdq, 24, D) if False else None
    dequant_weight(wg_n, wgT, I3, 32, sdG, rdG, wgdq, 8, I3)
    dequant_weight(wd_n, wdT, D, 96, sdD, rdD, wddq, 24, D)
    P.drain([sload, scast, sst, sprep, sdqd, sdqg, sdqs, sdqc, svs])
    P.emit_block()
    _exit_all(std)
    if stop_after == "dq":
        _exit_all(sb)
        for g in reversed(P.guards):
            g.__exit__(None, None, None)
        P.check()
        return nc

    sa = []
    winb = _enter(sa, nc.sbuf_tensor("winb", [128, 8 * 4096], BF16))
    wBb = _enter(sa, nc.sbuf_tensor("wBb", [128, 16 * 64], BF16))
    wCb = _enter(sa, nc.sbuf_tensor("wCb", [S64, C2], BF16))
    wob = _enter(sa, nc.sbuf_tensor("wob", [128, 2 * 1024], BF16))
    cvd = _enter(sa, nc.sbuf_tensor("cvd", [128, 2 * 4 * 128], BF16))
    x_t = _enter(sa, nc.sbuf_tensor("x_t", [128, NSUB * D], F32))
    a_sub = _enter(sa, nc.sbuf_tensor("a_sub", [128, D], F32))
    aT = _enter(sa, nc.sbuf_tensor("aT", [128, 8 * TT], BF16))
    xin = _enter(sa, nc.sbuf_tensor("xin", [128, 2 * 520], BF16))
    xtail = _enter(sa, nc.sbuf_tensor("xtail", [128, 16 * 4], BF16))
    u_bf = _enter(sa, nc.sbuf_tensor("u_bf", [128, 16 * TT], BF16))
    sxg = _enter(sa, nc.sbuf_tensor("sxg", [128, 16 * TT], BF16))
    y2t = _enter(sa, nc.sbuf_tensor("y2t", [128, 2 * TT], F32))
    hs0 = _enter(sa, nc.sbuf_tensor("hs0", [S64, TT], F32))
    hs1 = _enter(sa, nc.sbuf_tensor("hs1", [S64, TT], F32))
    hsb = _enter(sa, nc.sbuf_tensor("hsb", [S64, TT], BF16))
    bu_m = _enter(sa, nc.sbuf_tensor("bu_m", [S64, TT], F32))
    yo = _enter(sa, nc.sbuf_tensor("yo", [128, 8 * TT], F32))
    zb = _enter(sa, nc.sbuf_tensor("zb", [128, D], F32))
    nb = _enter(sa, nc.sbuf_tensor("nb", [128, D], F32))
    x1b = _enter(sa, nc.sbuf_tensor("x1b", [128, D], F32))
    wbuf = zb
    bsub = nb
    sdump = _enter(sa, nc.sbuf_tensor("sdump", [128, D], F32))
    bT_b = _enter(sa, nc.sbuf_tensor("bT_b", [128, 8 * TT], BF16))

    swl = P.sem("swl")
    sx = P.sem("sx")
    sfr = P.sem("sfr")
    sfs = P.sem("sfs")
    saT = P.sem("saT")
    saTe = P.sem("saTe")
    sxp = P.sem("sxp")
    sxe = P.sem("sxe")
    scv = P.sem("scv")
    sce = P.sem("sce")
    sbu = P.sem("sbu")
    ssc = P.sem("ssc")
    shb = P.sem("shb")
    swc = P.sem("swc")
    sy2 = P.sem("sy2")
    swo = P.sem("swo")
    swoe = P.sem("swoe")
    szt = P.sem("szt")
    sepi = P.sem("sepi")
    ssq = P.sem("ssq")
    sbt = P.sem("sbt")
    sbte = P.sem("sbte")
    sst1 = P.sem("sst1")
    sstb = P.sem("sstb")
    swos = P.sem("swos")
    scvs = P.sem("scvs")

    stw = []
    wstg3 = yo
    for dk in range(8):
        SY(lambda e, d=dk: e.dma_start(wstg3[:, :], winT[d * 128:(d + 1) * 128, :]),
           wait=[(sprep, sprep.n)], inc=swl.inc(16))
        V(lambda e, d=dk: e.tensor_copy(winb[:, d * 4096:(d + 1) * 4096], wstg3[:, :]),
          wait=[(swl, swl.n)], inc=sprep.inc())
    for ck in range(16):
        SY(lambda e, c=ck: e.dma_start(wstg3[:, 0:64], wBT[c * 128:(c + 1) * 128, :]),
           wait=[(sprep, sprep.n)], inc=swl.inc(16))
        V(lambda e, c=ck: e.tensor_scalar(wBb[:, c * 64:(c + 1) * 64], wstg3[:, 0:64],
                                          0.05, None, op0=A.mult),
          wait=[(swl, swl.n)], inc=sprep.inc())
    SY(lambda e: e.dma_start(wstg3[0:64, 0:C2], wCT), wait=[(sprep, sprep.n)],
       inc=swl.inc(16))
    V(lambda e: e.tensor_copy(wCb[:, :], wstg3[0:64, 0:C2]),
      wait=[(swl, swl.n)], inc=sprep.inc())
    V(lambda e: e.memset(xtail[:, :], 0.0), inc=sprep.inc())
    P.drain([swl, sprep])
    P.emit_block()

    for t in range(NT if nt_limit is None else min(NT, nt_limit)):
        warm = shard and (t == 0)
        ft = t - 1 if shard else t
        for s in range(NSUB):
            SY(lambda e, tt=t, ss=s: e.dma_start(
                x_t[:, ss * D:(ss + 1) * D],
                xp[tt * TT + ss * 128: tt * TT + (ss + 1) * 128, :]),
               wait=[(sepi, sepi.n), (sfr, sfr.n)] if t > 0 else [],
               inc=sx.inc(16))
        if dbg and t == (1 if shard else 0):
            G(lambda e: e.dma_start(dbg_x[:, :], x_t[:, :]),
              wait=[(sx, sx.n)], inc=P.sem("sdbg2").inc(16))
            G(lambda e: e.dma_start(dbg_w[:, :], winb[:, 0:4096]),
              inc=P.sem("sdbg2").inc(16))
        for s in range(NSUB):
            V(lambda e, ss=s: e.tensor_scalar(
                a_sub[:, :], x_t[:, ss * D:(ss + 1) * D], 1.0, 0.0,
                op0=A.mult, op1=A.add, accum_out=st["sx"][:, ss:ss + 1]),
              wait=[(sx, sx.n)], inc=sfs.inc())
            SC(lambda e, ss=s: e.activation(
                sdump[:, :], x_t[:, ss * D:(ss + 1) * D], AF.Square,
                accum_out=st["sq"][:, ss:ss + 1]),
               wait=[(sx, sx.n)], inc=ssq.inc())
        VS(lambda e: e.tensor_scalar(st["mu1"][:, :], st["sx"][:, :], 1.0 / D, None,
                                     op0=A.mult),
           wait=[(ssq, ssq.n), (sfs, sfs.n)])
        VS(lambda e: e.tensor_scalar(st["va"][:, :], st["sq"][:, :], 1.0 / D, EPS,
                                     op0=A.mult, op1=A.add))
        VS(lambda e: e.tensor_mul(st["vb"][:, :], st["mu1"][:, :], st["mu1"][:, :]))
        VS(lambda e: e.tensor_sub(st["va"][:, :], st["va"][:, :], st["vb"][:, :]),
           inc=sfs.inc())
        SC(lambda e: e.activation(st["vb"][:, :], st["va"][:, :], AF.Sqrt,
                                  bias=0.0, scale=1.0),
           wait=[(sfs, sfs.n)], inc=ssq.inc())
        VS(lambda e: e.reciprocal(st["r1"][:, :], st["vb"][:, :]),
           wait=[(ssq, ssq.n)], inc=sfs.inc())
        for s in range(NSUB):
            V(lambda e, ss=s: e.tensor_scalar(
                a_sub[:, :], x_t[:, ss * D:(ss + 1) * D],
                st["mu1"][:, ss:ss + 1], st["r1"][:, ss:ss + 1],
                op0=A.subtract, op1=A.mult),
              wait=[(ssq, ssq.n), (saTe, saTe.n), (svs, svs.n)],
              inc=sfr.inc())
            if dbg and t == (1 if shard else 0) and s == 0:
                G(lambda e: e.dma_start(dbg_a[:, :], a_sub[:, :]),
                  wait=[(sfr, sfr.n)], inc=P.sem("sdbg2").inc(16))
            pzt = pz0 if s % 2 == 0 else pz1
            for dk in range(8):
                T(lambda e, d=dk, p=pzt: e.transpose(
                    p[:, d * 128:(d + 1) * 128], a_sub[:, d * 128:(d + 1) * 128],
                    ident[:, :]),
                  wait=([(sfr, sfr.n), (saTe, max(0, saTe.n - 1)),
                         (sbte, sbte.n), (sepi, sepi.n)] if dk == 0 else []),
                  inc=(saT.inc() if dk == 7 else None))
            SC(lambda e, ss=s, p=pzt: e.activation(
                bass.AP(aT, ss * 128, [[8 * TT, 128], [TT, 8], [1, 128]]),
                p[:, :], AF.Identity),
               wait=[(saT, saT.n)], inc=saTe.inc())
        if (not warm and main_stop not in ("scan", "y2", "woutnost", "woutseq", "woutub", "woutmm", "woutev", "woutev2")) or main_stop == "stream":
            wout_base = swo.n
            for ck in range(16):
                G(lambda e, c=ck, buf=(ck % 2): e.dma_start(
                    wob[:, buf * 1024:(buf + 1) * 1024],
                    bass.AP(woutscr.tensor, c * 128 * D, [[D, 128], [1, D]])),
                  wait=([] if main_stop == "stream" else
                        [(swo, max(0, wout_base + ck - 1))]),
                  inc=swos.inc(16))
        conv_base = scv.n
        scvs_base = scvs.n
        for fk in range(16):
            SY(lambda e, c=fk, buf=(fk % 2): e.dma_start(
                cvd[:, buf * 512:(buf + 1) * 512],
                bass.AP(convdscr.tensor, c * 128 * 128,
                        [[128, 128], [16 * 128 * 128, 4], [1, 128]])),
               wait=[(scv, max(0, conv_base + fk - 1))], inc=scvs.inc(16))
        nchunks = 16 if warm else 32
        for fk in range(nchunks):
            pmt = pm0 if fk % 2 == 0 else pm1
            for dk in range(8):
                T(lambda e, d=dk, f=fk, p=pmt: e.matmul(
                    p[:, :], winb[:, d * 4096 + f * 128: d * 4096 + (f + 1) * 128],
                    aT[:, d * TT:(d + 1) * TT],
                    start=(d == 0), stop=(d == 7)),
                  wait=([(saTe, saTe.n), (sxe, max(0, sxe.n - 1))] if dk == 0 else []),
                  inc=(sxp.inc() if dk == 7 else None))
            if fk < 16:
                xb = fk % 2
                SC(lambda e, b=xb, p=pmt: e.activation(
                    xin[:, b * 520 + 4: b * 520 + 4 + TT], p[:, :], AF.Identity),
                   wait=[(sxp, sxp.n), (scv, max(0, scv.n - 1)), (sfr, sfr.n)],
                   inc=sxe.inc())
                if t in resets:
                    V(lambda e, b=xb: e.memset(xin[:, b * 520 + 1: b * 520 + 4], 0.0),
                      wait=[(sxe, sxe.n)], inc=sfr.inc())
                else:
                    V(lambda e, f=fk, b=xb: e.tensor_copy(
                        xin[:, b * 520 + 1: b * 520 + 4], xtail[:, f * 4: f * 4 + 3]),
                      wait=[(sxe, sxe.n)], inc=sfr.inc())
                V(lambda e, f=fk, b=xb: e.tensor_copy(
                    xtail[:, f * 4: f * 4 + 3],
                    xin[:, b * 520 + 4 + TT - 3: b * 520 + 4 + TT]),
                  inc=sfr.inc())
                for tap in range(4):
                    T(lambda e, b=xb, tp=tap: e.matmul(
                        pc0[:, :],
                        cvd[:, b * 512 + tp * 128: b * 512 + (tp + 1) * 128],
                        xin[:, b * 520 + 1 + tp: b * 520 + 1 + tp + TT],
                        start=(tp == 0), stop=(tp == 3)),
                      wait=([(sfr, sfr.n), (scvs, scvs_base + 16 * (fk + 1)),
                             (sce, sce.n), (sy2, sy2.n)] if tap == 0 else []),
                      inc=(scv.inc() if tap == 3 else None))
                SC(lambda e, f=fk: e.activation(
                    u_bf[:, f * TT:(f + 1) * TT], pc0[:, :], AF.Silu,
                    bias=cvb[:, f:f + 1], scale=1.0),
                   wait=[(scv, scv.n)], inc=sce.inc())
                T(lambda e, f=fk: e.matmul(
                    pbu[0:S64, :], wBb[:, f * 64:(f + 1) * 64],
                    u_bf[:, f * TT:(f + 1) * TT],
                    start=(f == 0), stop=(f == 15)),
                  wait=[(sce, sce.n)] + ([(ssc, ssc.n)] if fk == 0 else []),
                  inc=(sbu.inc() if fk == 15 else None))
            else:
                SC(lambda e, f=fk, p=pmt: e.activation(
                    sxg[:, (f - 16) * TT:(f - 15) * TT], p[:, :], AF.Silu),
                   wait=[(sxp, sxp.n)], inc=sxe.inc())
        hs_cur = hs0 if t % 2 == 0 else hs1
        hs_prev = hs1 if t % 2 == 0 else hs0
        if warm:
            V(lambda e: e.tensor_scalar(bu_m[:, :], pbu[0:S64, :], mask_sb[0:S64, 0:1],
                                        None, op0=A.mult),
              wait=[(sbu, sbu.n)])
            V(lambda e, h=hs_cur: e.tensor_tensor_scan(
                h[:, :], dec95[:, :], bu_m[:, :], zk64[0:S64, 0:1],
                op0=A.mult, op1=A.add),
              inc=ssc.inc())
            continue
        init = zk64[0:S64, 0:1] if (t in resets) else hs_prev[:, TT - 1:TT]
        V(lambda e, h=hs_cur, i=init: e.tensor_tensor_scan(
            h[:, :], dec95[:, :], pbu[0:S64, :], i, op0=A.mult, op1=A.add),
          wait=[(sbu, sbu.n)], inc=ssc.inc())
        V(lambda e, h=hs_cur: e.tensor_copy(hsb[:, :], h[:, :]), inc=shb.inc())
        if main_stop in ("scan", "stream"):
            continue
        for ck in range(16):
            pmt = pm0 if ck % 2 == 0 else pm1
            T(lambda e, c=ck, p=pmt: e.matmul(
                p[:, :], wCb[:, c * 128:(c + 1) * 128], hsb[:, :],
                start=True, stop=True),
              wait=([(shb, shb.n), (sxe, sxe.n)] if ck == 0 else
                    [(sy2, max(0, sy2.n - 1))]),
              inc=swc.inc())
            V(lambda e, c=ck, p=pmt, yb=(ck % 2): e.scalar_tensor_tensor(
                y2t[:, yb * TT:(yb + 1) * TT], u_bf[:, c * TT:(c + 1) * TT],
                dpar[:, c:c + 1], p[:, :], op0=A.mult, op1=A.add),
              wait=[(swc, swc.n)])
            V(lambda e, c=ck, yb=(ck % 2): e.tensor_mul(
                sxg[:, c * TT:(c + 1) * TT], y2t[:, yb * TT:(yb + 1) * TT],
                sxg[:, c * TT:(c + 1) * TT]),
              inc=sy2.inc())
        if main_stop == "y2":
            continue
        wout_ps = [pm0[:, :], pm1[:, :], pc0[:, :], pbu[:, :],
                   pz0[:, 0:512], pz0[:, 512:1024], pz1[:, 0:512], pz1[:, 512:1024]]
        NDK = 4 if main_stop in ("wout4", "woutnost") else 8
        if main_stop in ("woutub", "woutmm"):
            NDK = 2
        if main_stop in ("woutev", "woutev2"):
            NDK = 0
        _seq = (main_stop in ("woutseq", "woutub", "woutmm"))
        _iter = ([(c, d) for d in range(2) for c in range(16)] if _seq else
                 [(c, d) for c in range(16) for d in range(NDK)])
        for ck, dk in _iter:
            if True:
                _lhs = (winb if main_stop in ("woutnost", "woutseq", "woutub", "woutmm", "woutev") else wob)
                _lo = (0 if main_stop in ("woutnost", "woutseq", "woutub") else (ck % 2) * 1024)
                _rhs = (u_bf if main_stop in ("woutub", "woutmm") else sxg)
                T(lambda e, d=dk, c=ck, p=wout_ps[dk], L=_lhs, o=_lo, R=_rhs: e.matmul(
                    p, L[:, o + d * 128: o + (d + 1) * 128],
                    R[:, c * TT:(c + 1) * TT],
                    start=(c == 0), stop=(c == 15)),
                  wait=([(sy2, sy2.n)] +
                        ([] if main_stop in ("woutnost", "woutseq", "woutub", "woutmm", "woutev") else
                         [(swos, swos.n - 16 * (15 - ck))]) +
                        [(swoe, max(0, swoe.n - 8))] if dk == 0 else []),
                  inc=(swo.inc() if (dk == NDK - 1 and not _seq) or
                       (_seq and ck == 15) else None))
        NEV = 0 if main_stop == "woutmm" else (2 if (_seq or main_stop in ("woutev", "woutev2")) else NDK)
        _ydst = (sdump if main_stop == "woutev2" else yo)
        for dk in range(NEV):
            SC(lambda e, d=dk, p=wout_ps[dk], Y=_ydst: e.activation(
                (Y[:, 0:TT] if main_stop == "woutev2" else Y[:, d * TT:(d + 1) * TT]),
                p, AF.Identity),
               wait=[(swo, swo.n), (sy2, sy2.n)], inc=swoe.inc())
        if main_stop in ("wout", "wout4", "woutnost", "woutseq", "woutub", "woutmm", "woutev", "woutev2"):
            continue
        if dbg and t == (1 if shard else 0):
            G(lambda e: e.dma_start(dbg_u[:, :], u_bf[:, :]),
              wait=[(sce, sce.n)], inc=P.sem("sdbg2").inc(16))
            G(lambda e: e.dma_start(dbg_sxg[:, :], sxg[:, :]),
              wait=[(sy2, sy2.n)], inc=P.sem("sdbg2").inc(16))
            G(lambda e, h=hs_cur: e.dma_start(dbg_hs[:, :], h[:, :]),
              wait=[(ssc, ssc.n)], inc=P.sem("sdbg2").inc(16))
            G(lambda e: e.dma_start(dbg_yo[:, :], yo[:, :]),
              wait=[(swoe, swoe.n)], inc=P.sem("sdbg2").inc(16))
            G(lambda e: e.dma_start(dbg_aT[:, :], aT[:, :]),
              wait=[(saTe, saTe.n)], inc=P.sem("sdbg2").inc(16))
        for s in range(NSUB):
            pzt = pz0 if s % 2 == 0 else pz1
            for dk in range(8):
                T(lambda e, d=dk, ss=s, p=pzt: e.transpose(
                    p[:, d * 128:(d + 1) * 128],
                    yo[:, d * TT + ss * 128: d * TT + (ss + 1) * 128], ident[:, :]),
                  wait=([(swoe, swoe.n), (sepi, max(0, sepi.n - 6)),
                         (sbte, max(0, sbte.n - 1))] if dk == 0 else []),
                  inc=(szt.inc() if dk == 7 else None))
            V(lambda e, ss=s: e.tensor_scalar(
                a_sub[:, :], x_t[:, ss * D:(ss + 1) * D],
                st["mu1"][:, ss:ss + 1], st["r1"][:, ss:ss + 1],
                op0=A.subtract, op1=A.mult),
              wait=[(szt, szt.n)])
            V(lambda e, p=pzt, ss=s: e.scalar_tensor_tensor(
                zb[:, :], p[:, :], 1.0, a_sub[:, :], op0=A.mult, op1=A.add,
                accum_out=st["sx"][:, ss:ss + 1]),
              wait=[(ssq, ssq.n)], inc=sepi.inc())
            if dbg and t == (1 if shard else 0) and s == 0:
                G(lambda e: e.dma_start(dbg_z[:, :], zb[:, :]),
                  wait=[(sepi, sepi.n)], inc=P.sem("sdbg2").inc(16))
            SC(lambda e, ss=s: e.activation(sdump[:, :], zb[:, :], AF.Square,
                                            accum_out=st["sq"][:, ss:ss + 1]),
               wait=[(sepi, sepi.n)], inc=ssq.inc())
            VS(lambda e, ss=s: e.tensor_scalar(st["mu2"][:, ss:ss + 1],
                                               st["sx"][:, ss:ss + 1],
                                               1.0 / D, None, op0=A.mult),
               wait=[(ssq, ssq.n)])
            VS(lambda e, ss=s: e.tensor_scalar(st["va"][:, ss:ss + 1],
                                               st["sq"][:, ss:ss + 1],
                                               1.0 / D, EPS, op0=A.mult, op1=A.add))
            VS(lambda e, ss=s: e.tensor_mul(st["vb"][:, ss:ss + 1],
                                            st["mu2"][:, ss:ss + 1],
                                            st["mu2"][:, ss:ss + 1]))
            VS(lambda e, ss=s: e.tensor_sub(st["va"][:, ss:ss + 1],
                                            st["va"][:, ss:ss + 1],
                                            st["vb"][:, ss:ss + 1]), inc=sepi.inc())
            SC(lambda e: e.activation(st["vb"][:, :], st["va"][:, :],
                                      AF.Sqrt, bias=0.0, scale=1.0),
               wait=[(sepi, sepi.n)], inc=ssq.inc())
            VS(lambda e: e.reciprocal(st["r2"][:, :], st["vb"][:, :]),
               wait=[(ssq, ssq.n)])
            if main_stop == "epi1":
                continue
            V(lambda e, ss=s: e.tensor_scalar(nb[:, :], zb[:, :],
                                              st["mu2"][:, ss:ss + 1],
                                              st["r2"][:, ss:ss + 1],
                                              op0=A.subtract, op1=A.mult),
              wait=[(ssq, ssq.n), (svs, svs.n)])
            V(lambda e: e.tensor_mul(nb[:, :], nb[:, :], gbs[:, :]))
            V(lambda e, ss=s: e.scalar_tensor_tensor(
                wbuf[:, :], nb[:, :], 1.0, bbs[:, :], op0=A.mult, op1=A.add,
                accum_out=st["sx"][:, ss:ss + 1]),
              inc=sepi.inc())
            SC(lambda e, ss=s: e.activation(sdump[:, :], wbuf[:, :], AF.Square,
                                            accum_out=st["sq"][:, ss:ss + 1]),
               wait=[(sepi, sepi.n)], inc=ssq.inc())
            VS(lambda e, ss=s: e.tensor_scalar(st["mu3"][:, ss:ss + 1],
                                               st["sx"][:, ss:ss + 1],
                                               1.0 / D, None, op0=A.mult),
               wait=[(ssq, ssq.n)])
            VS(lambda e, ss=s: e.tensor_scalar(st["va"][:, ss:ss + 1],
                                               st["sq"][:, ss:ss + 1],
                                               1.0 / D, EPS, op0=A.mult, op1=A.add))
            VS(lambda e, ss=s: e.tensor_mul(st["vb"][:, ss:ss + 1],
                                            st["mu3"][:, ss:ss + 1],
                                            st["mu3"][:, ss:ss + 1]))
            VS(lambda e, ss=s: e.tensor_sub(st["va"][:, ss:ss + 1],
                                            st["va"][:, ss:ss + 1],
                                            st["vb"][:, ss:ss + 1]), inc=sepi.inc())
            SC(lambda e: e.activation(st["vb"][:, :], st["va"][:, :],
                                      AF.Sqrt, bias=0.0, scale=1.0),
               wait=[(sepi, sepi.n)], inc=ssq.inc())
            VS(lambda e: e.reciprocal(st["r3"][:, :], st["vb"][:, :]),
               wait=[(ssq, ssq.n)])
            V(lambda e, ss=s: e.tensor_scalar(nb[:, :], wbuf[:, :],
                                              st["mu3"][:, ss:ss + 1],
                                              st["r3"][:, ss:ss + 1],
                                              op0=A.subtract, op1=A.mult),
              wait=[(ssq, ssq.n), (svs, svs.n)])
            V(lambda e, ss=s: e.scalar_tensor_tensor(
                x1b[:, :], nb[:, :], 1.0, x_t[:, ss * D:(ss + 1) * D],
                op0=A.mult, op1=A.add, accum_out=st["sx"][:, ss:ss + 1]),
              wait=[(sst1, sst1.n)], inc=sepi.inc())
            SC(lambda e, ss=s: e.activation(sdump[:, :], x1b[:, :], AF.Square,
                                            accum_out=st["sq"][:, ss:ss + 1]),
               wait=[(sepi, sepi.n)], inc=ssq.inc())
            G(lambda e, tt=ft, ss=s: e.dma_start(
                x1scr[tt * TT + ss * 128: tt * TT + (ss + 1) * 128, :], x1b[:, :]),
              wait=[(ssq, ssq.n)], inc=sst1.inc(16))
            VS(lambda e, ss=s: e.tensor_scalar(st["mu2"][:, ss:ss + 1],
                                               st["sx"][:, ss:ss + 1],
                                               1.0 / D, None, op0=A.mult),
               wait=[(ssq, ssq.n)])
            VS(lambda e, ss=s: e.tensor_scalar(st["va"][:, ss:ss + 1],
                                               st["sq"][:, ss:ss + 1],
                                               1.0 / D, EPS, op0=A.mult, op1=A.add))
            VS(lambda e, ss=s: e.tensor_mul(st["vb"][:, ss:ss + 1],
                                            st["mu2"][:, ss:ss + 1],
                                            st["mu2"][:, ss:ss + 1]))
            VS(lambda e, ss=s: e.tensor_sub(st["va"][:, ss:ss + 1],
                                            st["va"][:, ss:ss + 1],
                                            st["vb"][:, ss:ss + 1]), inc=sepi.inc())
            SC(lambda e: e.activation(st["vb"][:, :], st["va"][:, :],
                                      AF.Sqrt, bias=0.0, scale=1.0),
               wait=[(sepi, sepi.n)], inc=ssq.inc())
            VS(lambda e: e.reciprocal(st["r2"][:, :], st["vb"][:, :]),
               wait=[(ssq, ssq.n)])
            if main_stop == "epi2":
                continue
            V(lambda e, ss=s: e.tensor_scalar(bsub[:, :], x1b[:, :],
                                              st["mu2"][:, ss:ss + 1],
                                              st["r2"][:, ss:ss + 1],
                                              op0=A.subtract, op1=A.mult),
              wait=[(ssq, ssq.n), (sbte, sbte.n), (svs, svs.n)], inc=sepi.inc())
            for dk in range(8):
                T(lambda e, d=dk, p=pzt: e.transpose(
                    p[:, d * 128:(d + 1) * 128], bsub[:, d * 128:(d + 1) * 128],
                    ident[:, :]),
                  wait=([(sepi, sepi.n)] if dk == 0 else []),
                  inc=(sbt.inc() if dk == 7 else None))
            SC(lambda e, ss=s, p=pzt: e.activation(
                bass.AP(bT_b, ss * 128, [[8 * TT, 128], [TT, 8], [1, 128]]),
                p[:, :], AF.Identity),
               wait=[(sbt, sbt.n), (sstb, sstb.n)], inc=sbte.inc())
            first_b = (t == (1 if shard else 0) and s == 0)
            VS(lambda e: e.tensor_reduce(
                st["vb"][:, 0:1], bsub[:, :], axis=AX, op=A.max,
                apply_absolute_value=True),
               wait=[(sepi, sepi.n)])
            if first_b:
                VS(lambda e: e.tensor_copy(bmax[:, 0:1], st["vb"][:, 0:1]),
                   inc=sepi.inc())
            else:
                VS(lambda e: e.tensor_max(bmax[:, 0:1], bmax[:, 0:1], st["vb"][:, 0:1]),
                   inc=sepi.inc())
        if main_stop in ("epi1", "epi2"):
            continue
        G(lambda e, tt=ft: e.dma_start(bTscr[tt, :, :], bT_b[:, :]),
          wait=[(sbte, sbte.n)], inc=sstb.inc(16))
    P.drain([sst1, sstb, sx, swos, scvs, svs])
    P.emit_block()
    _exit_all(sa)
    if stop_after == "attn":
        _exit_all(sb)
        for g in reversed(P.guards):
            g.__exit__(None, None, None)
        P.check()
        return nc

    scc = P.sem("scc")
    sccd = P.sem("sccd")

    def global_max(pairs):
        n = len(pairs)
        for i, (src_, dst) in enumerate(pairs):
            G(lambda e, s=src_, i=i: e.dma_start(
                bass.AP(foldd.tensor, i * 128, [[1, 128], [1, 1]]), s[:, 0:1]),
              wait=[(scc, scc.n), (sccd, sccd.n)], inc=sccd.inc(16))
        VS(lambda e: e.memset(locb[0:1, 0:16], 0.0), wait=[(sccd, sccd.n)],
           inc=scc.inc())
        for i in range(n):
            G(lambda e, i=i: e.dma_start(
                rowb[0:1, :], bass.AP(foldd.tensor, i * 128, [[128, 1], [1, 128]])),
              wait=[(sccd, sccd.n)], inc=sccd.inc(16))
            VS(lambda e, i=i: e.tensor_reduce(locb[0:1, i:i + 1], rowb[0:1, :],
                                              axis=AX, op=A.max),
               wait=[(sccd, sccd.n), (scc, scc.n)], inc=scc.inc())
        G(lambda e: e.dma_start(cc_in[0:1, 0:16], locb[0:1, 0:16]),
          wait=[(scc, scc.n)], inc=sccd.inc(16))
        if cc:
            G(lambda e: e.collective_compute(
                "AllReduce", A.max, replica_groups=[list(range(8))],
                ins=[cc_in.tensor.ap().opt()], outs=[cc_out.tensor.ap().opt()]),
              wait=[(sccd, sccd.n)], inc=scc.inc())
        else:
            G(lambda e: e.dma_start(cc_out[0:1, 0:16], cc_in[0:1, 0:16]),
              wait=[(sccd, sccd.n)], inc=sccd.inc(16))
        for i, (src_, dst) in enumerate(pairs):
            G(lambda e, i=i, d=dst: e.dma_start(
                d[:, 0:1], bass.AP(cc_out.tensor, i, [[0, 128], [1, 1]])),
              wait=[(scc, scc.n), (sccd, sccd.n)], inc=sccd.inc(16))
        for i, (src_, dst) in enumerate(pairs):
            VS(lambda e, d=dst: e.tensor_copy(
                d[:, 0:4], bass.AP(d, 0, [[4, 128], [0, 4]])),
               wait=[(sccd, sccd.n)], inc=scc.inc())

    def mix1(dst, src):
        VS(lambda e, d=dst, s=src: e.tensor_scalar(d[:, :], s[:, :], 1.0, None,
                                                   op0=A.subtract))
        VS(lambda e, d=dst: e.tensor_mul(d[:, :], d[:, :], ks[:, :]))
        VS(lambda e, d=dst: e.tensor_scalar(d[:, :], d[:, :], 1.0, None, op0=A.add))

    global_max([(bmax, scl["m_b"])])
    VS(lambda e: e.tensor_scalar(scl["s0"][:, :], scl["m_b"][:, :], r327[:, 0:1],
                                 1e-12, op0=A.mult, op1=A.max),
       wait=[(scc, scc.n), (sccd, sccd.n)])
    VS(lambda e: e.reciprocal(scl["r0"][:, :], scl["s0"][:, :]))
    mix1(scl["rK0"], scl["r0"])
    V(lambda e: e.nop() if False else e.tensor_copy(scl["tmpa"][:, :], scl["rK0"][:, :]),
      inc=scc.inc())
    P.emit_block()
    if stop_after == "b1":
        _exit_all(sb)
        for g in reversed(P.guards):
            g.__exit__(None, None, None)
        P.check()
        return nc

    sfa = []
    wupb = _enter(sfa, nc.sbuf_tensor("wupb", [128, 8 * I3], BF16))
    wgb = _enter(sfa, nc.sbuf_tensor("wgb", [128, 8 * I3], BF16))
    bTl = _enter(sfa, nc.sbuf_tensor("bTl", [128, 8 * TT], BF16))
    bTl2 = _enter(sfa, nc.sbuf_tensor("bTl2", [128, 8 * TT], BF16))
    xq = _enter(sfa, nc.sbuf_tensor("xq", [128, 8 * TT], BF16))
    xqf = _enter(sfa, nc.sbuf_tensor("xqf", [128, 8 * TT], F32))
    upe = _enter(sfa, nc.sbuf_tensor("upe", [128, 24 * TT], BF16))
    gte = _enter(sfa, nc.sbuf_tensor("gte", [128, 24 * TT], BF16))

    sfw = P.sem("sfw")
    sbl = P.sem("sbl")
    sqd = P.sem("sqd")
    sum_ = P.sem("sum_")
    sue = P.sem("sue")
    sgm = P.sem("sgm")
    sge = P.sem("sge")
    sus = P.sem("sus")
    sgs = P.sem("sgs")
    smx = P.sem("smx")

    for hk in range(8):
        SY(lambda e, h=hk: e.dma_start(wupb[:, h * I3:(h + 1) * I3], wupdq[h, :, :]),
           inc=sfw.inc(16))
        SY(lambda e, h=hk: e.dma_start(wgb[:, h * I3:(h + 1) * I3], wgdq[h, :, :]),
           inc=sfw.inc(16))
    for t in range(NFT):
        buf = bTl if t % 2 == 0 else bTl2
        SY(lambda e, tt=t, b=buf: e.dma_start(b[:, :], bTscr[tt, :, :]),
           wait=([(sstb, sstb.n)] if t == 0 else []) +
                ([(sqd, max(0, sqd.n - 1))] if t >= 2 else []),
           inc=sbl.inc(16))
        V(lambda e, b=buf: e.tensor_scalar(xqf[:, :], b[:, :], scl["rK0"][:, 0:1],
                                           None, op0=A.mult),
          wait=[(sbl, sbl.n), (scc, scc.n), (smx, max(0, smx.n - 24))])
        V(lambda e: e.tensor_scalar(xq[:, :], xqf[:, :], ckv[:, 0:1], ckv[:, 0:1],
                                    op0=A.add, op1=A.subtract),
          inc=sqd.inc())
        for ik in range(24):
            pmt = pm0 if ik % 2 == 0 else pm1
            pgt = pc0 if ik % 2 == 0 else pz0
            for hk in range(8):
                T(lambda e, i=ik, h=hk, p=pmt: e.matmul(
                    p[:, :], wupb[:, h * I3 + i * 128: h * I3 + (i + 1) * 128],
                    xq[:, h * TT:(h + 1) * TT], start=(h == 0), stop=(h == 7)),
                  wait=([(sqd, sqd.n), (sfw, sfw.n),
                         (sue, max(0, sue.n - 1))] if hk == 0 else []),
                  inc=(sum_.inc() if hk == 7 else None))
            SC(lambda e, i=ik, p=pmt: e.activation(
                upe[:, i * TT:(i + 1) * TT], p[:, :], AF.Identity),
               wait=[(sum_, sum_.n), (sus, sus.n)],
               inc=sue.inc())
            for hk in range(8):
                T(lambda e, i=ik, h=hk, p=pgt: e.matmul(
                    p[:, 0:512], wgb[:, h * I3 + i * 128: h * I3 + (i + 1) * 128],
                    xq[:, h * TT:(h + 1) * TT], start=(h == 0), stop=(h == 7)),
                  wait=([(sge, max(0, sge.n - 1))] if hk == 0 else []),
                  inc=(sgm.inc() if hk == 7 else None))
            SC(lambda e, i=ik, p=pgt: e.activation(
                gte[:, i * TT:(i + 1) * TT], p[:, 0:512], AF.Identity),
               wait=[(sgm, sgm.n), (sgs, sgs.n)],
               inc=sge.inc())
            VS(lambda e, i=ik: e.tensor_reduce(scl["tmpb"][:, 0:1],
                                               upe[:, i * TT:(i + 1) * TT],
                                               axis=AX, op=A.max,
                                               apply_absolute_value=True),
               wait=[(sue, sue.n)])
            if t == 0 and ik == 0:
                VS(lambda e: e.tensor_copy(fmax["up"][:, 0:1], scl["tmpb"][:, 0:1]))
            else:
                VS(lambda e: e.tensor_max(fmax["up"][:, 0:1], fmax["up"][:, 0:1],
                                          scl["tmpb"][:, 0:1]))
            VS(lambda e, i=ik: e.tensor_reduce(scl["tmpc"][:, 0:1],
                                               gte[:, i * TT:(i + 1) * TT],
                                               axis=AX, op=A.max,
                                               apply_absolute_value=True),
               wait=[(sge, sge.n)])
            if t == 0 and ik == 0:
                VS(lambda e: e.tensor_copy(fmax["g"][:, 0:1], scl["tmpc"][:, 0:1]),
                   inc=smx.inc())
            else:
                VS(lambda e: e.tensor_max(fmax["g"][:, 0:1], fmax["g"][:, 0:1],
                                          scl["tmpc"][:, 0:1]),
                   inc=smx.inc())
            if ik == 23:
                G(lambda e, tt=t: e.dma_start(upscr[tt, :, :], upe[:, :]),
                  wait=[(smx, smx.n)], inc=sus.inc(16))
                G(lambda e, tt=t: e.dma_start(gscr[tt, :, :], gte[:, :]),
                  inc=sgs.inc(16))
    P.drain([sus, sgs, svs])
    P.emit_block()
    _exit_all(sfa)
    if stop_after == "fa":
        _exit_all(sb)
        for g in reversed(P.guards):
            g.__exit__(None, None, None)
        P.check()
        return nc

    global_max([(fmax["up"], scl["m_up"]), (fmax["g"], scl["m_g"])])
    VS(lambda e: e.tensor_mul(scl["tmpa"][:, :], scl["m_up"][:, :], scl["s0"][:, :]),
       wait=[(scc, scc.n), (sccd, sccd.n)])
    VS(lambda e: e.tensor_scalar(scl["s_up"][:, :], scl["tmpa"][:, :], r327[:, 0:1],
                                 1e-12, op0=A.mult, op1=A.max))
    VS(lambda e: e.reciprocal(scl["r_up"][:, :], scl["s_up"][:, :]))
    VS(lambda e: e.tensor_mul(scl["tmpa"][:, :], scl["s0"][:, :], scl["r_up"][:, :]))
    mix1(scl["cU"], scl["tmpa"])
    VS(lambda e: e.tensor_mul(scl["tmpa"][:, :], scl["m_g"][:, :], scl["s0"][:, :]))
    VS(lambda e: e.tensor_scalar(scl["s_g"][:, :], scl["tmpa"][:, :], r327[:, 0:1],
                                 1e-12, op0=A.mult, op1=A.max))
    VS(lambda e: e.reciprocal(scl["r_g"][:, :], scl["s_g"][:, :]))
    VS(lambda e: e.tensor_mul(scl["tmpa"][:, :], scl["s0"][:, :], scl["r_g"][:, :]))
    mix1(scl["cG"], scl["tmpa"])
    VS(lambda e: e.tensor_mul(scl["tmpa"][:, :], scl["s_g"][:, :], ks[:, :]))
    VS(lambda e: e.tensor_scalar(scl["tmpb"][:, :], ks[:, :], -1.0, 1.0,
                                 op0=A.mult, op1=A.add))
    VS(lambda e: e.tensor_add(scl["sgmix"][:, :], scl["tmpa"][:, :], scl["tmpb"][:, :]))
    VS(lambda e: e.tensor_scalar(scl["tmpa"][:, :], ks[:, :], 65536.0, None, op0=A.mult))
    VS(lambda e: e.tensor_add(scl["c65"][:, :], scl["tmpa"][:, :], scl["tmpb"][:, :]),
       inc=scc.inc())
    P.emit_block()

    sb1 = []
    gl0 = _enter(sb1, nc.sbuf_tensor("gl0", [128, 24 * TT], BF16))
    qgb = _enter(sb1, nc.sbuf_tensor("qgb", [128, 24 * TT], BF16))
    sgf = _enter(sb1, nc.sbuf_tensor("sgf", [128, 12 * TT], F32))
    qsb = _enter(sb1, nc.sbuf_tensor("qsb", [128, 24 * TT], BF16))
    pb = _enter(sb1, nc.sbuf_tensor("pb", [128, 24 * TT], BF16))

    sgl = P.sem("sgl")
    sq1 = P.sem("sq1")
    ssg = P.sem("ssg")
    sq2 = P.sem("sq2")
    spp = P.sem("spp")
    sps = P.sem("sps")

    for t in range(NFT):
        SY(lambda e, tt=t: e.dma_start(gl0[:, :], gscr[tt, :, :]),
           wait=([(sgs, sgs.n)] if t == 0 else []) +
                ([(sps, sps.n)] if t >= 1 else []),
           inc=sgl.inc(16))
        for half in range(2):
            off = half * 12 * TT
            z = 12 * TT
            V(lambda e, o=off, zz=z: e.tensor_scalar(
                sgf[:, 0:zz], gl0[:, o:o + zz], scl["cG"][:, 0:1], None, op0=A.mult),
              wait=[(sgl, sgl.n), (scc, scc.n)] +
                   ([(spp, max(0, spp.n - 1))] if spp.n >= 1 else []))
            V(lambda e, o=off, zz=z: e.tensor_scalar(
                qgb[:, o:o + zz], sgf[:, 0:zz], ckv[:, 0:1], ckv[:, 0:1],
                op0=A.add, op1=A.subtract),
              inc=sq1.inc())
            SC(lambda e, o=off, zz=z: e.activation(
                sgf[:, 0:zz], qgb[:, o:o + zz], AF.Sigmoid, bias=0.0,
                scale=scl["sgmix"][:, 0:1]),
               wait=[(sq1, sq1.n)], inc=ssg.inc())
            V(lambda e, o=off, zz=z: e.tensor_scalar(
                sgf[:, 0:zz], sgf[:, 0:zz], scl["c65"][:, 0:1], None, op0=A.mult),
              wait=[(ssg, ssg.n)])
            V(lambda e, o=off, zz=z: e.tensor_scalar(
                qsb[:, o:o + zz], sgf[:, 0:zz], ckv[:, 0:1], ckv[:, 0:1],
                op0=A.add, op1=A.subtract))
            V(lambda e, o=off, zz=z: e.tensor_mul(
                pb[:, o:o + zz], qgb[:, o:o + zz], qsb[:, o:o + zz]),
              inc=sq2.inc())
            VS(lambda e, o=off, zz=z: e.tensor_reduce(
                scl["tmpb"][:, 0:1], pb[:, o:o + zz], axis=AX, op=A.max,
                apply_absolute_value=True),
               wait=[(sq2, sq2.n)])
            if t == 0 and half == 0:
                VS(lambda e: e.tensor_copy(fmax["p"][:, 0:1], scl["tmpb"][:, 0:1]),
                   inc=spp.inc())
            else:
                VS(lambda e: e.tensor_max(fmax["p"][:, 0:1], fmax["p"][:, 0:1],
                                          scl["tmpb"][:, 0:1]),
                   inc=spp.inc())
        G(lambda e, tt=t: e.dma_start(pscr[tt, :, :], pb[:, :]),
          wait=[(spp, spp.n)], inc=sps.inc(16))
    P.drain([sps, svs])
    P.emit_block()
    _exit_all(sb1)
    if stop_after == "fb1":
        _exit_all(sb)
        for g in reversed(P.guards):
            g.__exit__(None, None, None)
        P.check()
        return nc

    global_max([(fmax["p"], scl["m_p"])])
    VS(lambda e: e.tensor_mul(scl["tmpa"][:, :], scl["m_p"][:, :], scl["s_g"][:, :]),
       wait=[(scc, scc.n), (sccd, sccd.n)])
    VS(lambda e: e.tensor_scalar(scl["tmpa"][:, :], scl["tmpa"][:, :], 1.0 / 65536.0,
                                 None, op0=A.mult))
    VS(lambda e: e.tensor_scalar(scl["s_g2"][:, :], scl["tmpa"][:, :], r327[:, 0:1],
                                 1e-12, op0=A.mult, op1=A.max))
    VS(lambda e: e.reciprocal(scl["r_g2"][:, :], scl["s_g2"][:, :]))
    VS(lambda e: e.tensor_scalar(scl["tmpa"][:, :], scl["s_g"][:, :], 1.0 / 65536.0,
                                 None, op0=A.mult))
    VS(lambda e: e.tensor_mul(scl["tmpa"][:, :], scl["tmpa"][:, :], scl["r_g2"][:, :]))
    mix1(scl["c2m"], scl["tmpa"])
    VS(lambda e: e.tensor_copy(scl["tmpb"][:, :], scl["c2m"][:, :]), inc=scc.inc())
    P.emit_block()

    sb2 = []
    pl = _enter(sb2, nc.sbuf_tensor("pl", [128, 24 * TT], BF16))
    ul = _enter(sb2, nc.sbuf_tensor("ul", [128, 24 * TT], BF16))
    g2b = _enter(sb2, nc.sbuf_tensor("g2b", [128, 24 * TT], BF16))
    uqb = _enter(sb2, nc.sbuf_tensor("uqb", [128, 24 * TT], BF16))
    opb = _enter(sb2, nc.sbuf_tensor("opb", [128, 24 * TT], BF16))
    tf2 = _enter(sb2, nc.sbuf_tensor("tf2", [128, 12 * TT], F32))

    spl = P.sem("spl")
    sq3 = P.sem("sq3")
    sop = P.sem("sop")
    sos = P.sem("sos")

    for t in range(NFT):
        SY(lambda e, tt=t: e.dma_start(pl[:, :], pscr[tt, :, :]),
           wait=([(sps, sps.n), (sus, sus.n)] if t == 0 else []) +
                ([(sos, sos.n)] if t >= 1 else []),
           inc=spl.inc(16))
        SY(lambda e, tt=t: e.dma_start(ul[:, :], upscr[tt, :, :]),
           inc=spl.inc(16))
        for half in range(2):
            off = half * 12 * TT
            z = 12 * TT
            V(lambda e, o=off, zz=z: e.tensor_scalar(
                tf2[:, 0:zz], pl[:, o:o + zz], scl["c2m"][:, 0:1], None, op0=A.mult),
              wait=[(spl, spl.n), (scc, scc.n)] +
                   ([(sop, max(0, sop.n - 1))] if sop.n >= 1 else []))
            V(lambda e, o=off, zz=z: e.tensor_scalar(
                g2b[:, o:o + zz], tf2[:, 0:zz], ckv[:, 0:1], ckv[:, 0:1],
                op0=A.add, op1=A.subtract))
            V(lambda e, o=off, zz=z: e.tensor_scalar(
                tf2[:, 0:zz], ul[:, o:o + zz], scl["cU"][:, 0:1], None, op0=A.mult))
            V(lambda e, o=off, zz=z: e.tensor_scalar(
                uqb[:, o:o + zz], tf2[:, 0:zz], ckv[:, 0:1], ckv[:, 0:1],
                op0=A.add, op1=A.subtract))
            V(lambda e, o=off, zz=z: e.tensor_mul(
                opb[:, o:o + zz], g2b[:, o:o + zz], uqb[:, o:o + zz]),
              inc=sq3.inc())
            VS(lambda e, o=off, zz=z: e.tensor_reduce(
                scl["tmpb"][:, 0:1], opb[:, o:o + zz], axis=AX, op=A.max,
                apply_absolute_value=True),
               wait=[(sq3, sq3.n)])
            if t == 0 and half == 0:
                VS(lambda e: e.tensor_copy(fmax["op"][:, 0:1], scl["tmpb"][:, 0:1]),
                   inc=sop.inc())
            else:
                VS(lambda e: e.tensor_max(fmax["op"][:, 0:1], fmax["op"][:, 0:1],
                                          scl["tmpb"][:, 0:1]),
                   inc=sop.inc())
        G(lambda e, tt=t: e.dma_start(opscr[tt, :, :], opb[:, :]),
          wait=[(sop, sop.n)], inc=sos.inc(16))
    P.drain([sos, svs])
    P.emit_block()
    _exit_all(sb2)
    if stop_after == "fb2":
        _exit_all(sb)
        for g in reversed(P.guards):
            g.__exit__(None, None, None)
        P.check()
        return nc

    global_max([(fmax["op"], scl["m_op"])])
    VS(lambda e: e.tensor_mul(scl["tmpc"][:, :], scl["s_g2"][:, :], scl["s_up"][:, :]),
       wait=[(scc, scc.n), (sccd, sccd.n)])
    VS(lambda e: e.tensor_mul(scl["tmpb"][:, :], scl["tmpc"][:, :], scl["m_op"][:, :]))
    VS(lambda e: e.tensor_scalar(scl["s_o"][:, :], scl["tmpb"][:, :], r327[:, 0:1],
                                 1e-12, op0=A.mult, op1=A.max))
    VS(lambda e: e.reciprocal(scl["r_o"][:, :], scl["s_o"][:, :]))
    VS(lambda e: e.tensor_mul(scl["tmpa"][:, :], scl["tmpc"][:, :], scl["r_o"][:, :]))
    mix1(scl["c3m"], scl["tmpa"])
    VS(lambda e: e.tensor_mul(scl["tmpa"][:, :], scl["s_o"][:, :], ks[:, :]))
    VS(lambda e: e.tensor_scalar(scl["tmpb"][:, :], ks[:, :], -1.0, 1.0,
                                 op0=A.mult, op1=A.add))
    VS(lambda e: e.tensor_add(scl["cout"][:, :], scl["tmpa"][:, :], scl["tmpb"][:, :]),
       inc=scc.inc())
    P.emit_block()

    sb3 = []
    wdb = _enter(sb3, nc.sbuf_tensor("wdb", [128, 24 * D], BF16))
    opl = _enter(sb3, nc.sbuf_tensor("opl", [128, 24 * TT], BF16))
    oqb = _enter(sb3, nc.sbuf_tensor("oqb", [128, 24 * TT], BF16))
    tf3 = _enter(sb3, nc.sbuf_tensor("tf3", [128, 12 * TT], F32))
    yf = _enter(sb3, nc.sbuf_tensor("yf", [128, 8 * TT], F32))
    zf = _enter(sb3, nc.sbuf_tensor("zf", [128, D], F32))
    n3 = _enter(sb3, nc.sbuf_tensor("n3", [128, D], F32))
    x1l = _enter(sb3, nc.sbuf_tensor("x1l", [128, 2 * D], F32))
    x2b = _enter(sb3, nc.sbuf_tensor("x2b", [128, D], F32))
    sdump2 = _enter(sb3, nc.sbuf_tensor("sdump2", [128, D], F32))

    sdw = P.sem("sdw")
    sol = P.sem("sol")
    sq4 = P.sem("sq4")
    sdm = P.sem("sdm")
    sde = P.sem("sde")
    szt2 = P.sem("szt2")
    sep2 = P.sem("sep2")
    ssq2 = P.sem("ssq2")
    sxl = P.sem("sxl")
    sout = P.sem("sout")

    for ik in range(24):
        SY(lambda e, i=ik: e.dma_start(wdb[:, i * D:(i + 1) * D], wddq[i, :, :]),
           inc=sdw.inc(16))
    for t in range(NFT):
        SY(lambda e, tt=t: e.dma_start(opl[:, :], opscr[tt, :, :]),
           wait=([(sos, sos.n)] if t == 0 else []) +
                ([(sq4, sq4.n)] if t >= 1 else []),
           inc=sol.inc(16))
        for half in range(2):
            off = half * 12 * TT
            z = 12 * TT
            V(lambda e, o=off, zz=z: e.tensor_scalar(
                tf3[:, 0:zz], opl[:, o:o + zz], scl["c3m"][:, 0:1], None, op0=A.mult),
              wait=[(sol, sol.n), (scc, scc.n)] +
                   ([(sdm, max(0, sdm.n - 1))] if sdm.n >= 1 else []))
            V(lambda e, o=off, zz=z: e.tensor_scalar(
                oqb[:, o:o + zz], tf3[:, 0:zz], ckv[:, 0:1], ckv[:, 0:1],
                op0=A.add, op1=A.subtract),
              inc=sq4.inc())
        for dk in range(8):
            pmt = pm0 if dk % 2 == 0 else pm1
            for ik in range(24):
                T(lambda e, d=dk, i=ik, p=pmt: e.matmul(
                    p[:, :], wdb[:, i * D + d * 128: i * D + (d + 1) * 128],
                    oqb[:, i * TT:(i + 1) * TT], start=(i == 0), stop=(i == 23)),
                  wait=([(sq4, sq4.n), (sdw, sdw.n),
                         (sde, max(0, sde.n - 1))] if ik == 0 else []),
                  inc=(sdm.inc() if ik == 23 else None))
            SC(lambda e, d=dk, p=pmt: e.activation(
                yf[:, d * TT:(d + 1) * TT], p[:, :], AF.Identity, bias=0.0,
                scale=scl["cout"][:, 0:1]),
               wait=[(sdm, sdm.n)], inc=sde.inc())
        for s in range(NSUB):
            pzt = pz0 if s % 2 == 0 else pz1
            xb = s % 2
            SY(lambda e, tt=t, ss=s, b=xb: e.dma_start(
                x1l[:, b * D:(b + 1) * D],
                x1scr[tt * TT + ss * 128: tt * TT + (ss + 1) * 128, :]),
               wait=([(sst1, sst1.n)] if (t == 0 and s == 0) else []) +
                    ([(sep2, max(0, sep2.n - 3))] if sep2.n >= 3 else []),
               inc=sxl.inc(16))
            for dk in range(8):
                T(lambda e, d=dk, ss=s, p=pzt: e.transpose(
                    p[:, d * 128:(d + 1) * 128],
                    yf[:, d * TT + ss * 128: d * TT + (ss + 1) * 128], ident[:, :]),
                  wait=([(sde, sde.n), (sep2, max(0, sep2.n - 5))] if dk == 0 else []),
                  inc=(szt2.inc() if dk == 7 else None))
            V(lambda e, ss=s, p=pzt: e.tensor_scalar(
                zf[:, :], p[:, :], 1.0, 0.0, op0=A.mult, op1=A.add,
                accum_out=st["sx"][:, ss:ss + 1]),
              wait=[(szt2, szt2.n), (ssq2, ssq2.n)], inc=sep2.inc())
            SC(lambda e, ss=s: e.activation(sdump2[:, :], zf[:, :], AF.Square,
                                            accum_out=st["sq"][:, ss:ss + 1]),
               wait=[(sep2, sep2.n)], inc=ssq2.inc())
            VS(lambda e, ss=s: e.tensor_scalar(st["mu1"][:, ss:ss + 1],
                                              st["sx"][:, ss:ss + 1],
                                              1.0 / D, None, op0=A.mult),
               wait=[(ssq2, ssq2.n)])
            VS(lambda e, ss=s: e.tensor_scalar(st["va"][:, ss:ss + 1],
                                              st["sq"][:, ss:ss + 1],
                                              1.0 / D, EPS, op0=A.mult, op1=A.add))
            VS(lambda e, ss=s: e.tensor_mul(st["vb"][:, ss:ss + 1],
                                           st["mu1"][:, ss:ss + 1],
                                           st["mu1"][:, ss:ss + 1]))
            VS(lambda e, ss=s: e.tensor_sub(st["va"][:, ss:ss + 1],
                                           st["va"][:, ss:ss + 1],
                                           st["vb"][:, ss:ss + 1]), inc=sep2.inc())
            SC(lambda e: e.activation(st["vb"][:, :], st["va"][:, :],
                                      AF.Sqrt, bias=0.0, scale=1.0),
                wait=[(sep2, sep2.n)], inc=ssq2.inc())
            VS(lambda e: e.reciprocal(st["r1"][:, :], st["vb"][:, :]),
               wait=[(ssq2, ssq2.n)])
            V(lambda e, ss=s: e.tensor_scalar(n3[:, :], zf[:, :],
                                              st["mu1"][:, ss:ss + 1],
                                              st["r1"][:, ss:ss + 1],
                                              op0=A.subtract, op1=A.mult),
              wait=[(ssq2, ssq2.n), (svs, svs.n)])
            V(lambda e, b=xb: e.tensor_add(x2b[:, :], n3[:, :],
                                           x1l[:, b * D:(b + 1) * D]),
              wait=[(sxl, sxl.n), (sout, sout.n)], inc=sep2.inc())
            SY(lambda e, tt=t, ss=s: e.dma_start(
                out[tt * TT + ss * 128: tt * TT + (ss + 1) * 128, :], x2b[:, :]),
               wait=[(sep2, sep2.n)], inc=sout.inc(16))
    if dbg:
        _names = ["m_b", "s0", "r0", "rK0", "m_up", "s_up", "cU", "m_g", "s_g",
                  "cG", "sgmix", "c65", "m_p", "s_g2", "c2m", "m_op", "s_o",
                  "c3m", "cout", "r_o", "r_g2", "r_up", "r_g"]
        sdbg = P.sem("sdbg")
        for _i, _nm in enumerate(_names):
            G(lambda e, i=_i, b=scl[_nm]: e.dma_start(dbgscl[i:i + 1, :], b[0:1, :]),
              wait=[(scc, scc.n), (sout, sout.n)], inc=sdbg.inc(16))
        G(lambda e, b=ks: e.dma_start(dbgscl[30:31, :], b[0:1, :]), inc=sdbg.inc(16))
        G(lambda e, b=ckv: e.dma_start(dbgscl[31:32, :], b[0:1, :]), inc=sdbg.inc(16))
        G(lambda e: e.nop(), wait=[(sdbg, sdbg.n)])
    SY(lambda e: e.nop(), wait=[(sout, sout.n)])
    P.emit_block()
    _exit_all(sb3)

    _exit_all(sb)
    for g in reversed(P.guards):
        g.__exit__(None, None, None)
    P.check()
    return nc


def _prep_inputs(inputs, shard=True):
    x = np.asarray(inputs["L0_attn_x"])
    W_in = np.asarray(inputs["L0_attn_W_in"])
    conv_w = np.asarray(inputs["L0_attn_conv_w"])
    conv_b = np.asarray(inputs["L0_attn_conv_b"])
    W_B = np.asarray(inputs["L0_attn_W_B"])
    W_C = np.asarray(inputs["L0_attn_W_C"])
    D_param = np.asarray(inputs["L0_attn_D_param"])
    W_out = np.asarray(inputs["L0_attn_W_out"])
    ln_g = np.asarray(inputs["L0_attn_ln_gamma"])
    ln_b = np.asarray(inputs["L0_attn_ln_beta"])
    w_gate = np.asarray(inputs["L0_ffn_w_gate"])
    w_up = np.asarray(inputs["L0_ffn_w_up"])
    w_down = np.asarray(inputs["L0_ffn_w_down"])

    f = np.float32
    com = {
        "winT": np.ascontiguousarray(W_in.T, dtype=f),
        "wBT": np.ascontiguousarray(W_B.T, dtype=f),
        "wCT": np.ascontiguousarray(W_C.T, dtype=f),
        "woutT": np.ascontiguousarray(W_out.T, dtype=f),
        "convb": np.ascontiguousarray(np.asarray(conv_b, f).reshape(16, 128).T),
        "dparam": np.ascontiguousarray(np.asarray(D_param, f).reshape(16, 128).T),
        "gb": None, "bb": None,
        "gbin": np.ascontiguousarray(np.broadcast_to(ln_g, (128, D)), dtype=f),
        "bbin": np.ascontiguousarray(np.broadcast_to(ln_b, (128, D)), dtype=f),
        "wupT": np.ascontiguousarray(w_up.T, dtype=f),
        "wgT": np.ascontiguousarray(w_gate.T, dtype=f),
        "wdT": np.ascontiguousarray(w_down.T, dtype=f),
        "wup_n": np.ascontiguousarray(w_up, dtype=f),
        "wg_n": np.ascontiguousarray(w_gate, dtype=f),
        "wd_n": np.ascontiguousarray(w_down, dtype=f),
        "identin": np.eye(128, dtype=f),
    }
    del com["gb"], com["bb"]
    cd = np.zeros((64 * 128, 128), f)
    idx = np.arange(128)
    for tap in range(4):
        for ck in range(16):
            base = (tap * 16 + ck) * 128
            cd[base + idx, idx] = conv_w[ck * 128 + idx, 0, tap]
    com["convd"] = cd

    maps = []
    if shard:
        for c in range(8):
            b, q = divmod(c, 4)
            lo = q * 4096
            if q == 0:
                xs = np.concatenate([np.zeros((512, D), f), x[b, 0:4096]], axis=0)
            else:
                xs = x[b, lo - 512: lo + 4096]
            m = dict(com)
            m["xp"] = np.ascontiguousarray(xs, dtype=f)
            m["maskin"] = np.full((1, 4), 0.0 if q == 0 else 1.0, f)
            maps.append(m)
    else:
        xall = np.ascontiguousarray(x.reshape(32768, D), dtype=f)
        for c in range(8):
            m = dict(com)
            m["xp"] = xall
            m["maskin"] = np.full((1, 4), 1.0, f)
            maps.append(m)
    return maps


def kernel(**inputs):
    shard = True
    maps = _prep_inputs(inputs, shard=shard)
    nc = build(shard=shard)
    res = run_bass_kernel_spmd(nc, maps, core_ids=list(range(8)))
    out = np.empty((2, 16384, 1024), np.float32)
    if shard:
        for c in range(8):
            b, q = divmod(c, 4)
            out[b, q * 4096:(q + 1) * 4096] = res.results[c]["out"]
    else:
        out[:] = res.results[0]["out"].reshape(2, 16384, 1024)
    return out


# version 12 (final)
# speedup vs baseline: 6.2116x; 1.0490x over previous
import numpy as np
import concourse.bass as bass
import concourse.mybir as mybir
from concourse.bass_utils import run_bass_kernel_spmd

F32 = mybir.dt.float32
BF16 = mybir.dt.bfloat16
A = mybir.AluOpType
AF = mybir.ActivationFunctionType
AX = mybir.AxisListType.X

D = 1024
C2 = 2048
S64 = 64
I3 = 3072
TT = 512
NSUB = 4
C32 = 12582912.0
C52 = 6755399441055744.0
EPS = 1e-5


class Sem:
    def __init__(self, handle, name):
        self.h = handle
        self.name = name
        self.n = 0

    def inc(self, amt=1):
        self.n += amt
        return (self, amt)


class Planner:
    ENGINES = ("sync", "vector", "scalar", "tensor", "gpsimd")

    def __init__(self, nc):
        self.nc = nc
        self.guards = []
        self.sems = {}
        self.events = {e: [] for e in self.ENGINES}
        self.waits_seen = []

    def sem(self, name):
        if name not in self.sems:
            g = self.nc.semaphore(name)
            h = g.__enter__()
            self.guards.append(g)
            self.sems[name] = Sem(h, name)
        return self.sems[name]

    def ev(self, eng, fn, wait=(), inc=None):
        wait = [w for w in wait if w[1] > 0]
        for s, v in wait:
            self.waits_seen.append((s, v))
        self.events[eng].append((wait, fn, inc))

    def drain(self, sems):
        targets = [(s, s.n) for s in sems if s.n > 0]
        for e in self.ENGINES:
            self.ev(e, lambda eng: eng.nop(), wait=targets)

    def emit_block(self):
        evs = {e: self.events[e] for e in self.ENGINES if self.events[e]}
        if not evs:
            return
        with self.nc.Block() as blk:
            for ename, elist in evs.items():
                def body(eng, elist=elist):
                    for wait, fn, inc in elist:
                        for s, v in wait:
                            eng.wait_ge(s.h, v)
                        inst = fn(eng)
                        if inc is not None:
                            s, amt = inc
                            inst.then_inc(s.h, amt)
                getattr(blk, ename)(body)
        self.events = {e: [] for e in self.ENGINES}

    def check(self):
        for s, v in self.waits_seen:
            assert v <= s.n, f"wait on {s.name} for {v} but final count {s.n}"


def _enter(stack, guard):
    t = guard.__enter__()
    stack.append(guard)
    return t


def _exit_all(stack):
    for g in reversed(stack):
        g.__exit__(None, None, None)
    stack.clear()


def build(shard=True, dbg=False, cc=None, stop_after=None, nt_limit=None, main_stop=None):
    nc = bass.Bass("TRN2", target_bir_lowering=False,
                   detect_race_conditions=False)
    if cc is None:
        cc = shard
    if shard:
        NT = 9
        resets = {0}
        NTOK = 4096
        XROWS = 4608
    else:
        NT = 64
        resets = {0, 32}
        NTOK = 32768
        XROWS = 32768
    NFT = NTOK // TT

    xp = nc.dram_tensor("xp", [XROWS, D], F32, kind="ExternalInput").ap()
    winT = nc.dram_tensor("winT", [D, 4096], F32, kind="ExternalInput").ap()
    wBT = nc.dram_tensor("wBT", [C2, S64], F32, kind="ExternalInput").ap()
    wCT = nc.dram_tensor("wCT", [S64, C2], F32, kind="ExternalInput").ap()
    woutT = nc.dram_tensor("woutT", [C2, D], F32, kind="ExternalInput").ap()
    convd = nc.dram_tensor("convd", [64 * 128, 128], F32, kind="ExternalInput").ap()
    convb = nc.dram_tensor("convb", [128, 16], F32, kind="ExternalInput").ap()
    dparam = nc.dram_tensor("dparam", [128, 16], F32, kind="ExternalInput").ap()
    gbin = nc.dram_tensor("gbin", [128, D], F32, kind="ExternalInput").ap()
    bbin = nc.dram_tensor("bbin", [128, D], F32, kind="ExternalInput").ap()
    wupT = nc.dram_tensor("wupT", [D, I3], F32, kind="ExternalInput").ap()
    wgT = nc.dram_tensor("wgT", [D, I3], F32, kind="ExternalInput").ap()
    wdT = nc.dram_tensor("wdT", [I3, D], F32, kind="ExternalInput").ap()
    wup_n = nc.dram_tensor("wup_n", [I3, D], F32, kind="ExternalInput").ap()
    wg_n = nc.dram_tensor("wg_n", [I3, D], F32, kind="ExternalInput").ap()
    wd_n = nc.dram_tensor("wd_n", [D, I3], F32, kind="ExternalInput").ap()
    maskin = nc.dram_tensor("maskin", [1, 4], F32, kind="ExternalInput").ap()
    identin = nc.dram_tensor("identin", [128, 128], F32, kind="ExternalInput").ap()
    out = nc.dram_tensor("out", [NTOK, D], F32, kind="ExternalOutput").ap()

    kd = "ExternalOutput" if dbg else "Internal"
    x1scr = nc.dram_tensor("x1scr", [NTOK, D], F32, kind=kd).ap()
    bTscr = nc.dram_tensor("bTscr", [NTOK // TT, 128, 8 * TT], BF16, kind=kd).ap()
    woutscr = nc.dram_tensor("woutscr", [16, 128, D], BF16).ap()
    convdscr = nc.dram_tensor("convdscr", [64 * 128, 128], BF16).ap()
    wupdq = nc.dram_tensor("wupdq", [8, 128, I3], BF16, kind=kd).ap()
    wgdq = nc.dram_tensor("wgdq", [8, 128, I3], BF16).ap()
    wddq = nc.dram_tensor("wddq", [24, 128, D], BF16).ap()
    upscr = nc.dram_tensor("upscr", [NTOK // TT, 128, 24 * TT], BF16).ap()
    gscr = nc.dram_tensor("gscr", [NTOK // TT, 128, 24 * TT], BF16).ap()
    pscr = nc.dram_tensor("pscr", [NTOK // TT, 128, 24 * TT], BF16).ap()
    opscr = nc.dram_tensor("opscr", [NTOK // TT, 128, 24 * TT], BF16).ap()
    sdU = nc.dram_tensor("sdU", [32, I3], F32).ap()
    sdG = nc.dram_tensor("sdG", [32, I3], F32).ap()
    sdD = nc.dram_tensor("sdD", [96, D], F32).ap()
    rdU = nc.dram_tensor("rdU", [32, I3], F32).ap()
    rdG = nc.dram_tensor("rdG", [32, I3], F32).ap()
    rdD = nc.dram_tensor("rdD", [96, D], F32).ap()
    foldd = nc.dram_tensor("foldd", [4, 128], F32).ap()
    dbgscl = nc.dram_tensor("dbgscl", [32, 4], F32, kind=kd).ap()
    dbg_aT = nc.dram_tensor("dbg_aT", [128, 8 * TT], BF16, kind=kd).ap()
    dbg_u = nc.dram_tensor("dbg_u", [128, 16 * TT], BF16, kind=kd).ap()
    dbg_sxg = nc.dram_tensor("dbg_sxg", [128, 16 * TT], BF16, kind=kd).ap()
    dbg_hs = nc.dram_tensor("dbg_hs", [S64, TT], F32, kind=kd).ap()
    dbg_yo = nc.dram_tensor("dbg_yo", [128, 8 * TT], F32, kind=kd).ap()
    dbg_z = nc.dram_tensor("dbg_z", [128, D], F32, kind=kd).ap()
    dbg_x = nc.dram_tensor("dbg_x", [128, NSUB * D], F32, kind=kd).ap()
    dbg_a = nc.dram_tensor("dbg_a", [128, D], F32, kind=kd).ap()
    dbg_w = nc.dram_tensor("dbg_w", [128, 4096], BF16, kind=kd).ap()
    cc_in = nc.dram_tensor("cc_in", [1, 16], F32).ap()
    cc_out = nc.dram_tensor("cc_out", [1, 16], F32).ap()

    P = Planner(nc)
    sb = []

    ident = _enter(sb, nc.sbuf_tensor("ident", [128, 128], F32))
    ks = _enter(sb, nc.sbuf_tensor("ks", [128, 4], F32))
    ckv = _enter(sb, nc.sbuf_tensor("ckv", [128, 4], F32))
    r7 = _enter(sb, nc.sbuf_tensor("r7", [128, 4], F32))
    r327 = _enter(sb, nc.sbuf_tensor("r327", [128, 4], F32))
    mask_sb = _enter(sb, nc.sbuf_tensor("mask_sb", [128, 4], F32))
    dpar = _enter(sb, nc.sbuf_tensor("dpar", [128, 16], F32))
    cvb = _enter(sb, nc.sbuf_tensor("cvb", [128, 16], F32))
    gbs = _enter(sb, nc.sbuf_tensor("gbs", [128, D], F32))
    bbs = _enter(sb, nc.sbuf_tensor("bbs", [128, D], F32))
    dec95 = _enter(sb, nc.sbuf_tensor("dec95", [S64, TT], F32))
    zk64 = _enter(sb, nc.sbuf_tensor("zk64", [S64, 4], F32))
    scl = {}
    for nm in ("m_b", "s0", "r0", "rK0", "m_up", "s_up", "r_up", "cU",
               "m_g", "s_g", "r_g", "cG", "sgmix", "c65", "m_p", "s_g2",
               "r_g2", "c2m", "m_op", "s_o", "r_o", "c3m", "cout",
               "tmpa", "tmpb", "tmpc"):
        scl[nm] = _enter(sb, nc.sbuf_tensor("scl_" + nm, [128, 4], F32))
    bmax = _enter(sb, nc.sbuf_tensor("bmax", [128, 4], F32))
    dumb = _enter(sb, nc.sbuf_tensor("dumb", [128, 4], F32))
    fmax = {nm: _enter(sb, nc.sbuf_tensor("fmax_" + nm, [128, 4], F32))
            for nm in ("up", "g", "p", "op")}
    rowb = _enter(sb, nc.sbuf_tensor("rowb", [1, 128], F32))
    locb = _enter(sb, nc.sbuf_tensor("locb", [1, 16], F32))
    loc2b = _enter(sb, nc.sbuf_tensor("loc2b", [1, 16], F32))
    st = {nm: _enter(sb, nc.sbuf_tensor("st_" + nm, [128, 4], F32))
          for nm in ("sx", "sq", "mu1", "r1", "mu2", "r2", "mu3", "r3", "va", "vb")}
    pz0 = _enter(sb, nc.psum_tensor("pz0", [128, 1024], F32))
    pz1 = _enter(sb, nc.psum_tensor("pz1", [128, 1024], F32))
    pm0 = _enter(sb, nc.psum_tensor("pm0", [128, 512], F32))
    pm1 = _enter(sb, nc.psum_tensor("pm1", [128, 512], F32))
    pc0 = _enter(sb, nc.psum_tensor("pc0", [128, 512], F32))
    pbu = _enter(sb, nc.psum_tensor("pbu", [128, 512], F32))

    svs = P.sem("svs")

    def V(fn, wait=(), inc=None):
        P.ev("vector", fn, wait, inc)

    def VS(fn, wait=(), inc=None):
        P.ev("vector", fn, list(wait) + [(svs, svs.n)], svs.inc())
        if inc is not None:
            P.ev("vector",
                 lambda e: e.tensor_copy(dumb[:, 0:4], ks[:, 0:4]),
                 [(svs, svs.n)], inc)

    def SC(fn, wait=(), inc=None):
        P.ev("scalar", fn, wait, inc)

    def G(fn, wait=(), inc=None):
        P.ev("gpsimd", fn, wait, inc)

    def T(fn, wait=(), inc=None):
        P.ev("tensor", fn, wait, inc)

    def SY(fn, wait=(), inc=None):
        P.ev("sync", fn, wait, inc)

    sload = P.sem("sload")
    sprep = P.sem("sprep")
    scast = P.sem("scast")
    sst = P.sem("sst")

    for dst, src in ((ident[:, :], identin),
                     (mask_sb[:, :], bass.AP(maskin.tensor, 0, [[0, 128], [1, 4]])),
                     (dpar[:, :], dparam),
                     (cvb[:, :], convb),
                     (gbs[:, :], gbin),
                     (bbs[:, :], bbin)):
        SY(lambda e, d=dst, s=src: e.dma_start(d, s), inc=sload.inc(16))

    VS(lambda e: e.memset(ks[:, :], 1.0))
    VS(lambda e: e.scalar_tensor_tensor(scl["tmpb"][:, :], ks[:, :],
                                        1.0 + 2.0 ** -40, ks[:, :],
                                        op0=A.mult, op1=A.subtract))
    VS(lambda e: e.tensor_scalar(ks[:, :], scl["tmpb"][:, :], 2.0 ** 40, None,
                                 op0=A.mult))
    VS(lambda e: e.tensor_scalar(ckv[:, :], ks[:, :], C52, None, op0=A.mult))
    VS(lambda e: e.memset(scl["tmpa"][:, :], 7.0))
    VS(lambda e: e.reciprocal(r7[:, :], scl["tmpa"][:, :]))
    VS(lambda e: e.memset(scl["tmpa"][:, :], 32767.0))
    VS(lambda e: e.reciprocal(r327[:, :], scl["tmpa"][:, :]))
    V(lambda e: e.memset(dec95[:, :], 0.95))
    for _nm in ("sx", "sq", "mu1", "r1", "mu2", "r2", "mu3", "r3", "va", "vb"):
        VS(lambda e, b=st[_nm]: e.memset(b[:, :], 1.0))
    VS(lambda e: e.memset(zk64[:, :], 0.0), inc=sprep.inc())

    st0 = []
    wstg = _enter(st0, nc.sbuf_tensor("wstg", [128, 4096], F32))
    wstg2 = _enter(st0, nc.sbuf_tensor("wstg2", [128, 4096], F32))
    wbf = _enter(st0, nc.sbuf_tensor("wbf", [128, 4096], BF16))
    wbf2 = _enter(st0, nc.sbuf_tensor("wbf2", [128, 4096], BF16))
    jobs = [("wo", c) for c in range(16)] + [("cv", c) for c in range(4)]
    cast_v = []
    for j, (kind, c) in enumerate(jobs):
        sg = wstg if j % 2 == 0 else wstg2
        bg = wbf if j % 2 == 0 else wbf2
        w = [(sprep, cast_v[j - 2])] if j >= 2 else []
        if kind == "wo":
            SY(lambda e, c=c, b=sg: e.dma_start(b[:, 0:D],
                                                woutT[c * 128:(c + 1) * 128, :]),
               wait=w, inc=scast.inc(16))
            V(lambda e, b=sg, o=bg: e.tensor_copy(o[:, 0:D], b[:, 0:D]),
              wait=[(scast, scast.n), (sst, max(0, sst.n - 16))], inc=sprep.inc())
            cast_v.append(sprep.n)
            SY(lambda e, c=c, o=bg: e.dma_start(
                bass.AP(woutscr.tensor, c * 128 * D, [[D, 128], [1, D]]), o[:, 0:D]),
               wait=[(sprep, sprep.n)], inc=sst.inc(16))
        else:
            SY(lambda e, c=c, b=sg: e.dma_start(
                b[:, 0:2048],
                bass.AP(convd.tensor, c * 16 * 128 * 128,
                        [[128, 128], [128 * 128, 16], [1, 128]])),
               wait=w, inc=scast.inc(16))
            V(lambda e, b=sg, o=bg: e.tensor_copy(o[:, 0:2048], b[:, 0:2048]),
              wait=[(scast, scast.n), (sst, max(0, sst.n - 16))], inc=sprep.inc())
            cast_v.append(sprep.n)
            SY(lambda e, c=c, o=bg: e.dma_start(
                bass.AP(convdscr.tensor, c * 16 * 128 * 128,
                        [[128, 128], [128 * 128, 16], [1, 128]]),
                o[:, 0:2048]),
               wait=[(sprep, sprep.n)], inc=sst.inc(16))
    std = st0
    wnat = _enter(std, nc.sbuf_tensor("wnat", [128, I3], F32))
    snat = _enter(std, nc.sbuf_tensor("snat", [128, 768], F32))
    rnat = _enter(std, nc.sbuf_tensor("rnat", [128, 768], F32))
    sTs = _enter(std, nc.sbuf_tensor("sTs", [96, I3], F32))
    rTs = _enter(std, nc.sbuf_tensor("rTs", [96, I3], F32))
    sbc = _enter(std, nc.sbuf_tensor("sbc", [128, I3], F32))
    rbc = _enter(std, nc.sbuf_tensor("rbc", [128, I3], F32))
    wTb = _enter(std, nc.sbuf_tensor("wTb", [128, I3], F32))
    wT2 = _enter(std, nc.sbuf_tensor("wT2", [128, I3], F32))
    wqb = _enter(std, nc.sbuf_tensor("wqb", [128, I3], BF16))
    wqb2 = _enter(std, nc.sbuf_tensor("wqb2", [128, I3], BF16))
    wnat2 = _enter(std, nc.sbuf_tensor("wnat2", [128, I3], F32))
    sbc2 = _enter(std, nc.sbuf_tensor("sbc2", [128, I3], F32))
    rbc2 = _enter(std, nc.sbuf_tensor("rbc2", [128, I3], F32))
    sdqd = P.sem("sdqd")
    sdqg = P.sem("sdqg")
    sdqs = P.sem("sdqs")
    sdqc = P.sem("sdqc")

    def dequant_weight(nat_ap, T_ap, nrows, iblk, sdram, rdram, dqscr, nTchunks, tfree):
        nch = nrows // 128
        nfree = iblk * 32
        red_v = []
        for ck in range(nch):
            b = wnat if ck % 2 == 0 else wnat2
            w = ([(sdqc, red_v[ck - 2])] if ck >= 2 else [(sdqc, sdqc.n)])
            SY(lambda e, c=ck, b=b: e.dma_start(b[:, 0:nfree],
                                                nat_ap[c * 128:(c + 1) * 128, :]),
               wait=w, inc=sdqg.inc(16))
            V(lambda e, c=ck, b=b: e.tensor_reduce(
                bass.AP(snat, c * iblk, [[768, 128], [1, iblk], [1, 1]]),
                b[:, 0:nfree].rearrange("p (b k) -> p b k", k=32),
                axis=AX, op=A.max, apply_absolute_value=True),
              wait=[(sdqg, sdqg.n)], inc=sdqc.inc())
            red_v.append(sdqc.n)
        V(lambda e: e.tensor_scalar(snat[:, 0:nch * iblk], snat[:, 0:nch * iblk],
                                    r7[:, 0:1], 1e-12, op0=A.mult, op1=A.max),
          wait=[(sdqc, sdqc.n), (svs, svs.n)])
        V(lambda e: e.reciprocal(rnat[:, 0:nch * iblk], snat[:, 0:nch * iblk]),
          inc=sdqc.inc())
        for ck in range(nch):
            pzt = pz0 if ck % 2 == 0 else pz1
            T(lambda e, c=ck, p=pzt: e.transpose(
                p[0:iblk, 0:128],
                bass.AP(snat, c * iblk, [[768, 128], [1, iblk]]), ident[:, :]),
              wait=[(sdqc, sdqc.n), (sdqd, max(0, sdqd.n - 32))] if ck < 2 else
                   [(sdqc, sdqc.n)])
            T(lambda e, c=ck, p=pzt: e.transpose(
                p[0:iblk, 128:256],
                bass.AP(rnat, c * iblk, [[768, 128], [1, iblk]]), ident[:, :]),
              inc=sdqc.inc())
            V(lambda e, c=ck, p=pzt: e.tensor_copy(
                sTs[0:iblk, c * 128:(c + 1) * 128], p[0:iblk, 0:128]),
              wait=[(sdqc, sdqc.n)])
            V(lambda e, c=ck, p=pzt: e.tensor_copy(
                rTs[0:iblk, c * 128:(c + 1) * 128], p[0:iblk, 128:256]),
              inc=sdqc.inc())
        SY(lambda e: e.dma_start(sdram[0:iblk, 0:nrows], sTs[0:iblk, 0:nrows]),
           wait=[(sdqc, sdqc.n)], inc=sdqd.inc(16))
        SY(lambda e: e.dma_start(rdram[0:iblk, 0:nrows], rTs[0:iblk, 0:nrows]),
           inc=sdqd.inc(16))
        base_d = sdqd.n
        qdq_v = []
        st_v = []
        for hk in range(nTchunks):
            bW = wTb if hk % 2 == 0 else wT2
            bS = sbc if hk % 2 == 0 else sbc2
            bR = rbc if hk % 2 == 0 else rbc2
            w = ([(sdqc, qdq_v[hk - 2])] if hk >= 2 else [(sdqc, sdqc.n)])
            SY(lambda e, c=hk, b=bW: e.dma_start(b[:, 0:tfree],
                                                 T_ap[c * 128:(c + 1) * 128, :]),
               wait=w, inc=sdqg.inc(16))
            SY(lambda e, c=hk, b=bS: e.dma_start(
                b[:, 0:tfree],
                bass.AP(sdram.tensor, c * 4 * tfree, [[tfree, 4], [0, 32], [1, tfree]])),
               wait=([(sdqd, base_d)] if hk < 2 else
                     [(sdqc, qdq_v[hk - 2])]),
               inc=sdqs.inc(16))
            SY(lambda e, c=hk, b=bR: e.dma_start(
                b[:, 0:tfree],
                bass.AP(rdram.tensor, c * 4 * tfree, [[tfree, 4], [0, 32], [1, tfree]])),
               inc=sdqs.inc(16))
            V(lambda e, b=bW, r=bR: e.tensor_mul(b[:, 0:tfree], b[:, 0:tfree],
                                                 r[:, 0:tfree]),
              wait=[(sdqs, sdqs.n), (sdqg, sdqg.n)] +
                   ([(sdqd, max(0, st_v[hk - 2]))] if hk >= 2 else []))
            V(lambda e, b=bW: e.tensor_scalar(b[:, 0:tfree], b[:, 0:tfree],
                                              C32, C32, op0=A.add, op1=A.subtract))
            V(lambda e, b=bW: e.tensor_scalar(b[:, 0:tfree], b[:, 0:tfree],
                                              ckv[:, 0:1], ckv[:, 0:1],
                                              op0=A.add, op1=A.subtract))
            V(lambda e, b=bW, s=bS, q=(wqb if hk % 2 == 0 else wqb2):
              e.tensor_mul(q[:, 0:tfree], b[:, 0:tfree], s[:, 0:tfree]),
              inc=sdqc.inc())
            qdq_v.append(sdqc.n)
            SY(lambda e, c=hk, d=dqscr, q=(wqb if hk % 2 == 0 else wqb2):
               e.dma_start(d[c, :, :], q[:, 0:tfree]),
               wait=[(sdqc, sdqc.n)], inc=sdqd.inc(16))
            st_v.append(sdqd.n)

    dequant_weight(wup_n, wupT, I3, 32, sdU, rdU, wupdq, 8, I3)
    dequant_weight(wg_n, wgT, I3, 32, sdG, rdG, wgdq, 24, D) if False else None
    dequant_weight(wg_n, wgT, I3, 32, sdG, rdG, wgdq, 8, I3)
    dequant_weight(wd_n, wdT, D, 96, sdD, rdD, wddq, 24, D)
    P.drain([sload, scast, sst, sprep, sdqd, sdqg, sdqs, sdqc, svs])
    P.emit_block()
    _exit_all(std)
    if stop_after == "dq":
        _exit_all(sb)
        for g in reversed(P.guards):
            g.__exit__(None, None, None)
        P.check()
        return nc

    sa = []
    winb = _enter(sa, nc.sbuf_tensor("winb", [128, 8 * 4096], BF16))
    wBb = _enter(sa, nc.sbuf_tensor("wBb", [128, 16 * 64], BF16))
    wCb = _enter(sa, nc.sbuf_tensor("wCb", [S64, C2], BF16))
    wob = _enter(sa, nc.sbuf_tensor("wob", [128, 2 * 1024], BF16))
    cvd = _enter(sa, nc.sbuf_tensor("cvd", [128, 2 * 4 * 128], BF16))
    x_t = _enter(sa, nc.sbuf_tensor("x_t", [128, NSUB * D], F32))
    a_sub = _enter(sa, nc.sbuf_tensor("a_sub", [128, D], F32))
    aT = _enter(sa, nc.sbuf_tensor("aT", [128, 8 * TT], BF16))
    xin = _enter(sa, nc.sbuf_tensor("xin", [128, 2 * 520], BF16))
    xtail = _enter(sa, nc.sbuf_tensor("xtail", [128, 16 * 4], BF16))
    u_bf = _enter(sa, nc.sbuf_tensor("u_bf", [128, 16 * TT], BF16))
    sxg = _enter(sa, nc.sbuf_tensor("sxg", [128, 16 * TT], BF16))
    y2t = _enter(sa, nc.sbuf_tensor("y2t", [128, 2 * TT], F32))
    hs0 = _enter(sa, nc.sbuf_tensor("hs0", [S64, TT], F32))
    hs1 = _enter(sa, nc.sbuf_tensor("hs1", [S64, TT], F32))
    hsb = _enter(sa, nc.sbuf_tensor("hsb", [S64, TT], BF16))
    bu_m = _enter(sa, nc.sbuf_tensor("bu_m", [S64, TT], F32))
    yo = _enter(sa, nc.sbuf_tensor("yo", [128, 8 * TT], F32))
    zb = _enter(sa, nc.sbuf_tensor("zb", [128, D], F32))
    nb = _enter(sa, nc.sbuf_tensor("nb", [128, D], F32))
    x1b = _enter(sa, nc.sbuf_tensor("x1b", [128, D], F32))
    wbuf = zb
    bsub = nb
    sdump = _enter(sa, nc.sbuf_tensor("sdump", [128, D], F32))
    bT_b = _enter(sa, nc.sbuf_tensor("bT_b", [128, 8 * TT], BF16))

    swl = P.sem("swl")
    sx = P.sem("sx")
    sfr = P.sem("sfr")
    sfs = P.sem("sfs")
    saT = P.sem("saT")
    saTe = P.sem("saTe")
    sxp = P.sem("sxp")
    sxe = P.sem("sxe")
    scv = P.sem("scv")
    sce = P.sem("sce")
    sbu = P.sem("sbu")
    ssc = P.sem("ssc")
    shb = P.sem("shb")
    swc = P.sem("swc")
    sy2 = P.sem("sy2")
    swo = P.sem("swo")
    swoe = P.sem("swoe")
    szt = P.sem("szt")
    sepi = P.sem("sepi")
    ssq = P.sem("ssq")
    sbt = P.sem("sbt")
    sbte = P.sem("sbte")
    sst1 = P.sem("sst1")
    sstb = P.sem("sstb")
    swos = P.sem("swos")
    scvs = P.sem("scvs")

    stw = []
    wstg3 = yo
    for dk in range(8):
        SY(lambda e, d=dk: e.dma_start(wstg3[:, :], winT[d * 128:(d + 1) * 128, :]),
           wait=[(sprep, sprep.n)], inc=swl.inc(16))
        V(lambda e, d=dk: e.tensor_copy(winb[:, d * 4096:(d + 1) * 4096], wstg3[:, :]),
          wait=[(swl, swl.n)], inc=sprep.inc())
    for ck in range(16):
        SY(lambda e, c=ck: e.dma_start(wstg3[:, 0:64], wBT[c * 128:(c + 1) * 128, :]),
           wait=[(sprep, sprep.n)], inc=swl.inc(16))
        V(lambda e, c=ck: e.tensor_scalar(wBb[:, c * 64:(c + 1) * 64], wstg3[:, 0:64],
                                          0.05, None, op0=A.mult),
          wait=[(swl, swl.n)], inc=sprep.inc())
    SY(lambda e: e.dma_start(wstg3[0:64, 0:C2], wCT), wait=[(sprep, sprep.n)],
       inc=swl.inc(16))
    V(lambda e: e.tensor_copy(wCb[:, :], wstg3[0:64, 0:C2]),
      wait=[(swl, swl.n)], inc=sprep.inc())
    V(lambda e: e.memset(xtail[:, :], 0.0), inc=sprep.inc())
    P.drain([swl, sprep])
    P.emit_block()

    for t in range(NT if nt_limit is None else min(NT, nt_limit)):
        warm = shard and (t == 0)
        ft = t - 1 if shard else t
        for s in range(NSUB):
            SY(lambda e, tt=t, ss=s: e.dma_start(
                x_t[:, ss * D:(ss + 1) * D],
                xp[tt * TT + ss * 128: tt * TT + (ss + 1) * 128, :]),
               wait=[(sepi, sepi.n), (sfr, sfr.n)] if t > 0 else [],
               inc=sx.inc(16))
        if dbg and t == (1 if shard else 0):
            G(lambda e: e.dma_start(dbg_x[:, :], x_t[:, :]),
              wait=[(sx, sx.n)], inc=P.sem("sdbg2").inc(16))
            G(lambda e: e.dma_start(dbg_w[:, :], winb[:, 0:4096]),
              inc=P.sem("sdbg2").inc(16))
        for s in range(NSUB):
            V(lambda e, ss=s: e.tensor_scalar(
                a_sub[:, :], x_t[:, ss * D:(ss + 1) * D], 1.0, 0.0,
                op0=A.mult, op1=A.add, accum_out=st["sx"][:, ss:ss + 1]),
              wait=[(sx, sx.n)], inc=sfs.inc())
            SC(lambda e, ss=s: e.activation(
                sdump[:, :], x_t[:, ss * D:(ss + 1) * D], AF.Square,
                accum_out=st["sq"][:, ss:ss + 1]),
               wait=[(sx, sx.n)], inc=ssq.inc())
        VS(lambda e: e.tensor_scalar(st["mu1"][:, :], st["sx"][:, :], 1.0 / D, None,
                                     op0=A.mult),
           wait=[(ssq, ssq.n), (sfs, sfs.n)])
        VS(lambda e: e.tensor_scalar(st["va"][:, :], st["sq"][:, :], 1.0 / D, EPS,
                                     op0=A.mult, op1=A.add))
        VS(lambda e: e.tensor_mul(st["vb"][:, :], st["mu1"][:, :], st["mu1"][:, :]))
        VS(lambda e: e.tensor_sub(st["va"][:, :], st["va"][:, :], st["vb"][:, :]),
           inc=sfs.inc())
        SC(lambda e: e.activation(st["vb"][:, :], st["va"][:, :], AF.Sqrt,
                                  bias=0.0, scale=1.0),
           wait=[(sfs, sfs.n)], inc=ssq.inc())
        VS(lambda e: e.reciprocal(st["r1"][:, :], st["vb"][:, :]),
           wait=[(ssq, ssq.n)], inc=sfs.inc())
        for s in range(NSUB):
            V(lambda e, ss=s: e.tensor_scalar(
                a_sub[:, :], x_t[:, ss * D:(ss + 1) * D],
                st["mu1"][:, ss:ss + 1], st["r1"][:, ss:ss + 1],
                op0=A.subtract, op1=A.mult),
              wait=[(ssq, ssq.n), (saTe, saTe.n), (svs, svs.n)],
              inc=sfr.inc())
            if dbg and t == (1 if shard else 0) and s == 0:
                G(lambda e: e.dma_start(dbg_a[:, :], a_sub[:, :]),
                  wait=[(sfr, sfr.n)], inc=P.sem("sdbg2").inc(16))
            pzt = pz0 if s % 2 == 0 else pz1
            for dk in range(8):
                T(lambda e, d=dk, p=pzt: e.transpose(
                    p[:, d * 128:(d + 1) * 128], a_sub[:, d * 128:(d + 1) * 128],
                    ident[:, :]),
                  wait=([(sfr, sfr.n), (saTe, max(0, saTe.n - 1)),
                         (sbte, sbte.n), (sepi, sepi.n)] if dk == 0 else []),
                  inc=(saT.inc() if dk == 7 else None))
            SC(lambda e, ss=s, p=pzt: e.activation(
                bass.AP(aT, ss * 128, [[8 * TT, 128], [TT, 8], [1, 128]]),
                p[:, :], AF.Identity),
               wait=[(saT, saT.n)], inc=saTe.inc())
        if (not warm and main_stop not in ("scan", "y2", "woutnost", "woutseq", "woutub", "woutmm", "woutev", "woutev2")) or main_stop == "stream":
            wout_base = swo.n
            for ck in range(16):
                G(lambda e, c=ck, buf=(ck % 2): e.dma_start(
                    wob[:, buf * 1024:(buf + 1) * 1024],
                    bass.AP(woutscr.tensor, c * 128 * D, [[D, 128], [1, D]])),
                  wait=([] if main_stop == "stream" else
                        [(swo, max(0, wout_base + ck - 1))]),
                  inc=swos.inc(16))
        conv_base = scv.n
        scvs_base = scvs.n
        for fk in range(16):
            SY(lambda e, c=fk, buf=(fk % 2): e.dma_start(
                cvd[:, buf * 512:(buf + 1) * 512],
                bass.AP(convdscr.tensor, c * 128 * 128,
                        [[128, 128], [16 * 128 * 128, 4], [1, 128]])),
               wait=[(scv, max(0, conv_base + fk - 1))], inc=scvs.inc(16))
        nchunks = 16 if warm else 32
        for fk in range(nchunks):
            pmt = pm0 if fk % 2 == 0 else pm1
            for dk in range(8):
                T(lambda e, d=dk, f=fk, p=pmt: e.matmul(
                    p[:, :], winb[:, d * 4096 + f * 128: d * 4096 + (f + 1) * 128],
                    aT[:, d * TT:(d + 1) * TT],
                    start=(d == 0), stop=(d == 7)),
                  wait=([(saTe, saTe.n), (sxe, max(0, sxe.n - 1))] if dk == 0 else []),
                  inc=(sxp.inc() if dk == 7 else None))
            if fk < 16:
                xb = fk % 2
                SC(lambda e, b=xb, p=pmt: e.activation(
                    xin[:, b * 520 + 4: b * 520 + 4 + TT], p[:, :], AF.Identity),
                   wait=[(sxp, sxp.n), (scv, max(0, scv.n - 1)), (sfr, sfr.n)],
                   inc=sxe.inc())
                if t in resets:
                    V(lambda e, b=xb: e.memset(xin[:, b * 520 + 1: b * 520 + 4], 0.0),
                      wait=[(sxe, sxe.n)], inc=sfr.inc())
                else:
                    V(lambda e, f=fk, b=xb: e.tensor_copy(
                        xin[:, b * 520 + 1: b * 520 + 4], xtail[:, f * 4: f * 4 + 3]),
                      wait=[(sxe, sxe.n)], inc=sfr.inc())
                V(lambda e, f=fk, b=xb: e.tensor_copy(
                    xtail[:, f * 4: f * 4 + 3],
                    xin[:, b * 520 + 4 + TT - 3: b * 520 + 4 + TT]),
                  inc=sfr.inc())
                for tap in range(4):
                    T(lambda e, b=xb, tp=tap: e.matmul(
                        pc0[:, :],
                        cvd[:, b * 512 + tp * 128: b * 512 + (tp + 1) * 128],
                        xin[:, b * 520 + 1 + tp: b * 520 + 1 + tp + TT],
                        start=(tp == 0), stop=(tp == 3)),
                      wait=([(sfr, sfr.n), (scvs, scvs_base + 16 * (fk + 1)),
                             (sce, sce.n), (sy2, sy2.n)] if tap == 0 else []),
                      inc=(scv.inc() if tap == 3 else None))
                SC(lambda e, f=fk: e.activation(
                    u_bf[:, f * TT:(f + 1) * TT], pc0[:, :], AF.Silu,
                    bias=cvb[:, f:f + 1], scale=1.0),
                   wait=[(scv, scv.n)], inc=sce.inc())
                T(lambda e, f=fk: e.matmul(
                    pbu[0:S64, :], wBb[:, f * 64:(f + 1) * 64],
                    u_bf[:, f * TT:(f + 1) * TT],
                    start=(f == 0), stop=(f == 15)),
                  wait=[(sce, sce.n)] + ([(ssc, ssc.n)] if fk == 0 else []),
                  inc=(sbu.inc() if fk == 15 else None))
            else:
                SC(lambda e, f=fk, p=pmt: e.activation(
                    sxg[:, (f - 16) * TT:(f - 15) * TT], p[:, :], AF.Silu),
                   wait=[(sxp, sxp.n)], inc=sxe.inc())
        hs_cur = hs0 if t % 2 == 0 else hs1
        hs_prev = hs1 if t % 2 == 0 else hs0
        if warm:
            V(lambda e: e.tensor_scalar(bu_m[:, :], pbu[0:S64, :], mask_sb[0:S64, 0:1],
                                        None, op0=A.mult),
              wait=[(sbu, sbu.n)])
            V(lambda e, h=hs_cur: e.tensor_tensor_scan(
                h[:, :], dec95[:, :], bu_m[:, :], zk64[0:S64, 0:1],
                op0=A.mult, op1=A.add),
              inc=ssc.inc())
            continue
        init = zk64[0:S64, 0:1] if (t in resets) else hs_prev[:, TT - 1:TT]
        V(lambda e, h=hs_cur, i=init: e.tensor_tensor_scan(
            h[:, :], dec95[:, :], pbu[0:S64, :], i, op0=A.mult, op1=A.add),
          wait=[(sbu, sbu.n)], inc=ssc.inc())
        V(lambda e, h=hs_cur: e.tensor_copy(hsb[:, :], h[:, :]), inc=shb.inc())
        if main_stop in ("scan", "stream"):
            continue
        for ck in range(16):
            pmt = pm0 if ck % 2 == 0 else pm1
            T(lambda e, c=ck, p=pmt: e.matmul(
                p[:, :], wCb[:, c * 128:(c + 1) * 128], hsb[:, :],
                start=True, stop=True),
              wait=([(shb, shb.n), (sxe, sxe.n)] if ck == 0 else
                    [(sy2, max(0, sy2.n - 1))]),
              inc=swc.inc())
            V(lambda e, c=ck, p=pmt, yb=(ck % 2): e.scalar_tensor_tensor(
                y2t[:, yb * TT:(yb + 1) * TT], u_bf[:, c * TT:(c + 1) * TT],
                dpar[:, c:c + 1], p[:, :], op0=A.mult, op1=A.add),
              wait=[(swc, swc.n)])
            V(lambda e, c=ck, yb=(ck % 2): e.tensor_mul(
                sxg[:, c * TT:(c + 1) * TT], y2t[:, yb * TT:(yb + 1) * TT],
                sxg[:, c * TT:(c + 1) * TT]),
              inc=sy2.inc())
        if main_stop == "y2":
            continue
        wout_ps = [pm0[:, :], pm1[:, :], pc0[:, :], pbu[:, :],
                   pz0[:, 0:512], pz0[:, 512:1024], pz1[:, 0:512], pz1[:, 512:1024]]
        NDK = 4 if main_stop in ("wout4", "woutnost") else 8
        if main_stop in ("woutub", "woutmm"):
            NDK = 2
        if main_stop in ("woutev", "woutev2"):
            NDK = 0
        _seq = (main_stop in ("woutseq", "woutub", "woutmm"))
        _iter = ([(c, d) for d in range(2) for c in range(16)] if _seq else
                 [(c, d) for c in range(16) for d in range(NDK)])
        for ck, dk in _iter:
            if True:
                _lhs = (winb if main_stop in ("woutnost", "woutseq", "woutub", "woutmm", "woutev") else wob)
                _lo = (0 if main_stop in ("woutnost", "woutseq", "woutub") else (ck % 2) * 1024)
                _rhs = (u_bf if main_stop in ("woutub", "woutmm") else sxg)
                T(lambda e, d=dk, c=ck, p=wout_ps[dk], L=_lhs, o=_lo, R=_rhs: e.matmul(
                    p, L[:, o + d * 128: o + (d + 1) * 128],
                    R[:, c * TT:(c + 1) * TT],
                    start=(c == 0), stop=(c == 15)),
                  wait=([(sy2, sy2.n)] +
                        ([] if main_stop in ("woutnost", "woutseq", "woutub", "woutmm", "woutev") else
                         [(swos, swos.n - 16 * (15 - ck))]) +
                        [(swoe, max(0, swoe.n - 8))] if dk == 0 else []),
                  inc=(swo.inc() if (dk == NDK - 1 and not _seq) or
                       (_seq and ck == 15) else None))
        NEV = 0 if main_stop == "woutmm" else (2 if (_seq or main_stop in ("woutev", "woutev2")) else NDK)
        _ydst = (sdump if main_stop == "woutev2" else yo)
        for dk in range(NEV):
            SC(lambda e, d=dk, p=wout_ps[dk], Y=_ydst: e.activation(
                (Y[:, 0:TT] if main_stop == "woutev2" else Y[:, d * TT:(d + 1) * TT]),
                p, AF.Identity),
               wait=[(swo, swo.n), (sy2, sy2.n)], inc=swoe.inc())
        if main_stop in ("wout", "wout4", "woutnost", "woutseq", "woutub", "woutmm", "woutev", "woutev2"):
            continue
        if dbg and t == (1 if shard else 0):
            G(lambda e: e.dma_start(dbg_u[:, :], u_bf[:, :]),
              wait=[(sce, sce.n)], inc=P.sem("sdbg2").inc(16))
            G(lambda e: e.dma_start(dbg_sxg[:, :], sxg[:, :]),
              wait=[(sy2, sy2.n)], inc=P.sem("sdbg2").inc(16))
            G(lambda e, h=hs_cur: e.dma_start(dbg_hs[:, :], h[:, :]),
              wait=[(ssc, ssc.n)], inc=P.sem("sdbg2").inc(16))
            G(lambda e: e.dma_start(dbg_yo[:, :], yo[:, :]),
              wait=[(swoe, swoe.n)], inc=P.sem("sdbg2").inc(16))
            G(lambda e: e.dma_start(dbg_aT[:, :], aT[:, :]),
              wait=[(saTe, saTe.n)], inc=P.sem("sdbg2").inc(16))
        for s in range(NSUB):
            pzt = pz0 if s % 2 == 0 else pz1
            for dk in range(8):
                T(lambda e, d=dk, ss=s, p=pzt: e.transpose(
                    p[:, d * 128:(d + 1) * 128],
                    yo[:, d * TT + ss * 128: d * TT + (ss + 1) * 128], ident[:, :]),
                  wait=([(swoe, swoe.n), (sepi, max(0, sepi.n - 6)),
                         (sbte, max(0, sbte.n - 1))] if dk == 0 else []),
                  inc=(szt.inc() if dk == 7 else None))
            V(lambda e, ss=s: e.tensor_scalar(
                a_sub[:, :], x_t[:, ss * D:(ss + 1) * D],
                st["mu1"][:, ss:ss + 1], st["r1"][:, ss:ss + 1],
                op0=A.subtract, op1=A.mult),
              wait=[(szt, szt.n)])
            V(lambda e, p=pzt, ss=s: e.scalar_tensor_tensor(
                zb[:, :], p[:, :], 1.0, a_sub[:, :], op0=A.mult, op1=A.add,
                accum_out=st["sx"][:, ss:ss + 1]),
              wait=[(ssq, ssq.n)], inc=sepi.inc())
            if dbg and t == (1 if shard else 0) and s == 0:
                G(lambda e: e.dma_start(dbg_z[:, :], zb[:, :]),
                  wait=[(sepi, sepi.n)], inc=P.sem("sdbg2").inc(16))
            SC(lambda e, ss=s: e.activation(sdump[:, :], zb[:, :], AF.Square,
                                            accum_out=st["sq"][:, ss:ss + 1]),
               wait=[(sepi, sepi.n)], inc=ssq.inc())
            VS(lambda e, ss=s: e.tensor_scalar(st["mu2"][:, ss:ss + 1],
                                               st["sx"][:, ss:ss + 1],
                                               1.0 / D, None, op0=A.mult),
               wait=[(ssq, ssq.n)])
            VS(lambda e, ss=s: e.tensor_scalar(st["va"][:, ss:ss + 1],
                                               st["sq"][:, ss:ss + 1],
                                               1.0 / D, EPS, op0=A.mult, op1=A.add))
            VS(lambda e, ss=s: e.tensor_mul(st["vb"][:, ss:ss + 1],
                                            st["mu2"][:, ss:ss + 1],
                                            st["mu2"][:, ss:ss + 1]))
            VS(lambda e, ss=s: e.tensor_sub(st["va"][:, ss:ss + 1],
                                            st["va"][:, ss:ss + 1],
                                            st["vb"][:, ss:ss + 1]), inc=sepi.inc())
            SC(lambda e: e.activation(st["vb"][:, :], st["va"][:, :],
                                      AF.Sqrt, bias=0.0, scale=1.0),
               wait=[(sepi, sepi.n)], inc=ssq.inc())
            VS(lambda e: e.reciprocal(st["r2"][:, :], st["vb"][:, :]),
               wait=[(ssq, ssq.n)])
            if main_stop == "epi1":
                continue
            V(lambda e, ss=s: e.tensor_scalar(nb[:, :], zb[:, :],
                                              st["mu2"][:, ss:ss + 1],
                                              st["r2"][:, ss:ss + 1],
                                              op0=A.subtract, op1=A.mult),
              wait=[(ssq, ssq.n), (svs, svs.n)])
            V(lambda e: e.tensor_mul(nb[:, :], nb[:, :], gbs[:, :]))
            V(lambda e, ss=s: e.scalar_tensor_tensor(
                wbuf[:, :], nb[:, :], 1.0, bbs[:, :], op0=A.mult, op1=A.add,
                accum_out=st["sx"][:, ss:ss + 1]),
              inc=sepi.inc())
            SC(lambda e, ss=s: e.activation(sdump[:, :], wbuf[:, :], AF.Square,
                                            accum_out=st["sq"][:, ss:ss + 1]),
               wait=[(sepi, sepi.n)], inc=ssq.inc())
            VS(lambda e, ss=s: e.tensor_scalar(st["mu3"][:, ss:ss + 1],
                                               st["sx"][:, ss:ss + 1],
                                               1.0 / D, None, op0=A.mult),
               wait=[(ssq, ssq.n)])
            VS(lambda e, ss=s: e.tensor_scalar(st["va"][:, ss:ss + 1],
                                               st["sq"][:, ss:ss + 1],
                                               1.0 / D, EPS, op0=A.mult, op1=A.add))
            VS(lambda e, ss=s: e.tensor_mul(st["vb"][:, ss:ss + 1],
                                            st["mu3"][:, ss:ss + 1],
                                            st["mu3"][:, ss:ss + 1]))
            VS(lambda e, ss=s: e.tensor_sub(st["va"][:, ss:ss + 1],
                                            st["va"][:, ss:ss + 1],
                                            st["vb"][:, ss:ss + 1]), inc=sepi.inc())
            SC(lambda e: e.activation(st["vb"][:, :], st["va"][:, :],
                                      AF.Sqrt, bias=0.0, scale=1.0),
               wait=[(sepi, sepi.n)], inc=ssq.inc())
            VS(lambda e: e.reciprocal(st["r3"][:, :], st["vb"][:, :]),
               wait=[(ssq, ssq.n)])
            V(lambda e, ss=s: e.tensor_scalar(nb[:, :], wbuf[:, :],
                                              st["mu3"][:, ss:ss + 1],
                                              st["r3"][:, ss:ss + 1],
                                              op0=A.subtract, op1=A.mult),
              wait=[(ssq, ssq.n), (svs, svs.n)])
            V(lambda e, ss=s: e.scalar_tensor_tensor(
                x1b[:, :], nb[:, :], 1.0, x_t[:, ss * D:(ss + 1) * D],
                op0=A.mult, op1=A.add, accum_out=st["sx"][:, ss:ss + 1]),
              wait=[(sst1, sst1.n)], inc=sepi.inc())
            SC(lambda e, ss=s: e.activation(sdump[:, :], x1b[:, :], AF.Square,
                                            accum_out=st["sq"][:, ss:ss + 1]),
               wait=[(sepi, sepi.n)], inc=ssq.inc())
            G(lambda e, tt=ft, ss=s: e.dma_start(
                x1scr[tt * TT + ss * 128: tt * TT + (ss + 1) * 128, :], x1b[:, :]),
              wait=[(ssq, ssq.n)], inc=sst1.inc(16))
            VS(lambda e, ss=s: e.tensor_scalar(st["mu2"][:, ss:ss + 1],
                                               st["sx"][:, ss:ss + 1],
                                               1.0 / D, None, op0=A.mult),
               wait=[(ssq, ssq.n)])
            VS(lambda e, ss=s: e.tensor_scalar(st["va"][:, ss:ss + 1],
                                               st["sq"][:, ss:ss + 1],
                                               1.0 / D, EPS, op0=A.mult, op1=A.add))
            VS(lambda e, ss=s: e.tensor_mul(st["vb"][:, ss:ss + 1],
                                            st["mu2"][:, ss:ss + 1],
                                            st["mu2"][:, ss:ss + 1]))
            VS(lambda e, ss=s: e.tensor_sub(st["va"][:, ss:ss + 1],
                                            st["va"][:, ss:ss + 1],
                                            st["vb"][:, ss:ss + 1]), inc=sepi.inc())
            SC(lambda e: e.activation(st["vb"][:, :], st["va"][:, :],
                                      AF.Sqrt, bias=0.0, scale=1.0),
               wait=[(sepi, sepi.n)], inc=ssq.inc())
            VS(lambda e: e.reciprocal(st["r2"][:, :], st["vb"][:, :]),
               wait=[(ssq, ssq.n)])
            if main_stop == "epi2":
                continue
            V(lambda e, ss=s: e.tensor_scalar(bsub[:, :], x1b[:, :],
                                              st["mu2"][:, ss:ss + 1],
                                              st["r2"][:, ss:ss + 1],
                                              op0=A.subtract, op1=A.mult),
              wait=[(ssq, ssq.n), (sbte, sbte.n), (svs, svs.n)], inc=sepi.inc())
            for dk in range(8):
                T(lambda e, d=dk, p=pzt: e.transpose(
                    p[:, d * 128:(d + 1) * 128], bsub[:, d * 128:(d + 1) * 128],
                    ident[:, :]),
                  wait=([(sepi, sepi.n)] if dk == 0 else []),
                  inc=(sbt.inc() if dk == 7 else None))
            SC(lambda e, ss=s, p=pzt: e.activation(
                bass.AP(bT_b, ss * 128, [[8 * TT, 128], [TT, 8], [1, 128]]),
                p[:, :], AF.Identity),
               wait=[(sbt, sbt.n), (sstb, sstb.n)], inc=sbte.inc())
            first_b = (t == (1 if shard else 0) and s == 0)
            VS(lambda e: e.tensor_reduce(
                st["vb"][:, 0:1], bsub[:, :], axis=AX, op=A.max,
                apply_absolute_value=True),
               wait=[(sepi, sepi.n)])
            if first_b:
                VS(lambda e: e.tensor_copy(bmax[:, 0:1], st["vb"][:, 0:1]),
                   inc=sepi.inc())
            else:
                VS(lambda e: e.tensor_max(bmax[:, 0:1], bmax[:, 0:1], st["vb"][:, 0:1]),
                   inc=sepi.inc())
        if main_stop in ("epi1", "epi2"):
            continue
        G(lambda e, tt=ft: e.dma_start(bTscr[tt, :, :], bT_b[:, :]),
          wait=[(sbte, sbte.n)], inc=sstb.inc(16))
    P.drain([sst1, sstb, sx, swos, scvs, svs])
    P.emit_block()
    _exit_all(sa)
    if stop_after == "attn":
        _exit_all(sb)
        for g in reversed(P.guards):
            g.__exit__(None, None, None)
        P.check()
        return nc

    scc = P.sem("scc")
    sccd = P.sem("sccd")

    def global_max(pairs):
        n = len(pairs)
        for i, (src_, dst) in enumerate(pairs):
            G(lambda e, s=src_, i=i: e.dma_start(
                bass.AP(foldd.tensor, i * 128, [[1, 128], [1, 1]]), s[:, 0:1]),
              wait=[(scc, scc.n), (sccd, sccd.n)], inc=sccd.inc(16))
        VS(lambda e: e.memset(locb[0:1, 0:16], 0.0), wait=[(sccd, sccd.n)],
           inc=scc.inc())
        for i in range(n):
            G(lambda e, i=i: e.dma_start(
                rowb[0:1, :], bass.AP(foldd.tensor, i * 128, [[128, 1], [1, 128]])),
              wait=[(sccd, sccd.n)], inc=sccd.inc(16))
            VS(lambda e, i=i: e.tensor_reduce(locb[0:1, i:i + 1], rowb[0:1, :],
                                              axis=AX, op=A.max),
               wait=[(sccd, sccd.n), (scc, scc.n)], inc=scc.inc())
        G(lambda e: e.dma_start(cc_in[0:1, 0:16], locb[0:1, 0:16]),
          wait=[(scc, scc.n)], inc=sccd.inc(16))
        if cc:
            G(lambda e: e.collective_compute(
                "AllReduce", A.max, replica_groups=[list(range(8))],
                ins=[cc_in.tensor.ap().opt()], outs=[cc_out.tensor.ap().opt()]),
              wait=[(sccd, sccd.n)], inc=scc.inc())
        else:
            G(lambda e: e.dma_start(cc_out[0:1, 0:16], cc_in[0:1, 0:16]),
              wait=[(sccd, sccd.n)], inc=sccd.inc(16))
        for i, (src_, dst) in enumerate(pairs):
            G(lambda e, i=i, d=dst: e.dma_start(
                d[:, 0:1], bass.AP(cc_out.tensor, i, [[0, 128], [1, 1]])),
              wait=[(scc, scc.n), (sccd, sccd.n)], inc=sccd.inc(16))
        for i, (src_, dst) in enumerate(pairs):
            VS(lambda e, d=dst: e.tensor_copy(
                d[:, 0:4], bass.AP(d, 0, [[4, 128], [0, 4]])),
               wait=[(sccd, sccd.n)], inc=scc.inc())

    def mix1(dst, src):
        VS(lambda e, d=dst, s=src: e.tensor_scalar(d[:, :], s[:, :], 1.0, None,
                                                   op0=A.subtract))
        VS(lambda e, d=dst: e.tensor_mul(d[:, :], d[:, :], ks[:, :]))
        VS(lambda e, d=dst: e.tensor_scalar(d[:, :], d[:, :], 1.0, None, op0=A.add))

    global_max([(bmax, scl["m_b"])])
    VS(lambda e: e.tensor_scalar(scl["s0"][:, :], scl["m_b"][:, :], r327[:, 0:1],
                                 1e-12, op0=A.mult, op1=A.max),
       wait=[(scc, scc.n), (sccd, sccd.n)])
    VS(lambda e: e.reciprocal(scl["r0"][:, :], scl["s0"][:, :]))
    mix1(scl["rK0"], scl["r0"])
    V(lambda e: e.nop() if False else e.tensor_copy(scl["tmpa"][:, :], scl["rK0"][:, :]),
      inc=scc.inc())
    P.emit_block()
    if stop_after == "b1":
        _exit_all(sb)
        for g in reversed(P.guards):
            g.__exit__(None, None, None)
        P.check()
        return nc

    sfa = []
    wupb = _enter(sfa, nc.sbuf_tensor("wupb", [128, 8 * I3], BF16))
    wgb = _enter(sfa, nc.sbuf_tensor("wgb", [128, 8 * I3], BF16))
    bTl = _enter(sfa, nc.sbuf_tensor("bTl", [128, 8 * TT], BF16))
    bTl2 = _enter(sfa, nc.sbuf_tensor("bTl2", [128, 8 * TT], BF16))
    xq = _enter(sfa, nc.sbuf_tensor("xq", [128, 8 * TT], BF16))
    xqf = _enter(sfa, nc.sbuf_tensor("xqf", [128, 8 * TT], F32))
    upe = _enter(sfa, nc.sbuf_tensor("upe", [128, 24 * TT], BF16))
    gte = _enter(sfa, nc.sbuf_tensor("gte", [128, 24 * TT], BF16))

    sfw = P.sem("sfw")
    sbl = P.sem("sbl")
    sqd = P.sem("sqd")
    sum_ = P.sem("sum_")
    sue = P.sem("sue")
    sgm = P.sem("sgm")
    sge = P.sem("sge")
    sus = P.sem("sus")
    sgs = P.sem("sgs")
    smx = P.sem("smx")

    for hk in range(8):
        SY(lambda e, h=hk: e.dma_start(wupb[:, h * I3:(h + 1) * I3], wupdq[h, :, :]),
           inc=sfw.inc(16))
        SY(lambda e, h=hk: e.dma_start(wgb[:, h * I3:(h + 1) * I3], wgdq[h, :, :]),
           inc=sfw.inc(16))
    for t in range(NFT):
        buf = bTl if t % 2 == 0 else bTl2
        SY(lambda e, tt=t, b=buf: e.dma_start(b[:, :], bTscr[tt, :, :]),
           wait=([(sstb, sstb.n)] if t == 0 else []) +
                ([(sqd, max(0, sqd.n - 1))] if t >= 2 else []),
           inc=sbl.inc(16))
        V(lambda e, b=buf: e.tensor_scalar(xqf[:, :], b[:, :], scl["rK0"][:, 0:1],
                                           None, op0=A.mult),
          wait=[(sbl, sbl.n), (scc, scc.n), (smx, max(0, smx.n - 24))])
        V(lambda e: e.tensor_scalar(xq[:, :], xqf[:, :], ckv[:, 0:1], ckv[:, 0:1],
                                    op0=A.add, op1=A.subtract),
          inc=sqd.inc())
        for ik in range(24):
            pmt = pm0 if ik % 2 == 0 else pm1
            pgt = pc0 if ik % 2 == 0 else pz0
            for hk in range(8):
                T(lambda e, i=ik, h=hk, p=pmt: e.matmul(
                    p[:, :], wupb[:, h * I3 + i * 128: h * I3 + (i + 1) * 128],
                    xq[:, h * TT:(h + 1) * TT], start=(h == 0), stop=(h == 7)),
                  wait=([(sqd, sqd.n), (sfw, sfw.n),
                         (sue, max(0, sue.n - 1))] if hk == 0 else []),
                  inc=(sum_.inc() if hk == 7 else None))
            SC(lambda e, i=ik, p=pmt: e.activation(
                upe[:, i * TT:(i + 1) * TT], p[:, :], AF.Identity),
               wait=[(sum_, sum_.n), (sus, sus.n)],
               inc=sue.inc())
            for hk in range(8):
                T(lambda e, i=ik, h=hk, p=pgt: e.matmul(
                    p[:, 0:512], wgb[:, h * I3 + i * 128: h * I3 + (i + 1) * 128],
                    xq[:, h * TT:(h + 1) * TT], start=(h == 0), stop=(h == 7)),
                  wait=([(sge, max(0, sge.n - 1))] if hk == 0 else []),
                  inc=(sgm.inc() if hk == 7 else None))
            SC(lambda e, i=ik, p=pgt: e.activation(
                gte[:, i * TT:(i + 1) * TT], p[:, 0:512], AF.Identity),
               wait=[(sgm, sgm.n), (sgs, sgs.n)],
               inc=sge.inc())
            VS(lambda e, i=ik: e.tensor_reduce(scl["tmpb"][:, 0:1],
                                               upe[:, i * TT:(i + 1) * TT],
                                               axis=AX, op=A.max,
                                               apply_absolute_value=True),
               wait=[(sue, sue.n)])
            if t == 0 and ik == 0:
                VS(lambda e: e.tensor_copy(fmax["up"][:, 0:1], scl["tmpb"][:, 0:1]))
            else:
                VS(lambda e: e.tensor_max(fmax["up"][:, 0:1], fmax["up"][:, 0:1],
                                          scl["tmpb"][:, 0:1]))
            VS(lambda e, i=ik: e.tensor_reduce(scl["tmpc"][:, 0:1],
                                               gte[:, i * TT:(i + 1) * TT],
                                               axis=AX, op=A.max,
                                               apply_absolute_value=True),
               wait=[(sge, sge.n)])
            if t == 0 and ik == 0:
                VS(lambda e: e.tensor_copy(fmax["g"][:, 0:1], scl["tmpc"][:, 0:1]),
                   inc=smx.inc())
            else:
                VS(lambda e: e.tensor_max(fmax["g"][:, 0:1], fmax["g"][:, 0:1],
                                          scl["tmpc"][:, 0:1]),
                   inc=smx.inc())
            if ik == 23:
                G(lambda e, tt=t: e.dma_start(upscr[tt, :, :], upe[:, :]),
                  wait=[(smx, smx.n)], inc=sus.inc(16))
                G(lambda e, tt=t: e.dma_start(gscr[tt, :, :], gte[:, :]),
                  inc=sgs.inc(16))
    P.drain([sus, sgs, svs])
    P.emit_block()
    _exit_all(sfa)
    if stop_after == "fa":
        _exit_all(sb)
        for g in reversed(P.guards):
            g.__exit__(None, None, None)
        P.check()
        return nc

    global_max([(fmax["up"], scl["m_up"]), (fmax["g"], scl["m_g"])])
    VS(lambda e: e.tensor_mul(scl["tmpa"][:, :], scl["m_up"][:, :], scl["s0"][:, :]),
       wait=[(scc, scc.n), (sccd, sccd.n)])
    VS(lambda e: e.tensor_scalar(scl["s_up"][:, :], scl["tmpa"][:, :], r327[:, 0:1],
                                 1e-12, op0=A.mult, op1=A.max))
    VS(lambda e: e.reciprocal(scl["r_up"][:, :], scl["s_up"][:, :]))
    VS(lambda e: e.tensor_mul(scl["tmpa"][:, :], scl["s0"][:, :], scl["r_up"][:, :]))
    mix1(scl["cU"], scl["tmpa"])
    VS(lambda e: e.tensor_mul(scl["tmpa"][:, :], scl["m_g"][:, :], scl["s0"][:, :]))
    VS(lambda e: e.tensor_scalar(scl["s_g"][:, :], scl["tmpa"][:, :], r327[:, 0:1],
                                 1e-12, op0=A.mult, op1=A.max))
    VS(lambda e: e.reciprocal(scl["r_g"][:, :], scl["s_g"][:, :]))
    VS(lambda e: e.tensor_mul(scl["tmpa"][:, :], scl["s0"][:, :], scl["r_g"][:, :]))
    mix1(scl["cG"], scl["tmpa"])
    VS(lambda e: e.tensor_mul(scl["tmpa"][:, :], scl["s_g"][:, :], ks[:, :]))
    VS(lambda e: e.tensor_scalar(scl["tmpb"][:, :], ks[:, :], -1.0, 1.0,
                                 op0=A.mult, op1=A.add))
    VS(lambda e: e.tensor_add(scl["sgmix"][:, :], scl["tmpa"][:, :], scl["tmpb"][:, :]))
    VS(lambda e: e.tensor_scalar(scl["tmpa"][:, :], ks[:, :], 65536.0, None, op0=A.mult))
    VS(lambda e: e.tensor_add(scl["c65"][:, :], scl["tmpa"][:, :], scl["tmpb"][:, :]),
       inc=scc.inc())
    P.emit_block()

    sb1 = []
    gl0 = _enter(sb1, nc.sbuf_tensor("gl0", [128, 24 * TT], BF16))
    gl0b = _enter(sb1, nc.sbuf_tensor("gl0b", [128, 24 * TT], BF16))
    qgb = _enter(sb1, nc.sbuf_tensor("qgb", [128, 24 * TT], BF16))
    sgf = _enter(sb1, nc.sbuf_tensor("sgf", [128, 12 * TT], F32))
    qsb = _enter(sb1, nc.sbuf_tensor("qsb", [128, 24 * TT], BF16))
    pb = _enter(sb1, nc.sbuf_tensor("pb", [128, 24 * TT], BF16))

    sgl = P.sem("sgl")
    sq1 = P.sem("sq1")
    ssg = P.sem("ssg")
    sq2 = P.sem("sq2")
    spp = P.sem("spp")
    sps = P.sem("sps")

    for t in range(NFT):
        gbuf = gl0 if t % 2 == 0 else gl0b
        SY(lambda e, tt=t, b=gbuf: e.dma_start(b[:, :], gscr[tt, :, :]),
           wait=([(sgs, sgs.n)] if t == 0 else []) +
                ([(sps, max(0, sps.n - 16))] if t >= 2 else []),
           inc=sgl.inc(16))
        for half in range(2):
            off = half * 12 * TT
            z = 12 * TT
            V(lambda e, o=off, zz=z, b=gbuf: e.tensor_scalar(
                sgf[:, 0:zz], b[:, o:o + zz], scl["cG"][:, 0:1], None, op0=A.mult),
              wait=[(sgl, sgl.n), (scc, scc.n)] +
                   ([(spp, max(0, spp.n - 1))] if spp.n >= 1 else []))
            V(lambda e, o=off, zz=z: e.tensor_scalar(
                qgb[:, o:o + zz], sgf[:, 0:zz], ckv[:, 0:1], ckv[:, 0:1],
                op0=A.add, op1=A.subtract),
              inc=sq1.inc())
            SC(lambda e, o=off, zz=z: e.activation(
                sgf[:, 0:zz], qgb[:, o:o + zz], AF.Sigmoid, bias=0.0,
                scale=scl["sgmix"][:, 0:1]),
               wait=[(sq1, sq1.n)], inc=ssg.inc())
            V(lambda e, o=off, zz=z: e.tensor_scalar(
                sgf[:, 0:zz], sgf[:, 0:zz], scl["c65"][:, 0:1], None, op0=A.mult),
              wait=[(ssg, ssg.n)])
            V(lambda e, o=off, zz=z: e.tensor_scalar(
                qsb[:, o:o + zz], sgf[:, 0:zz], ckv[:, 0:1], ckv[:, 0:1],
                op0=A.add, op1=A.subtract))
            V(lambda e, o=off, zz=z: e.tensor_mul(
                pb[:, o:o + zz], qgb[:, o:o + zz], qsb[:, o:o + zz]),
              wait=[(sps, sps.n)], inc=sq2.inc())
            VS(lambda e, o=off, zz=z: e.tensor_reduce(
                scl["tmpb"][:, 0:1], pb[:, o:o + zz], axis=AX, op=A.max,
                apply_absolute_value=True),
               wait=[(sq2, sq2.n)])
            if t == 0 and half == 0:
                VS(lambda e: e.tensor_copy(fmax["p"][:, 0:1], scl["tmpb"][:, 0:1]),
                   inc=spp.inc())
            else:
                VS(lambda e: e.tensor_max(fmax["p"][:, 0:1], fmax["p"][:, 0:1],
                                          scl["tmpb"][:, 0:1]),
                   inc=spp.inc())
        G(lambda e, tt=t: e.dma_start(pscr[tt, :, :], pb[:, :]),
          wait=[(spp, spp.n)], inc=sps.inc(16))
    P.drain([sps, svs])
    P.emit_block()
    _exit_all(sb1)
    if stop_after == "fb1":
        _exit_all(sb)
        for g in reversed(P.guards):
            g.__exit__(None, None, None)
        P.check()
        return nc

    global_max([(fmax["p"], scl["m_p"])])
    VS(lambda e: e.tensor_mul(scl["tmpa"][:, :], scl["m_p"][:, :], scl["s_g"][:, :]),
       wait=[(scc, scc.n), (sccd, sccd.n)])
    VS(lambda e: e.tensor_scalar(scl["tmpa"][:, :], scl["tmpa"][:, :], 1.0 / 65536.0,
                                 None, op0=A.mult))
    VS(lambda e: e.tensor_scalar(scl["s_g2"][:, :], scl["tmpa"][:, :], r327[:, 0:1],
                                 1e-12, op0=A.mult, op1=A.max))
    VS(lambda e: e.reciprocal(scl["r_g2"][:, :], scl["s_g2"][:, :]))
    VS(lambda e: e.tensor_scalar(scl["tmpa"][:, :], scl["s_g"][:, :], 1.0 / 65536.0,
                                 None, op0=A.mult))
    VS(lambda e: e.tensor_mul(scl["tmpa"][:, :], scl["tmpa"][:, :], scl["r_g2"][:, :]))
    mix1(scl["c2m"], scl["tmpa"])
    VS(lambda e: e.tensor_copy(scl["tmpb"][:, :], scl["c2m"][:, :]), inc=scc.inc())
    P.emit_block()

    sb2 = []
    pl = _enter(sb2, nc.sbuf_tensor("pl", [128, 24 * TT], BF16))
    plb = _enter(sb2, nc.sbuf_tensor("plb", [128, 24 * TT], BF16))
    ul = _enter(sb2, nc.sbuf_tensor("ul", [128, 24 * TT], BF16))
    ulb = _enter(sb2, nc.sbuf_tensor("ulb", [128, 24 * TT], BF16))
    g2b = _enter(sb2, nc.sbuf_tensor("g2b", [128, 24 * TT], BF16))
    uqb = _enter(sb2, nc.sbuf_tensor("uqb", [128, 24 * TT], BF16))
    opb = _enter(sb2, nc.sbuf_tensor("opb", [128, 24 * TT], BF16))
    tf2 = _enter(sb2, nc.sbuf_tensor("tf2", [128, 12 * TT], F32))

    spl = P.sem("spl")
    sq3 = P.sem("sq3")
    sop = P.sem("sop")
    sos = P.sem("sos")

    for t in range(NFT):
        pbuf = pl if t % 2 == 0 else plb
        ubuf = ul if t % 2 == 0 else ulb
        SY(lambda e, tt=t, b=pbuf: e.dma_start(b[:, :], pscr[tt, :, :]),
           wait=([(sps, sps.n), (sus, sus.n)] if t == 0 else []) +
                ([(sos, max(0, sos.n - 16))] if t >= 2 else []),
           inc=spl.inc(16))
        SY(lambda e, tt=t, b=ubuf: e.dma_start(b[:, :], upscr[tt, :, :]),
           inc=spl.inc(16))
        for half in range(2):
            off = half * 12 * TT
            z = 12 * TT
            V(lambda e, o=off, zz=z, b=pbuf: e.tensor_scalar(
                tf2[:, 0:zz], b[:, o:o + zz], scl["c2m"][:, 0:1], None, op0=A.mult),
              wait=[(spl, spl.n), (scc, scc.n)] +
                   ([(sop, max(0, sop.n - 1))] if sop.n >= 1 else []))
            V(lambda e, o=off, zz=z: e.tensor_scalar(
                g2b[:, o:o + zz], tf2[:, 0:zz], ckv[:, 0:1], ckv[:, 0:1],
                op0=A.add, op1=A.subtract))
            V(lambda e, o=off, zz=z, b=ubuf: e.tensor_scalar(
                tf2[:, 0:zz], b[:, o:o + zz], scl["cU"][:, 0:1], None, op0=A.mult))
            V(lambda e, o=off, zz=z: e.tensor_scalar(
                uqb[:, o:o + zz], tf2[:, 0:zz], ckv[:, 0:1], ckv[:, 0:1],
                op0=A.add, op1=A.subtract))
            V(lambda e, o=off, zz=z: e.tensor_mul(
                opb[:, o:o + zz], g2b[:, o:o + zz], uqb[:, o:o + zz]),
              wait=[(sos, sos.n)], inc=sq3.inc())
            VS(lambda e, o=off, zz=z: e.tensor_reduce(
                scl["tmpb"][:, 0:1], opb[:, o:o + zz], axis=AX, op=A.max,
                apply_absolute_value=True),
               wait=[(sq3, sq3.n)])
            if t == 0 and half == 0:
                VS(lambda e: e.tensor_copy(fmax["op"][:, 0:1], scl["tmpb"][:, 0:1]),
                   inc=sop.inc())
            else:
                VS(lambda e: e.tensor_max(fmax["op"][:, 0:1], fmax["op"][:, 0:1],
                                          scl["tmpb"][:, 0:1]),
                   inc=sop.inc())
        G(lambda e, tt=t: e.dma_start(opscr[tt, :, :], opb[:, :]),
          wait=[(sop, sop.n)], inc=sos.inc(16))
    P.drain([sos, svs])
    P.emit_block()
    _exit_all(sb2)
    if stop_after == "fb2":
        _exit_all(sb)
        for g in reversed(P.guards):
            g.__exit__(None, None, None)
        P.check()
        return nc

    global_max([(fmax["op"], scl["m_op"])])
    VS(lambda e: e.tensor_mul(scl["tmpc"][:, :], scl["s_g2"][:, :], scl["s_up"][:, :]),
       wait=[(scc, scc.n), (sccd, sccd.n)])
    VS(lambda e: e.tensor_mul(scl["tmpb"][:, :], scl["tmpc"][:, :], scl["m_op"][:, :]))
    VS(lambda e: e.tensor_scalar(scl["s_o"][:, :], scl["tmpb"][:, :], r327[:, 0:1],
                                 1e-12, op0=A.mult, op1=A.max))
    VS(lambda e: e.reciprocal(scl["r_o"][:, :], scl["s_o"][:, :]))
    VS(lambda e: e.tensor_mul(scl["tmpa"][:, :], scl["tmpc"][:, :], scl["r_o"][:, :]))
    mix1(scl["c3m"], scl["tmpa"])
    VS(lambda e: e.tensor_mul(scl["tmpa"][:, :], scl["s_o"][:, :], ks[:, :]))
    VS(lambda e: e.tensor_scalar(scl["tmpb"][:, :], ks[:, :], -1.0, 1.0,
                                 op0=A.mult, op1=A.add))
    VS(lambda e: e.tensor_add(scl["cout"][:, :], scl["tmpa"][:, :], scl["tmpb"][:, :]),
       inc=scc.inc())
    P.emit_block()

    sb3 = []
    wdb = _enter(sb3, nc.sbuf_tensor("wdb", [128, 24 * D], BF16))
    opl = _enter(sb3, nc.sbuf_tensor("opl", [128, 24 * TT], BF16))
    oqb = _enter(sb3, nc.sbuf_tensor("oqb", [128, 24 * TT], BF16))
    tf3 = _enter(sb3, nc.sbuf_tensor("tf3", [128, 12 * TT], F32))
    yf = _enter(sb3, nc.sbuf_tensor("yf", [128, 8 * TT], F32))
    zf = _enter(sb3, nc.sbuf_tensor("zf", [128, D], F32))
    n3 = _enter(sb3, nc.sbuf_tensor("n3", [128, D], F32))
    x1l = _enter(sb3, nc.sbuf_tensor("x1l", [128, 2 * D], F32))
    x2b = _enter(sb3, nc.sbuf_tensor("x2b", [128, D], F32))
    sdump2 = _enter(sb3, nc.sbuf_tensor("sdump2", [128, D], F32))

    sdw = P.sem("sdw")
    sol = P.sem("sol")
    sq4 = P.sem("sq4")
    sdm = P.sem("sdm")
    sde = P.sem("sde")
    szt2 = P.sem("szt2")
    sep2 = P.sem("sep2")
    ssq2 = P.sem("ssq2")
    sxl = P.sem("sxl")
    sout = P.sem("sout")

    for ik in range(24):
        SY(lambda e, i=ik: e.dma_start(wdb[:, i * D:(i + 1) * D], wddq[i, :, :]),
           inc=sdw.inc(16))
    for t in range(NFT):
        SY(lambda e, tt=t: e.dma_start(opl[:, :], opscr[tt, :, :]),
           wait=([(sos, sos.n)] if t == 0 else []) +
                ([(sq4, sq4.n)] if t >= 1 else []),
           inc=sol.inc(16))
        for half in range(2):
            off = half * 12 * TT
            z = 12 * TT
            V(lambda e, o=off, zz=z: e.tensor_scalar(
                tf3[:, 0:zz], opl[:, o:o + zz], scl["c3m"][:, 0:1], None, op0=A.mult),
              wait=[(sol, sol.n), (scc, scc.n)] +
                   ([(sdm, max(0, sdm.n - 1))] if sdm.n >= 1 else []))
            V(lambda e, o=off, zz=z: e.tensor_scalar(
                oqb[:, o:o + zz], tf3[:, 0:zz], ckv[:, 0:1], ckv[:, 0:1],
                op0=A.add, op1=A.subtract),
              inc=sq4.inc())
        for dk in range(8):
            pmt = pm0 if dk % 2 == 0 else pm1
            for ik in range(24):
                T(lambda e, d=dk, i=ik, p=pmt: e.matmul(
                    p[:, :], wdb[:, i * D + d * 128: i * D + (d + 1) * 128],
                    oqb[:, i * TT:(i + 1) * TT], start=(i == 0), stop=(i == 23)),
                  wait=([(sq4, sq4.n), (sdw, sdw.n),
                         (sde, max(0, sde.n - 1))] if ik == 0 else []),
                  inc=(sdm.inc() if ik == 23 else None))
            SC(lambda e, d=dk, p=pmt: e.activation(
                yf[:, d * TT:(d + 1) * TT], p[:, :], AF.Identity, bias=0.0,
                scale=scl["cout"][:, 0:1]),
               wait=[(sdm, sdm.n)], inc=sde.inc())
        for s in range(NSUB):
            pzt = pz0 if s % 2 == 0 else pz1
            xb = s % 2
            SY(lambda e, tt=t, ss=s, b=xb: e.dma_start(
                x1l[:, b * D:(b + 1) * D],
                x1scr[tt * TT + ss * 128: tt * TT + (ss + 1) * 128, :]),
               wait=([(sst1, sst1.n)] if (t == 0 and s == 0) else []) +
                    ([(sep2, max(0, sep2.n - 3))] if sep2.n >= 3 else []),
               inc=sxl.inc(16))
            for dk in range(8):
                T(lambda e, d=dk, ss=s, p=pzt: e.transpose(
                    p[:, d * 128:(d + 1) * 128],
                    yf[:, d * TT + ss * 128: d * TT + (ss + 1) * 128], ident[:, :]),
                  wait=([(sde, sde.n), (sep2, max(0, sep2.n - 5))] if dk == 0 else []),
                  inc=(szt2.inc() if dk == 7 else None))
            V(lambda e, ss=s, p=pzt: e.tensor_scalar(
                zf[:, :], p[:, :], 1.0, 0.0, op0=A.mult, op1=A.add,
                accum_out=st["sx"][:, ss:ss + 1]),
              wait=[(szt2, szt2.n), (ssq2, ssq2.n)], inc=sep2.inc())
            SC(lambda e, ss=s: e.activation(sdump2[:, :], zf[:, :], AF.Square,
                                            accum_out=st["sq"][:, ss:ss + 1]),
               wait=[(sep2, sep2.n)], inc=ssq2.inc())
            VS(lambda e, ss=s: e.tensor_scalar(st["mu1"][:, ss:ss + 1],
                                              st["sx"][:, ss:ss + 1],
                                              1.0 / D, None, op0=A.mult),
               wait=[(ssq2, ssq2.n)])
            VS(lambda e, ss=s: e.tensor_scalar(st["va"][:, ss:ss + 1],
                                              st["sq"][:, ss:ss + 1],
                                              1.0 / D, EPS, op0=A.mult, op1=A.add))
            VS(lambda e, ss=s: e.tensor_mul(st["vb"][:, ss:ss + 1],
                                           st["mu1"][:, ss:ss + 1],
                                           st["mu1"][:, ss:ss + 1]))
            VS(lambda e, ss=s: e.tensor_sub(st["va"][:, ss:ss + 1],
                                           st["va"][:, ss:ss + 1],
                                           st["vb"][:, ss:ss + 1]), inc=sep2.inc())
            SC(lambda e: e.activation(st["vb"][:, :], st["va"][:, :],
                                      AF.Sqrt, bias=0.0, scale=1.0),
                wait=[(sep2, sep2.n)], inc=ssq2.inc())
            VS(lambda e: e.reciprocal(st["r1"][:, :], st["vb"][:, :]),
               wait=[(ssq2, ssq2.n)])
            V(lambda e, ss=s: e.tensor_scalar(n3[:, :], zf[:, :],
                                              st["mu1"][:, ss:ss + 1],
                                              st["r1"][:, ss:ss + 1],
                                              op0=A.subtract, op1=A.mult),
              wait=[(ssq2, ssq2.n), (svs, svs.n)])
            V(lambda e, b=xb: e.tensor_add(x2b[:, :], n3[:, :],
                                           x1l[:, b * D:(b + 1) * D]),
              wait=[(sxl, sxl.n), (sout, sout.n)], inc=sep2.inc())
            SY(lambda e, tt=t, ss=s: e.dma_start(
                out[tt * TT + ss * 128: tt * TT + (ss + 1) * 128, :], x2b[:, :]),
               wait=[(sep2, sep2.n)], inc=sout.inc(16))
    if dbg:
        _names = ["m_b", "s0", "r0", "rK0", "m_up", "s_up", "cU", "m_g", "s_g",
                  "cG", "sgmix", "c65", "m_p", "s_g2", "c2m", "m_op", "s_o",
                  "c3m", "cout", "r_o", "r_g2", "r_up", "r_g"]
        sdbg = P.sem("sdbg")
        for _i, _nm in enumerate(_names):
            G(lambda e, i=_i, b=scl[_nm]: e.dma_start(dbgscl[i:i + 1, :], b[0:1, :]),
              wait=[(scc, scc.n), (sout, sout.n)], inc=sdbg.inc(16))
        G(lambda e, b=ks: e.dma_start(dbgscl[30:31, :], b[0:1, :]), inc=sdbg.inc(16))
        G(lambda e, b=ckv: e.dma_start(dbgscl[31:32, :], b[0:1, :]), inc=sdbg.inc(16))
        G(lambda e: e.nop(), wait=[(sdbg, sdbg.n)])
    SY(lambda e: e.nop(), wait=[(sout, sout.n)])
    P.emit_block()
    _exit_all(sb3)

    _exit_all(sb)
    for g in reversed(P.guards):
        g.__exit__(None, None, None)
    P.check()
    return nc


def _prep_inputs(inputs, shard=True):
    x = np.asarray(inputs["L0_attn_x"])
    W_in = np.asarray(inputs["L0_attn_W_in"])
    conv_w = np.asarray(inputs["L0_attn_conv_w"])
    conv_b = np.asarray(inputs["L0_attn_conv_b"])
    W_B = np.asarray(inputs["L0_attn_W_B"])
    W_C = np.asarray(inputs["L0_attn_W_C"])
    D_param = np.asarray(inputs["L0_attn_D_param"])
    W_out = np.asarray(inputs["L0_attn_W_out"])
    ln_g = np.asarray(inputs["L0_attn_ln_gamma"])
    ln_b = np.asarray(inputs["L0_attn_ln_beta"])
    w_gate = np.asarray(inputs["L0_ffn_w_gate"])
    w_up = np.asarray(inputs["L0_ffn_w_up"])
    w_down = np.asarray(inputs["L0_ffn_w_down"])

    f = np.float32
    com = {
        "winT": np.ascontiguousarray(W_in.T, dtype=f),
        "wBT": np.ascontiguousarray(W_B.T, dtype=f),
        "wCT": np.ascontiguousarray(W_C.T, dtype=f),
        "woutT": np.ascontiguousarray(W_out.T, dtype=f),
        "convb": np.ascontiguousarray(np.asarray(conv_b, f).reshape(16, 128).T),
        "dparam": np.ascontiguousarray(np.asarray(D_param, f).reshape(16, 128).T),
        "gb": None, "bb": None,
        "gbin": np.ascontiguousarray(np.broadcast_to(ln_g, (128, D)), dtype=f),
        "bbin": np.ascontiguousarray(np.broadcast_to(ln_b, (128, D)), dtype=f),
        "wupT": np.ascontiguousarray(w_up.T, dtype=f),
        "wgT": np.ascontiguousarray(w_gate.T, dtype=f),
        "wdT": np.ascontiguousarray(w_down.T, dtype=f),
        "wup_n": np.ascontiguousarray(w_up, dtype=f),
        "wg_n": np.ascontiguousarray(w_gate, dtype=f),
        "wd_n": np.ascontiguousarray(w_down, dtype=f),
        "identin": np.eye(128, dtype=f),
    }
    del com["gb"], com["bb"]
    cd = np.zeros((64 * 128, 128), f)
    idx = np.arange(128)
    for tap in range(4):
        for ck in range(16):
            base = (tap * 16 + ck) * 128
            cd[base + idx, idx] = conv_w[ck * 128 + idx, 0, tap]
    com["convd"] = cd

    maps = []
    if shard:
        for c in range(8):
            b, q = divmod(c, 4)
            lo = q * 4096
            if q == 0:
                xs = np.concatenate([np.zeros((512, D), f), x[b, 0:4096]], axis=0)
            else:
                xs = x[b, lo - 512: lo + 4096]
            m = dict(com)
            m["xp"] = np.ascontiguousarray(xs, dtype=f)
            m["maskin"] = np.full((1, 4), 0.0 if q == 0 else 1.0, f)
            maps.append(m)
    else:
        xall = np.ascontiguousarray(x.reshape(32768, D), dtype=f)
        for c in range(8):
            m = dict(com)
            m["xp"] = xall
            m["maskin"] = np.full((1, 4), 1.0, f)
            maps.append(m)
    return maps


def kernel(**inputs):
    shard = True
    maps = _prep_inputs(inputs, shard=shard)
    nc = build(shard=shard)
    res = run_bass_kernel_spmd(nc, maps, core_ids=list(range(8)))
    out = np.empty((2, 16384, 1024), np.float32)
    if shard:
        for c in range(8):
            b, q = divmod(c, 4)
            out[b, q * 4096:(q + 1) * 4096] = res.results[c]["out"]
    else:
        out[:] = res.results[0]["out"].reshape(2, 16384, 1024)
    return out
